# version 40
# baseline (speedup 1.0000x reference)
import numpy as np
import ml_dtypes
import concourse.bass as bass
import concourse.mybir as mybir
from concourse.bass_utils import run_bass_kernel_spmd
from contextlib import ExitStack

F32 = mybir.dt.float32
BF16 = mybir.dt.bfloat16
I32 = mybir.dt.int32
AF = mybir.ActivationFunctionType
ALU = mybir.AluOpType
AX = mybir.AxisListType

B, T, C = 8, 2048, 1024
H, KVH, D = 16, 4, 64
HID = 3072
TC = T // 128
CC = C // 128
NPAIR = H // 2
EWID = 3968
QB = 1024
DEQ = 1.0 / 127.0
EPS = 1e-5
NSL = 12

_start = 2.0 ** (-(2.0 ** (-(np.log2(H) - 3.0))))
SLOPES = [float(_start ** (h + 1)) / 8.0 for h in range(H)]

ENGINES = ("sync", "gpsimd", "tensor", "vector", "scalar")
DEBUG_TAPS = False


class Ctr:
    __slots__ = ("h", "n")

    def __init__(self, h):
        self.h = h
        self.n = 0


def _ap(t, off, dims):
    th = t.tensor if isinstance(t, bass.AP) else t
    return bass.AP(th, off, dims)


def build_nc():
    nc = bass.Bass("TRN2", target_bir_lowering=False)

    x_d = nc.dram_tensor("x", [T, C], F32, kind="ExternalInput").ap()
    wqT_d = nc.dram_tensor("wqT", [C, C], F32, kind="ExternalInput").ap()
    wkvT_d = nc.dram_tensor("wkvT", [C, 2 * KVH * D], F32, kind="ExternalInput").ap()
    wpT_d = nc.dram_tensor("wpT", [C, C], F32, kind="ExternalInput").ap()
    bp_d = nc.dram_tensor("bproj", [C], F32, kind="ExternalInput").ap()
    wgT_d = nc.dram_tensor("wgT", [C, HID], I32, kind="ExternalInput").ap()
    wuT_d = nc.dram_tensor("wuT", [C, HID], I32, kind="ExternalInput").ap()
    wdT_d = nc.dram_tensor("wdT", [HID, C], I32, kind="ExternalInput").ap()
    ewt_d = nc.dram_tensor("ewt", [NPAIR * 128, 2 * EWID], BF16, kind="ExternalInput").ap()
    idb_d = nc.dram_tensor("idb", [128, 128], BF16, kind="ExternalInput").ap()
    idf_d = nc.dram_tensor("idf", [128, 128], F32, kind="ExternalInput").ap()
    onc_d = nc.dram_tensor("onc", [128, 1], BF16, kind="ExternalInput").ap()
    onrb_d = nc.dram_tensor("onrb", [1, 128], BF16, kind="ExternalInput").ap()
    on2_d = nc.dram_tensor("on2", [128, 128], BF16, kind="ExternalInput").ap()
    onrf_d = nc.dram_tensor("onrf", [1, 128], F32, kind="ExternalInput").ap()
    out_d = nc.dram_tensor("out", [T, C], F32, kind="ExternalOutput").ap()

    taps = {}
    if DEBUG_TAPS:
        for nm, shp, dt in [
            ("tap_xfm", [128, CC * T], BF16), ("tap_q", [128, CC * T], BF16),
            ("tap_k", [128, 2 * T], BF16), ("tap_vhat", [128, KVH * TC * 65], BF16),
            ("tap_negm", [128, NPAIR], F32), ("tap_ofm", [128, CC * T], BF16),
            ("tap_x2", [128, TC * C], F32), ("tap_hfm", [128, 24 * 1024], BF16),
            ("tap_x2fm", [128, CC * T], BF16), ("tap_r15", [128, C], F32),
            ("tap_ln", [128, 4], F32),
        ]:
            taps[nm] = nc.dram_tensor(nm, shp, dt, kind="ExternalOutput").ap()

    ev = {k: [] for k in ENGINES}

    def emit(eng, c, fn, waits=(), k=1, fences=()):
        prev_full = None
        if k >= 16:
            c = _dma_ctr(c)
            if c.n > 0:
                prev_full = (c, c.n)
        c.n += k
        ww = {}
        for t in waits:
            if t is None:
                continue
            wc, wv = t
            if wc is c:
                continue
            ww[wc] = max(ww.get(wc, 0), wv)
        for t in fences:
            if t is None:
                continue
            wc, wv = t
            ww[wc] = max(ww.get(wc, 0), wv)
        if prev_full is not None:
            wc, wv = prev_full
            ww[wc] = max(ww.get(wc, 0), wv)
        ev[eng].append((fn, list(ww.items()), c, k))
        return (c, c.n)

    with ExitStack() as top:
        E = top.enter_context
        A = E(nc.sbuf_tensor("A", [128, TC * C], F32))
        FM = E(nc.sbuf_tensor("FM", [128, CC * T], BF16))
        idb = E(nc.sbuf_tensor("idb_s", [128, 128], BF16))
        idf = E(nc.sbuf_tensor("idf_s", [128, 128], F32))
        onc = E(nc.sbuf_tensor("onc_s", [128, 1], BF16))
        onrb = E(nc.sbuf_tensor("onrb_s", [1, 128], BF16))
        on2 = E(nc.sbuf_tensor("on2_s", [128, 128], BF16))
        onrf = E(nc.sbuf_tensor("onrf_s", [1, 128], F32))
        negM = [E(nc.sbuf_tensor(f"negM{i}", [128, 1], F32)) for i in range(NPAIR)]
        epsv = E(nc.sbuf_tensor("epsv", [128, 1], F32))
        wp_bf = E(nc.sbuf_tensor("wp_bf", [128, CC * C], BF16))
        bpb = E(nc.sbuf_tensor("bpb", [128, C], F32))

        sems = {k: Ctr(E(nc.semaphore(f"s_{k}"))) for k in
                ("di", "dg", "pe", "act", "dve", "out")}
        di, dg, pe, act, dve, so = (sems[k] for k in ("di", "dg", "pe", "act", "dve", "out"))
        DIP = [Ctr(E(nc.semaphore(f"s_dip{i}"))) for i in range(6)]
        DGP = [Ctr(E(nc.semaphore(f"s_dgp{i}"))) for i in range(8)]
        _pool_idx = {"di": 0, "dg": 0}

        def _dma_ctr(c):
            if c is di:
                _pool_idx["di"] += 1
                return DIP[_pool_idx["di"] % len(DIP)]
            if c is dg:
                _pool_idx["dg"] += 1
                return DGP[_pool_idx["dg"] % len(DGP)]
            return c

        blk = E(nc.Block())

        sBC = ExitStack()
        EB = sBC.enter_context
        QBUF = EB(nc.sbuf_tensor("QBUF", [128, CC * T], BF16))
        Kfm = EB(nc.sbuf_tensor("Kfm", [128, 2 * T], BF16))
        VSL = 65
        Vhat = EB(nc.sbuf_tensor("Vhat", [128, KVH * TC * VSL], BF16))

        sB = ExitStack()
        EB2 = sB.enter_context
        wq_bf = EB2(nc.sbuf_tensor("wq_bf", [128, CC * C], BF16))
        wkv_bf = EB2(nc.sbuf_tensor("wkv_bf", [128, CC * 512], BF16))
        Vfm = EB2(nc.sbuf_tensor("Vfm", [128, 2 * T], BF16))
        q2s = Vfm
        mrow = EB2(nc.sbuf_tensor("mrow", [1, 64], F32))

        psq = [EB2(nc.psum_tensor(f"psq{i}", [128, 512], F32)) for i in range(4)]
        pstp = [EB2(nc.psum_tensor(f"pstp{i}", [128, 1024], BF16)) for i in range(2)]
        psn = EB2(nc.psum_tensor("psn", [128, 512], F32))

        t_eps = emit("vector", dve, lambda e: e.memset(epsv[:, :], EPS))
        t_ldx = emit("sync", di, lambda e: e.dma_start(
            out=_ap(A, 0, [[TC * C, 128], [C, TC], [1, C]]),
            in_=x_d.rearrange("(tc p) c -> p tc c", p=128)), k=16)
        t_idb = emit("sync", di, lambda e: e.dma_start(out=idb[:, :], in_=idb_d), k=16)
        t_idf = emit("sync", di, lambda e: e.dma_start(out=idf[:, :], in_=idf_d), k=16)
        t_onc = emit("sync", di, lambda e: e.dma_start(out=onc[:, :], in_=onc_d), k=16)
        t_onrb = emit("sync", di, lambda e: e.dma_start(out=onrb[:, :], in_=onrb_d), k=16)
        t_on2 = emit("sync", di, lambda e: e.dma_start(out=on2[:, :], in_=on2_d), k=16)
        t_onrf = emit("sync", di, lambda e: e.dma_start(out=onrf[:, :], in_=onrf_d), k=16)
        def _ldxb_chunk(k4):
            return emit("gpsimd", dg, lambda e, k4=k4: e.dma_start(
                out=_ap(QBUF, k4 * 4 * C, [[TC * C, 128], [C, 4], [1, C]]),
                in_=x_d.rearrange("(tc p) c -> p tc c", p=128)[:, 4 * k4:4 * k4 + 4, :]),
                k=16)
        t_ldxb = [None] * 4
        t_ldxb[0] = _ldxb_chunk(0)
        t_wq = emit("gpsimd", dg, lambda e: e.dma_start(
            out=_ap(wq_bf, 0, [[CC * C, 128], [C, CC], [1, C]]),
            in_=wqT_d.rearrange("(cc p) e -> p cc e", p=128)), k=16)
        t_ldxb[1] = _ldxb_chunk(1)
        t_wkv = emit("gpsimd", dg, lambda e: e.dma_start(
            out=_ap(wkv_bf, 0, [[CC * 512, 128], [512, CC], [1, 512]]),
            in_=wkvT_d.rearrange("(cc p) e -> p cc e", p=128)), k=16)
        t_ldxb[2] = _ldxb_chunk(2)
        t_ldxb[3] = _ldxb_chunk(3)
        t_wp = emit("gpsimd", dg, lambda e: e.dma_start(
            out=_ap(wp_bf, 0, [[CC * C, 128], [C, CC], [1, C]]),
            in_=wpT_d.rearrange("(cc p) e -> p cc e", p=128)), k=16)

        t_bpb = emit("sync", di, lambda e: e.dma_start(
            out=bpb[:, :], in_=_ap(bp_d, 0, [[0, 128], [1, C]])), k=16)

        evx = [None] * TC
        tpx = [None] * TC
        for tc in range(TC):
            pb = pstp[tc % 2]
            def f_tp(e, tc=tc, pb=pb):
                last = None
                for cc in range(CC):
                    last = e.transpose(
                        pb[:, cc * 128:(cc + 1) * 128],
                        QBUF[:, tc * C + cc * 128: tc * C + cc * 128 + 128],
                        idb[:, :])
                return last
            tpx[tc] = emit("tensor", pe, f_tp,
                           waits=[t_ldxb[tc // 4], t_idb, evx[tc - 2] if tc >= 2 else None])
            def f_ev(e, tc=tc, pb=pb):
                src = _ap(pb, 0, [[1024, 128], [128, CC], [1, 128]])
                dst = _ap(FM, tc * 128, [[CC * T, 128], [T, CC], [1, 128]])
                return e.tensor_copy(dst, src)
            evx[tc] = emit("vector", dve, f_ev, waits=[tpx[tc]])
        t_evx = evx[-1]

        qev, kvev = [], []
        qev_map, kvev_map = {}, {}
        allev = []
        gi = 0
        for tb in range(4):
            for ec in range(CC):
                ps = psq[gi % 4]
                def f_mm(e, ec=ec, tb=tb, ps=ps):
                    last = None
                    for cc in range(CC):
                        last = e.matmul(
                            ps[:, :],
                            wq_bf[:, cc * C + ec * 128: cc * C + ec * 128 + 128],
                            FM[:, cc * T + tb * 512: cc * T + tb * 512 + 512],
                            start=(cc == 0), stop=(cc == CC - 1))
                    return last
                tmm = emit("tensor", pe, f_mm,
                           waits=[t_wq, evx[4 * tb + 3],
                                  allev[gi - 4] if gi >= 4 else None])
                def f_evq(e, ec=ec, tb=tb, ps=ps):
                    return e.activation(
                        QBUF[:, ec * T + tb * 512: ec * T + tb * 512 + 512],
                        ps[:, :], AF.Copy, bias=0.0, scale=0.125)
                t_ev = emit("scalar", act, f_evq, waits=[tmm])
                qev.append(t_ev)
                qev_map[(ec, tb)] = t_ev
                allev.append(t_ev)
                gi += 1
            for ec in range(4):
                ps = psq[gi % 4]
                def f_mm(e, ec=ec, tb=tb, ps=ps):
                    last = None
                    for cc in range(CC):
                        last = e.matmul(
                            ps[:, :],
                            wkv_bf[:, cc * 512 + ec * 128: cc * 512 + ec * 128 + 128],
                            FM[:, cc * T + tb * 512: cc * T + tb * 512 + 512],
                            start=(cc == 0), stop=(cc == CC - 1))
                    return last
                tmm = emit("tensor", pe, f_mm,
                           waits=[t_wkv, evx[4 * tb + 3],
                                  allev[gi - 4] if gi >= 4 else None])
                dst = Kfm if ec < 2 else Vfm
                dec = ec if ec < 2 else ec - 2
                def f_evk(e, dst=dst, dec=dec, tb=tb, ps=ps):
                    return e.activation(
                        dst[:, dec * T + tb * 512: dec * T + tb * 512 + 512],
                        ps[:, :], AF.Copy, bias=0.0, scale=1.0)
                t_ev = emit("scalar", act, f_evk, waits=[tmm])
                kvev.append(t_ev)
                kvev_map[(ec, tb)] = t_ev
                allev.append(t_ev)
                gi += 1
        t_kvev_all = kvev[-1]

        t_vzero = emit("vector", dve, lambda e: e.memset(Vhat[:, :], 0.0), waits=[])
        vh = []
        for ccp in range(2):
            for sc in range(TC):
                i = ccp * TC + sc
                pb = pstp[i % 2]
                def f_tp(e, ccp=ccp, sc=sc, pb=pb):
                    return e.transpose(
                        pb[:, 0:128],
                        Vfm[:, ccp * T + sc * 128: ccp * T + sc * 128 + 128],
                        idb[:, :])
                tw = [kvev_map[(2 + ccp, sc // 4)], vh[i - 2] if i >= 2 else None, t_vzero]
                if i < 2:
                    tw.append(t_evx)
                t_tpv = emit("tensor", pe, f_tp, waits=tw)
                def f_ev(e, ccp=ccp, sc=sc, pb=pb):
                    g0 = 2 * ccp
                    src = _ap(pb, 0, [[1024, 128], [64, 2], [1, 64]])
                    dst = _ap(Vhat, (g0 * TC + sc) * VSL,
                              [[KVH * TC * VSL, 128], [TC * VSL, 2], [1, 64]])
                    return e.tensor_copy(dst, src)
                vh.append(emit("vector", dve, f_ev, waits=[t_tpv]))
        def f_ones(e):
            last = None
            for k in range(KVH * TC):
                last = e.memset(Vhat[:, k * VSL + 64: k * VSL + 65], 1.0)
            return last
        t_vones = emit("vector", dve, f_ones, fences=[t_vzero])
        t_vhat = vh[-1]

        nrm_rd = []
        sq_last = None
        for pc in range(CC + 2):
            if pc < CC:
                def f_sq(e, pc=pc):
                    return e.activation(q2s[:, 0:T], QBUF[:, pc * T:(pc + 1) * T],
                                        AF.Square, bias=0.0, scale=1.0)
                sqw = [qev_map[(pc, 3)], nrm_rd[-1] if nrm_rd else None, t_vhat]
            else:
                def f_sq(e, pc=pc):
                    return e.activation(q2s[:, 0:T], Kfm[:, (pc - CC) * T:(pc - CC + 1) * T],
                                        AF.Square, bias=0.0, scale=1.0)
                sqw = [t_kvev_all, nrm_rd[-1] if nrm_rd else None, t_vhat]
            t_sq = emit("scalar", act, f_sq, waits=sqw)
            sq_last = t_sq
            for tb in range(4):
                k = pc * 4 + tb
                psx = psq[k % 3]
                def f_nm(e, tb=tb, psx=psx):
                    return e.matmul(psx[0:1, :], onc[:, :],
                                    q2s[:, tb * 512:(tb + 1) * 512],
                                    start=True, stop=True)
                t_nm = emit("tensor", pe, f_nm,
                            waits=[t_sq, t_onc,
                                   nrm_rd[k - 3] if k >= 3 else allev[44 + k]])
                def f_rd(e, pc=pc, tb=tb, psx=psx):
                    return e.tensor_reduce(mrow[0:1, 16 + pc * 4 + tb:17 + pc * 4 + tb],
                                           psx[0:1, :], AX.X, ALU.max)
                nrm_rd.append(emit("vector", dve, f_rd, waits=[t_nm]))
        def f_mx(e):
            last = None
            for pc in range(CC + 2):
                last = e.tensor_reduce(mrow[0:1, pc:pc + 1],
                                       mrow[0:1, 16 + 4 * pc:20 + 4 * pc],
                                       AX.X, ALU.max)
            return last
        nrm_rd.append(emit("vector", dve, f_mx, fences=[nrm_rd[-1]]))
        def f_prod(e):
            last = None
            for p in range(NPAIR):
                last = e.tensor_tensor(mrow[0:1, 24 + p:25 + p], mrow[0:1, p:p + 1],
                                       mrow[0:1, 8 + (p % 2):9 + (p % 2)], ALU.mult)
            return last
        t_prod = emit("vector", dve, f_prod, fences=[nrm_rd[-1]])
        t_ln = emit("scalar", act, lambda e: e.activation(
            mrow[0:1, 40:48], mrow[0:1, 24:32], AF.Ln, bias=0.0, scale=1.0),
            waits=[t_prod])
        t_ex = emit("scalar", act, lambda e: e.activation(
            mrow[0:1, 48:56], mrow[0:1, 40:48], AF.Exp, bias=0.0, scale=0.5),
            fences=[t_ln])
        t_mbc = emit("tensor", pe, lambda e: e.matmul(
            psn[:, 0:8], onrf[:, :], mrow[0:1, 48:56], start=True, stop=True),
            waits=[t_ex, t_onrf, nrm_rd[-1]])
        def f_negM(e):
            last = None
            for p in range(NPAIR):
                last = e.tensor_scalar(negM[p][:, :], psn[:, p:p + 1], -1.0, None,
                                       ALU.mult)
            return last
        t_negM = emit("vector", dve, f_negM, waits=[t_mbc])

        tap_gate_c = []
        tap_gate_d = []
        if DEBUG_TAPS:
            tap_gate_c.append(emit("sync", so, lambda e: e.dma_start(
                out=taps["tap_xfm"], in_=FM[:, :]), waits=[t_evx], k=16))
            tap_gate_d.append(emit("sync", so, lambda e: e.dma_start(
                out=taps["tap_q"], in_=QBUF[:, :]), waits=[qev[-1]], k=16))
            tap_gate_d.append(emit("sync", so, lambda e: e.dma_start(
                out=taps["tap_k"], in_=Kfm[:, :]), waits=[t_kvev_all], k=16))
            tap_gate_d.append(emit("sync", so, lambda e: e.dma_start(
                out=taps["tap_vhat"], in_=Vhat[:, :]), waits=[t_vhat, t_vones], k=16))
            for p in range(NPAIR):
                tap_gate_d.append(emit(
                    "sync", so,
                    lambda e, p=p: e.dma_start(out=taps["tap_negm"][:, p:p + 1],
                                               in_=negM[p][:, :],
                                               allow_slow_non_contiguous=True),
                    waits=[t_negM], k=16))

        sB.close()
        sC = ExitStack()
        EC = sC.enter_context
        Ew2 = [EC(nc.sbuf_tensor(f"Ew2_{i}", [128, 2 * 2432], BF16)) for i in range(2)]
        Pb = [EC(nc.sbuf_tensor(f"Pb{i}", [128, 1024], BF16)) for i in range(2)]
        PPb = [EC(nc.sbuf_tensor(f"PPb{i}", [128, 1024], BF16)) for i in range(3)]
        Osb = EC(nc.sbuf_tensor("Osb", [128, T], BF16))
        sig = EC(nc.sbuf_tensor("sig", [128, 512], F32))
        lnsig = EC(nc.sbuf_tensor("lnsig", [128, 512], F32))
        rs = EC(nc.sbuf_tensor("rs", [128, 512], BF16))
        STAB = [EC(nc.psum_tensor(f"STAB{i}", [128, 1024], F32)) for i in range(2)]
        OH = [[EC(nc.psum_tensor(f"OH{i}{hh}", [128, 512], F32)) for hh in range(2)]
              for i in range(2)]

        it = 0
        it_info = {}
        ew_ld, ew_war = {}, {}
        t_sig0 = emit("vector", dve, lambda e: e.memset(sig[:, :], 1.0),
                      waits=[t_mbc])
        exp_t, emu_t, mav_t = {}, {}, {}
        fmu_g, sgc_t, oev_t, rsg_t, mms_t = {}, {}, {}, {}, {}
        pair_last_emu = {}
        pending_sigma = [None]

        def sigma_s1(p, qb, qpar, qbg, last_it):
            lA, lB = mav_t[(last_it, 0)], mav_t[(last_it, 1)]
            def f_sgc(e, qpar=qpar):
                e.tensor_copy(sig[0:1, 0:512], OH[qpar][0][64:65, :])
                return e.tensor_copy(sig[64:65, 0:512], OH[qpar][1][64:65, :])
            sgc_t[qbg] = emit("vector", dve, f_sgc,
                              waits=[lA, lB, mms_t.get(qbg - 1)])

        def sigma_s2(p, qb, qpar, qbg, last_it):
            lA, lB = mav_t[(last_it, 0)], mav_t[(last_it, 1)]
            def f_oev(e, qb=qb, qpar=qpar):
                e.tensor_copy(Osb[0:64, qb * 512:(qb + 1) * 512], OH[qpar][0][0:64, :])
                return e.tensor_copy(Osb[64:128, qb * 512:(qb + 1) * 512],
                                     OH[qpar][1][0:64, :])
            oev_t[qbg] = emit("vector", dve, f_oev,
                              waits=[lA, lB, fmu_g.get(qbg - 4)])

        def sigma_s3(p, qb, qpar, qbg, last_it):
            rsg_a = emit("scalar", act, lambda e: e.activation(
                lnsig[:, :], sig[:, :], AF.Ln,
                bias=0.0, scale=1.0), waits=[sgc_t[qbg]])
            rsg_t[qbg] = emit("scalar", act, lambda e: e.activation(
                rs[:, :], lnsig[:, :], AF.Exp,
                bias=0.0, scale=-1.0), fences=[rsg_a],
                waits=[mms_t.get(qbg - 1)])

        def sigma_s4(p, qb, qpar, qbg, last_it):
            def f_mms(e, qpar=qpar):
                e.matmul(OH[qpar][0][0:64, :], on2[0:1, 0:64],
                         rs[0:1, 0:512], start=True, stop=True,
                         tile_position=(0, 0))
                return e.matmul(OH[qpar][0][64:128, :], on2[64:65, 0:64],
                                rs[64:65, 0:512],
                                start=True, stop=True, tile_position=(64, 64))
            mms_t[qbg] = emit("tensor", pe, f_mms,
                              waits=[rsg_t[qbg], t_on2,
                                     sgc_t[qbg], oev_t[qbg]])

        def sigma_s5(p, qb, qpar, qbg, last_it):
            def f_fmu(e, p=p, qb=qb, qpar=qpar):
                return e.tensor_tensor(
                    FM[:, p * T + qb * 512: p * T + (qb + 1) * 512],
                    Osb[:, qb * 512:(qb + 1) * 512], OH[qpar][0][:, :], ALU.mult)
            fmu_g[qbg] = emit("vector", dve, f_fmu,
                              waits=[mms_t[qbg], oev_t[qbg]] +
                                    (tap_gate_c if qbg == 0 else []))
        for p in range(NPAIR):
            hA, hB = 2 * p, 2 * p + 1
            gA, gB = hA % KVH, hB % KVH
            kc = p % 2
            def emit_ewld(p, qb):
                qbg = p * 4 + qb
                eww = [ew_war.get(qbg - 2)]
                if qbg <= 1:
                    eww += [t_mbc, t_negM]
                src = _ap(ewt_d, p * 128 * 2 * EWID + qb * 512,
                          [[2 * EWID, 128], [EWID, 2], [1, 2432]])
                ew_ld[qbg] = emit("gpsimd", dg, lambda e, src=src, qb=qb: e.dma_start(
                    out=Ew2[qb % 2][:, :], in_=src), waits=eww, k=16)
            if p == 0:
                emit_ewld(0, 0)
                emit_ewld(0, 1)
            def emit_mav(itx):
                if (itx, 0) in mav_t:
                    return
                px, qbx, scx = it_info[itx]
                par3 = itx % 3
                qpar = qbx % 2
                qbg = px * 4 + qbx
                gs = (2 * px % KVH, (2 * px + 1) % KVH)
                def f_av(e, gs=gs, scx=scx, par3=par3, qpar=qpar):
                    last = None
                    for hh in range(2):
                        last = e.matmul(
                            OH[qpar][hh][0:65, :],
                            Vhat[:, (gs[hh] * TC + scx) * 65:
                                 (gs[hh] * TC + scx) * 65 + 65],
                            PPb[par3][:, hh * 512: hh * 512 + 512],
                            start=(scx == 0), stop=(scx == TC - 1))
                    return last
                avw = [emu_t[itx]]
                if scx == 0:
                    avw.append(fmu_g.get(qbg - 2))
                ev = emit("tensor", pe, f_av, waits=avw)
                mav_t[(itx, 0)] = ev
                mav_t[(itx, 1)] = ev

            for qb in range(4):
                qbg = p * 4 + qb
                qpar = qb % 2
                for sc in range(TC):
                    it_info[it] = (p, qb, sc)
                    par = it % 2
                    def f_st(e, p=p, qb=qb, sc=sc, kc=kc, par=par):
                        e.matmul(
                            STAB[par][:, 0:512],
                            Kfm[0:64, kc * T + sc * 128: kc * T + sc * 128 + 128],
                            QBUF[0:64, p * T + qb * 512: p * T + qb * 512 + 512],
                            start=True, stop=True)
                        return e.matmul(
                            STAB[par][:, 512:1024],
                            Kfm[64:128, kc * T + sc * 128: kc * T + sc * 128 + 128],
                            QBUF[64:128, p * T + qb * 512: p * T + qb * 512 + 512],
                            start=True, stop=True)
                    stw = [exp_t.get(it - 2)]
                    if it == 0:
                        stw += [qev[-1], t_kvev_all, t_negM, t_vhat, t_vones,
                                sq_last]
                    t_st = emit("tensor", pe, f_st, waits=stw)
                    if it >= 3:
                        emit_mav(it - 3)
                    if pending_sigma[0] is not None:
                        if sc == 5:
                            sigma_s1(*pending_sigma[0])
                        elif sc == 6:
                            sigma_s2(*pending_sigma[0])
                        elif sc == 7:
                            sigma_s3(*pending_sigma[0])
                        elif sc == 11:
                            sigma_s4(*pending_sigma[0])
                        elif sc == 13:
                            sigma_s5(*pending_sigma[0])
                            pending_sigma[0] = None
                    def f_exp(e, par=par, p=p):
                        return e.activation(Pb[par][:, :], STAB[par][:, :], AF.Exp,
                                            bias=negM[p][:, :], scale=1.0)
                    exp_t[it] = emit("scalar", act, f_exp,
                                     waits=[t_st, emu_t.get(it - 2)])
                    offb = 1920 - sc * 128
                    def f_emu(e, par=par, par3=it % 3, offb=offb, qb=qb):
                        o3 = _ap(PPb[par3], 0, [[1024, 128], [512, 2], [1, 512]])
                        i3 = _ap(Pb[par], 0, [[1024, 128], [512, 2], [1, 512]])
                        e3 = _ap(Ew2[qb % 2], offb, [[2 * 2432, 128], [2432, 2], [1, 512]])
                        return e.tensor_tensor(o3, i3, e3, ALU.mult)
                    emu_t[it] = emit("vector", dve, f_emu,
                                     waits=[exp_t[it], mav_t.get((it - 3, 0)),
                                            mav_t.get((it - 3, 1))] +
                                           ([ew_ld[qbg]] if sc == 0 else []))
                    it += 1
                ew_war[qbg] = emu_t[it - 1]
                if qbg + 2 < NPAIR * 4:
                    emit_ewld((qbg + 2) // 4, (qbg + 2) % 4)
                if qbg == NPAIR * 4 - 1:
                    emit_mav(it - 3)
                    emit_mav(it - 2)
                    emit_mav(it - 1)
                pending_sigma[0] = (p, qb, qpar, qbg, it - 1)
            pair_last_emu[p] = emu_t[it - 1]
        sigma_s1(*pending_sigma[0])
        sigma_s2(*pending_sigma[0])
        sigma_s3(*pending_sigma[0])
        sigma_s4(*pending_sigma[0])
        sigma_s5(*pending_sigma[0])
        pending_sigma[0] = None
        t_fmu_last = fmu_g[NPAIR * 4 - 1]

        sC.close()
        sBC.close()

        sD = ExitStack()
        ED = sD.enter_context
        r_sb = [ED(nc.sbuf_tensor(f"r_sb{i}", [128, C], F32)) for i in range(3)]
        sq = [ED(nc.sbuf_tensor(f"sq{i}", [128, C], F32)) for i in range(3)]
        x2bf = [ED(nc.sbuf_tensor(f"x2bf{i}", [128, C], BF16)) for i in range(2)]
        lnS = [ED(nc.sbuf_tensor(f"lnS{i}", [128, 1], F32)) for i in range(3)]
        lnM = [ED(nc.sbuf_tensor(f"lnM{i}", [128, 1], F32)) for i in range(3)]
        lnV = [ED(nc.sbuf_tensor(f"lnV{i}", [128, 1], F32)) for i in range(3)]
        lnR = [ED(nc.sbuf_tensor(f"lnR{i}", [128, 1], F32)) for i in range(3)]
        psD = [ED(nc.psum_tensor(f"psD{i}", [128, 512], F32)) for i in range(6)]
        pstp2 = [ED(nc.psum_tensor(f"pstp2{i}", [128, 1024], BF16)) for i in range(2)]

        tap_gate_e = []
        if DEBUG_TAPS:
            tap_gate_e.append(emit("sync", so, lambda e: e.dma_start(
                out=taps["tap_ofm"], in_=FM[:, :]), waits=[t_fmu_last], k=16))
        t_dbar = emit("gpsimd", dg, lambda e: e.nop(),
                      waits=[t_fmu_last] + tap_gate_d)

        res_t, xn_t, xbf_t, tp2_t, ev2_t = [], [None] * TC, [None] * TC, [None] * TC, [None] * TC

        def emit_tp2(tcx):
            parx = tcx % 2
            pb = pstp2[parx]
            def f_tp2(e, parx=parx, pb=pb):
                last = None
                for cc in range(CC):
                    last = e.transpose(pb[:, cc * 128:(cc + 1) * 128],
                                       x2bf[parx][:, cc * 128:(cc + 1) * 128],
                                       idb[:, :])
                return last
            tp2_t[tcx] = emit("tensor", pe, f_tp2,
                              waits=[xbf_t[tcx],
                                     ev2_t[tcx - 2] if tcx >= 2 else None])
            def f_ev2(e, tcx=tcx, pb=pb):
                src = _ap(pb, 0, [[1024, 128], [128, CC], [1, 128]])
                dst = _ap(FM, tcx * 128, [[CC * T, 128], [T, CC], [1, 128]])
                return e.tensor_copy(dst, src)
            ev2_t[tcx] = emit("vector", dve, f_ev2,
                              waits=[tp2_t[tcx]] + (tap_gate_e if tcx == 0 else []))

        gi = 0
        for tc in range(TC):
            par = tc % 3
            for eb in range(2):
                ps = psD[gi % 6]
                def f_mmp(e, tc=tc, eb=eb, ps=ps):
                    last = None
                    for cc in range(CC):
                        last = e.matmul(
                            ps[:, :],
                            FM[:, cc * T + tc * 128: cc * T + tc * 128 + 128],
                            wp_bf[:, cc * C + eb * 512: cc * C + eb * 512 + 512],
                            start=(cc == 0), stop=(cc == CC - 1))
                    return last
                mpw = [t_wp, res_t[gi - 6] if gi >= 6 else None]
                if gi == 0:
                    mpw.append(t_fmu_last)
                    mpw.append(t_dbar)
                t_mmp = emit("tensor", pe, f_mmp, waits=mpw)
                def f_res1(e, tc=tc, eb=eb, ps=ps, par=par):
                    return e.scalar_tensor_tensor(
                        r_sb[par][:, eb * 512:(eb + 1) * 512], ps[:, :], 1.0,
                        A[:, tc * C + eb * 512: tc * C + eb * 512 + 512],
                        ALU.bypass, ALU.add)
                t_res1 = emit("vector", dve, f_res1,
                              waits=[t_mmp, t_bpb, t_dbar,
                                     xn_t[tc - 3] if tc >= 3 else None])
                def f_res2(e, eb=eb, par=par):
                    sl = r_sb[par][:, eb * 512:(eb + 1) * 512]
                    return e.tensor_tensor(
                        sl, sl, bpb[:, eb * 512:(eb + 1) * 512], ALU.add)
                res_t.append(emit("vector", dve, f_res2, fences=[t_res1]))
                gi += 1
            t_cp1 = emit("scalar", act, lambda e, par=par: e.activation(
                sq[par][:, :], r_sb[par][:, :], AF.Copy, bias=0.0, scale=1.0,
                accum_out=lnS[par][:, :]), waits=[res_t[-1]])
            t_rs1 = emit("vector", dve, lambda e, par=par: e.tensor_scalar(
                lnM[par][:, :], lnS[par][:, :], -1.0 / C, None, ALU.mult),
                waits=[t_cp1])
            def f_sq1(e, par=par):
                return e.activation(sq[par][:, :], r_sb[par][:, :], AF.Square,
                                    bias=lnM[par][:, :], scale=1.0,
                                    accum_out=lnV[par][:, :])
            t_sq1 = emit("scalar", act, f_sq1, waits=[t_rs1])
            t_rsa = emit("scalar", act, lambda e, par=par: e.activation(
                lnR[par][:, :], lnV[par][:, :], AF.Ln,
                bias=epsv[:, :], scale=1.0 / C), fences=[t_sq1], waits=[t_eps])
            t_rstd = emit("scalar", act, lambda e, par=par: e.activation(
                lnR[par][:, :], lnR[par][:, :], AF.Exp,
                bias=0.0, scale=-0.5), fences=[t_rsa])
            def f_xn(e, tc=tc, par=par):
                return e.tensor_scalar(A[:, tc * C:(tc + 1) * C], r_sb[par][:, :],
                                       lnM[par][:, :], lnR[par][:, :],
                                       ALU.add, ALU.mult)
            xn_t[tc] = emit("vector", dve, f_xn, waits=[t_rstd])
            def f_xbf(e, tc=tc):
                return e.activation(x2bf[tc % 2][:, :], A[:, tc * C:(tc + 1) * C],
                                    AF.Copy, bias=0.0, scale=1.0)
            if tc >= 2:
                emit_tp2(tc - 2)
            xbf_t[tc] = emit("scalar", act, f_xbf,
                             waits=[xn_t[tc], tp2_t[tc - 2] if tc >= 2 else None])
        emit_tp2(TC - 2)
        emit_tp2(TC - 1)
        if DEBUG_TAPS:
            emit("sync", so, lambda e: e.dma_start(
                out=taps["tap_r15"], in_=r_sb[1][:, :]), waits=[res_t[-1]], k=16)
            for i, tsr in enumerate([lnS[1], lnM[1], lnV[1], lnR[1]]):
                emit("sync", so, lambda e, i=i, tsr=tsr: e.dma_start(
                    out=taps["tap_ln"][:, i:i + 1], in_=tsr[:, :],
                    allow_slow_non_contiguous=True), waits=[xn_t[TC - 1]], k=16)
        tap_gate_f = []
        if DEBUG_TAPS:
            tap_gate_f.append(emit("sync", so, lambda e: e.dma_start(
                out=taps["tap_x2"], in_=A[:, :]), waits=[xn_t[TC - 1]], k=16))

        sD.close()

        sE = ExitStack()
        EE = sE.enter_context
        h_fm = EE(nc.sbuf_tensor("h_fm", [128, 24 * 1024], BF16))
        wg_s = [EE(nc.sbuf_tensor(f"wg_s{i}", [128, CC * 256], BF16)) for i in range(2)]
        wu_s = [EE(nc.sbuf_tensor(f"wu_s{i}", [128, CC * 256], BF16)) for i in range(2)]
        wd_e = [EE(nc.sbuf_tensor(f"wd_e{i}", [128, 24 * 128], BF16)) for i in range(2)]
        gsil = [EE(nc.sbuf_tensor(f"gsil{i}", [128, 512], BF16)) for i in range(2)]
        x3s = [EE(nc.sbuf_tensor(f"x3s{i}", [128, 512], F32)) for i in range(2)]
        ln2S = [EE(nc.sbuf_tensor(f"ln2S{i}", [128, 1], F32)) for i in range(2)]
        ln2M = [EE(nc.sbuf_tensor(f"ln2M{i}", [128, 1], F32)) for i in range(TC)]
        ln2V = [EE(nc.sbuf_tensor(f"ln2V{i}", [128, 1], F32)) for i in range(TC)]
        ln2R = [EE(nc.sbuf_tensor(f"ln2R{i}", [128, 1], F32)) for i in range(TC)]
        sq2 = [EE(nc.sbuf_tensor("sq2e", [128, C], F32))] * 2
        psg = [EE(nc.psum_tensor(f"psg{i}", [128, 512], F32)) for i in range(2)]
        psu = [EE(nc.psum_tensor(f"psu{i}", [128, 512], F32)) for i in range(2)]
        psd = [EE(nc.psum_tensor(f"psd{i}", [128, 512], F32)) for i in range(2)]
        ps3 = [EE(nc.psum_tensor(f"ps3{i}", [128, 512], F32)) for i in range(2)]

        wgT_r = wgT_d.rearrange("(cc p) h -> p cc h", p=128)
        wuT_r = wuT_d.rearrange("(cc p) h -> p cc h", p=128)
        wdT_r = wdT_d.rearrange("(hcb p) d -> p hcb d", p=128)

        emit_ln2_fns = {}
        ln2rv_t, ln2r_t = {}, {}
        mmg_t, mmu_t, sil_t, hmu_t = {}, {}, {}, {}
        mmd_t, xev_t, tp3_t, radd_t = {}, {}, {}, {}
        xn2_t = {}
        lwg_t, lwu_t, lwd_t = {}, {}, {}
        sto_t = {}

        def mk_lg(tt, sl):
            def f_lg(e, sl=sl):
                return e.dma_start(
                    out=wg_s[sl % 2].ap().rearrange("p (cc h) -> p cc h", cc=CC),
                    in_=wgT_r[:, :, sl * 256:(sl + 1) * 256])
            def f_lu(e, sl=sl):
                return e.dma_start(
                    out=wu_s[sl % 2].ap().rearrange("p (cc h) -> p cc h", cc=CC),
                    in_=wuT_r[:, :, sl * 256:(sl + 1) * 256])
            war_g = mmg_t.get((tt, sl - 2, 1, 1)) or mmg_t.get((tt - 1, NSL - 2 + sl, 1, 1))
            war_u = mmu_t.get((tt, sl - 2, 1, 1)) or mmu_t.get((tt - 1, NSL - 2 + sl, 1, 1))
            lww = [t_fmu_last] if (tt == 0 and sl < 2) else []
            lwg_t[(tt, sl)] = emit("gpsimd", dg, f_lg, waits=lww + [war_g], k=16)
            lwu_t[(tt, sl)] = emit("gpsimd", dg, f_lu, waits=lww + [war_u], k=16)

        def mk_sto(tti):
            def f_sto(e, tti=tti):
                src = _ap(A, tti * 8 * C, [[TC * C, 128], [C, 8], [1, C]])
                dst = out_d.rearrange("(g p) c -> p g c",
                                      p=128)[:, 8 * tti:8 * (tti + 1), :]
                return e.dma_start(out=dst, in_=src)
            return f_sto

        def mk_ld(tt, ec):
            def f_ld(e, ec=ec):
                return e.dma_start(
                    out=wd_e[ec % 2].ap().rearrange("p (hcb d) -> p hcb d", hcb=24),
                    in_=wdT_r[:, :, ec * 128:(ec + 1) * 128])
            war_d = mmd_t.get((tt, ec - 2, 1)) or mmd_t.get((tt - 1, CC - 2 + ec, 1))
            lwd_t[(tt, ec)] = emit("gpsimd", dg, f_ld,
                                   waits=([t_fmu_last] if (tt == 0 and ec < 2) else []) + [war_d],
                                   k=16)

        for tt in range(2):
            for sl in (0, 1):
                mk_lg(tt, sl)

            idx = 0
            for sl in range(NSL):
                for hb2 in range(2):
                    hcb = sl * 2 + hb2
                    for tb2 in range(2):
                        pg, pu = psg[idx % 2], psu[idx % 2]
                        def f_mg(e, sl=sl, hb2=hb2, tb2=tb2, pg=pg, tt=tt):
                            last = None
                            for cc in range(CC):
                                last = e.matmul(
                                    pg[:, :],
                                    wg_s[sl % 2][:, cc * 256 + hb2 * 128: cc * 256 + hb2 * 128 + 128],
                                    FM[:, cc * T + tt * 1024 + tb2 * 512: cc * T + tt * 1024 + tb2 * 512 + 512],
                                    start=(cc == 0), stop=(cc == CC - 1))
                            return last
                        mgw = [lwg_t[(tt, sl)], sil_t.get((tt, idx - 2))]
                        if tt == 0 and idx == 0:
                            mgw.append(ev2_t[7])
                            mgw.append(res_t[30])
                        if tt == 0 and idx == 1:
                            mgw.append(res_t[31])
                        if tt == 1 and idx == 0:
                            mgw.append(ev2_t[TC - 1])
                        mmg_t[(tt, sl, hb2, tb2)] = emit("tensor", pe, f_mg, waits=mgw)
                        def f_mu(e, sl=sl, hb2=hb2, tb2=tb2, pu=pu, tt=tt):
                            last = None
                            for cc in range(CC):
                                last = e.matmul(
                                    pu[:, :],
                                    wu_s[sl % 2][:, cc * 256 + hb2 * 128: cc * 256 + hb2 * 128 + 128],
                                    FM[:, cc * T + tt * 1024 + tb2 * 512: cc * T + tt * 1024 + tb2 * 512 + 512],
                                    start=(cc == 0), stop=(cc == CC - 1))
                            return last
                        muw = [lwu_t[(tt, sl)], hmu_t.get((tt, idx - 2))]
                        if idx < 2 and tt == 1:
                            muw.append(mmd_t.get((0, CC - 1, 1)))
                        if tt == 0 and idx == 0:
                            muw.append(res_t[26])
                        if tt == 0 and idx == 1:
                            muw.append(res_t[27])
                        mmu_t[(tt, sl, hb2, tb2)] = emit("tensor", pe, f_mu, waits=muw)
                        def f_sil(e, pg=pg, idx=idx):
                            return e.activation(gsil[idx % 2][:, :], pg[:, :], AF.Silu,
                                                bias=0.0, scale=DEQ)
                        sil_t[(tt, idx)] = emit(
                            "scalar", act, f_sil,
                            waits=[mmg_t[(tt, sl, hb2, tb2)], hmu_t.get((tt, idx - 2))])
                        def f_hmu(e, pu=pu, idx=idx, hcb=hcb, tb2=tb2):
                            return e.scalar_tensor_tensor(
                                h_fm[:, hcb * 1024 + tb2 * 512: hcb * 1024 + tb2 * 512 + 512],
                                pu[:, :], DEQ, gsil[idx % 2][:, :], ALU.mult, ALU.mult)
                        hmw = [mmu_t[(tt, sl, hb2, tb2)], sil_t[(tt, idx)]]
                        if idx < 2 and tt == 1:
                            hmw.append(mmd_t.get((0, CC - 1, 1)))
                        if idx < 2 and tt == 0:
                            hmw.append(xbf_t[TC - 1])
                        hmu_t[(tt, idx)] = emit("vector", dve, f_hmu, waits=hmw)
                        idx += 1
                if sl + 2 < NSL:
                    mk_lg(tt, sl + 2)
                if sl == NSL - 1:
                    for ec0 in (0, 1):
                        mk_ld(tt, ec0)
                if tt == 1 and sl < 8:
                    emit_ln2_fns[0](sl)
            t_hmu_last = hmu_t[(tt, idx - 1)]
            if DEBUG_TAPS and tt == 0:
                tap_gate_h = [emit("sync", so, lambda e: e.dma_start(
                    out=taps["tap_hfm"], in_=h_fm[:, :]), waits=[t_hmu_last], k=16)]
            elif DEBUG_TAPS and tt == 1:
                hmu_t[(1, -2)] = tap_gate_h[0]


            j = 0
            for ec in range(CC):
                for tb2 in range(2):
                    pd = psd[j % 2]
                    def f_md(e, ec=ec, tb2=tb2, pd=pd, tt=tt):
                        last = None
                        for hcb in range(24):
                            last = e.matmul(
                                pd[:, :],
                                wd_e[ec % 2][:, hcb * 128:(hcb + 1) * 128],
                                h_fm[:, hcb * 1024 + tb2 * 512: hcb * 1024 + tb2 * 512 + 512],
                                start=(hcb == 0), stop=(hcb == 23))
                        return last
                    mdw = [lwd_t[(tt, ec)], t_hmu_last, xev_t.get((tt, j - 2))]
                    if tt == 0 and j == 0:
                        mdw.append(res_t[28])
                    if tt == 0 and j == 1:
                        mdw.append(res_t[29])
                    mmd_t[(tt, ec, tb2)] = emit("tensor", pe, f_md, waits=mdw)
                    def f_xev(e, pd=pd, j=j):
                        return e.tensor_scalar(x3s[j % 2][:, :], pd[:, :], DEQ, None,
                                               ALU.mult)
                    xev_t[(tt, j)] = emit("vector", dve, f_xev,
                                          waits=[mmd_t[(tt, ec, tb2)],
                                                 tp3_t.get((tt, j - 2))])
                    p3 = ps3[j % 2]
                    def f_tp3(e, j=j, p3=p3):
                        last = None
                        for jj in range(4):
                            last = e.transpose(p3[:, jj * 128:(jj + 1) * 128],
                                               x3s[j % 2][:, jj * 128:(jj + 1) * 128],
                                               idf[:, :])
                        return last
                    tp3w = [xev_t[(tt, j)], t_idf, radd_t.get((tt, j - 2))]
                    if tt == 0 and j == 0:
                        tp3w.append(ev2_t[TC - 2])
                    if tt == 0 and j == 1:
                        tp3w.append(ev2_t[TC - 1])
                    tp3_t[(tt, j)] = emit("tensor", pe, f_tp3, waits=tp3w)
                    def f_radd(e, j=j, p3=p3, ec=ec, tb2=tb2, tt=tt):
                        last = None
                        for jj in range(4):
                            tcj = tt * 8 + tb2 * 4 + jj
                            sl_ = A[:, tcj * C + ec * 128: tcj * C + ec * 128 + 128]
                            last = e.tensor_tensor(sl_, p3[:, jj * 128:(jj + 1) * 128],
                                                   sl_, ALU.add)
                        return last
                    radd_t[(tt, j)] = emit("vector", dve, f_radd,
                                           waits=[tp3_t[(tt, j)]] +
                                                 (tap_gate_f if (tt == 0 and j == 0) else []))
                    j += 1
                    if tt == 1 and 1 <= j <= 8:
                        tc0 = j - 1
                        t_rsb0 = emit("scalar", act, lambda e, tc=tc0: e.activation(
                            ln2R[tc][:, :], ln2V[tc][:, :], AF.Ln,
                            bias=epsv[:, :], scale=1.0 / C),
                            fences=[ln2rv_t[tc0]], waits=[t_eps])
                        ln2r_t[tc0] = emit("scalar", act, lambda e, tc=tc0: e.activation(
                            ln2R[tc][:, :], ln2R[tc][:, :], AF.Exp,
                            bias=0.0, scale=-0.5), fences=[t_rsb0])
                        xn2_t[tc0] = emit("vector", dve, lambda e, tc=tc0: e.tensor_scalar(
                            A[:, tc * C:(tc + 1) * C], A[:, tc * C:(tc + 1) * C],
                            ln2M[tc][:, :], ln2R[tc][:, :],
                            ALU.add, ALU.mult), waits=[ln2r_t[tc0]])
                        if tc0 == 7:
                            sto_t[0] = emit("sync", so, mk_sto(0),
                                            waits=[xn2_t[k] for k in range(8)],
                                            k=16)
                if tb2 == 1 and ec + 2 < CC:
                    mk_ld(tt, ec + 2)

            def emit_ln2(tc):
                par = tc % 2
                t_red2 = emit("vector", dve, lambda e, tc=tc, par=par: e.tensor_reduce(
                    ln2S[par][:, :], A[:, tc * C:(tc + 1) * C], AX.X, ALU.add))
                t_rs2 = emit("vector", dve, lambda e, tc=tc, par=par: e.tensor_scalar(
                    ln2M[tc][:, :], ln2S[par][:, :], -1.0 / C, None, ALU.mult),
                    fences=[t_red2])
                def f_sq2(e, tc=tc, par=par):
                    return e.activation(sq2[par][:, :], A[:, tc * C:(tc + 1) * C],
                                        AF.Square,
                                        bias=ln2M[tc][:, :], scale=1.0,
                                        accum_out=ln2V[tc][:, :])
                ln2rv_t[tc] = emit("scalar", act, f_sq2, waits=[t_rs2],
                                   fences=[ln2rv_t.get(tc - 1)])
            emit_ln2_fns[tt] = emit_ln2

        for tc in range(8, TC):
            emit_ln2_fns[1](tc)
        for tc in range(8, TC):
            t_rsb = emit("scalar", act, lambda e, tc=tc: e.activation(
                ln2R[tc][:, :], ln2V[tc][:, :], AF.Ln,
                bias=epsv[:, :], scale=1.0 / C), fences=[ln2rv_t[tc]], waits=[t_eps])
            ln2r_t[tc] = emit("scalar", act, lambda e, tc=tc: e.activation(
                ln2R[tc][:, :], ln2R[tc][:, :], AF.Exp,
                bias=0.0, scale=-0.5), fences=[t_rsb])
        for tc in range(8, TC):
            def f_xn2(e, tc=tc):
                return e.tensor_scalar(A[:, tc * C:(tc + 1) * C],
                                       A[:, tc * C:(tc + 1) * C],
                                       ln2M[tc][:, :], ln2R[tc][:, :],
                                       ALU.add, ALU.mult)
            xn2_t[tc] = emit("vector", dve, f_xn2, waits=[ln2r_t[tc]])
        sto_t[1] = emit("sync", so, mk_sto(1),
                        waits=[xn2_t[8 + k] for k in range(8)], k=16)

        sE.close()

        def run(e, name):
            for fn, waits, c, k in ev[name]:
                for wc, wv in waits:
                    e.wait_ge(wc.h, wv)
                ins = fn(e)
                if ins is None:
                    ins = e.nop()
                ins.then_inc(c.h, k)
            e.wait_ge(so.h, so.n)

        @blk.sync
        def _(e):
            run(e, "sync")

        @blk.gpsimd
        def _(e):
            run(e, "gpsimd")

        @blk.tensor
        def _(e):
            run(e, "tensor")

        @blk.vector
        def _(e):
            run(e, "vector")

        @blk.scalar
        def _(e):
            run(e, "scalar")

    return nc


def _consts():
    u = np.arange(EWID, dtype=np.float32)[None, :]
    p = np.arange(128, dtype=np.float32)[:, None]
    t1 = np.maximum(u - 1920.0 - p, 0.0)
    ewt = np.empty((NPAIR, 128, 2 * EWID), np.float32)
    for pp in range(NPAIR):
        ewt[pp, :, 0:EWID] = np.exp(-SLOPES[2 * pp] * t1)
        ewt[pp, :, EWID:] = np.exp(-SLOPES[2 * pp + 1] * t1)
    return {
        "ewt": ewt.reshape(NPAIR * 128, 2 * EWID).astype(ml_dtypes.bfloat16),
        "idb": np.eye(128, dtype=ml_dtypes.bfloat16),
        "idf": np.eye(128, dtype=np.float32),
        "onc": np.ones((128, 1), dtype=ml_dtypes.bfloat16),
        "onrb": np.ones((1, 128), dtype=ml_dtypes.bfloat16),
        "on2": np.ones((128, 128), dtype=ml_dtypes.bfloat16),
        "onrf": np.ones((1, 128), dtype=np.float32),
    }


def kernel(L0_attn_x, L0_attn_Wq, L0_attn_Wkv, L0_attn_Wproj, L0_attn_bproj,
           L0_ffn_w_gate, L0_ffn_w_up, L0_ffn_w_down):
    x = np.ascontiguousarray(np.asarray(L0_attn_x, dtype=np.float32))
    wqT = np.ascontiguousarray(np.asarray(L0_attn_Wq, dtype=np.float32).T)
    wkvT = np.ascontiguousarray(np.asarray(L0_attn_Wkv, dtype=np.float32).T)
    wpT = np.ascontiguousarray(np.asarray(L0_attn_Wproj, dtype=np.float32).T)
    bp = np.ascontiguousarray(np.asarray(L0_attn_bproj, dtype=np.float32))
    wgT = np.ascontiguousarray(np.asarray(L0_ffn_w_gate, dtype=np.int32).T)
    wuT = np.ascontiguousarray(np.asarray(L0_ffn_w_up, dtype=np.int32).T)
    wdT = np.ascontiguousarray(np.asarray(L0_ffn_w_down, dtype=np.int32).T)
    cst = _consts()

    nc = build_nc()
    in_maps = []
    for b in range(B):
        m = dict(x=np.ascontiguousarray(x[b]), wqT=wqT, wkvT=wkvT, wpT=wpT,
                 bproj=bp, wgT=wgT, wuT=wuT, wdT=wdT)
        m.update(cst)
        in_maps.append(m)
    res = run_bass_kernel_spmd(nc, in_maps, core_ids=list(range(B)))
    return np.stack([res.results[i]["out"] for i in range(B)], axis=0)


if __name__ == "__main__":
    rng = np.random.default_rng(0)
    ins = {
        "L0_attn_x": rng.standard_normal((B, T, C), dtype=np.float32),
        "L0_attn_Wq": (rng.standard_normal((C, C)) * 0.02).astype(np.float32),
        "L0_attn_Wkv": (rng.standard_normal((512, C)) * 0.02).astype(np.float32),
        "L0_attn_Wproj": (rng.standard_normal((C, C)) * 0.02).astype(np.float32),
        "L0_attn_bproj": rng.standard_normal(C).astype(np.float32),
        "L0_ffn_w_gate": rng.integers(-127, 128, (HID, C)).astype(np.int32),
        "L0_ffn_w_up": rng.integers(-127, 128, (HID, C)).astype(np.int32),
        "L0_ffn_w_down": rng.integers(-127, 128, (C, HID)).astype(np.int32),
    }
    out = kernel(**ins)
    print(out.shape, out.dtype, np.abs(out).mean())


# version 41
# speedup vs baseline: 1.0223x; 1.0223x over previous
import numpy as np
import ml_dtypes
import concourse.bass as bass
import concourse.mybir as mybir
from concourse.bass_utils import run_bass_kernel_spmd
from contextlib import ExitStack

F32 = mybir.dt.float32
BF16 = mybir.dt.bfloat16
I32 = mybir.dt.int32
AF = mybir.ActivationFunctionType
ALU = mybir.AluOpType
AX = mybir.AxisListType

B, T, C = 8, 2048, 1024
H, KVH, D = 16, 4, 64
HID = 3072
TC = T // 128
CC = C // 128
NPAIR = H // 2
EWID = 3968
QB = 1024
DEQ = 1.0 / 127.0
EPS = 1e-5
NSL = 12

_start = 2.0 ** (-(2.0 ** (-(np.log2(H) - 3.0))))
SLOPES = [float(_start ** (h + 1)) / 8.0 for h in range(H)]

ENGINES = ("sync", "gpsimd", "tensor", "vector", "scalar")
DEBUG_TAPS = False


class Ctr:
    __slots__ = ("h", "n")

    def __init__(self, h):
        self.h = h
        self.n = 0


def _ap(t, off, dims):
    th = t.tensor if isinstance(t, bass.AP) else t
    return bass.AP(th, off, dims)


def build_nc():
    nc = bass.Bass("TRN2", target_bir_lowering=False)

    x_d = nc.dram_tensor("x", [T, C], F32, kind="ExternalInput").ap()
    wqT_d = nc.dram_tensor("wqT", [C, C], F32, kind="ExternalInput").ap()
    wkvT_d = nc.dram_tensor("wkvT", [C, 2 * KVH * D], F32, kind="ExternalInput").ap()
    wpT_d = nc.dram_tensor("wpT", [C, C], F32, kind="ExternalInput").ap()
    bp_d = nc.dram_tensor("bproj", [C], F32, kind="ExternalInput").ap()
    wgT_d = nc.dram_tensor("wgT", [C, HID], I32, kind="ExternalInput").ap()
    wuT_d = nc.dram_tensor("wuT", [C, HID], I32, kind="ExternalInput").ap()
    wdT_d = nc.dram_tensor("wdT", [HID, C], I32, kind="ExternalInput").ap()
    ewt_d = nc.dram_tensor("ewt", [NPAIR * 128, 2 * EWID], BF16, kind="ExternalInput").ap()
    idb_d = nc.dram_tensor("idb", [128, 128], BF16, kind="ExternalInput").ap()
    idf_d = nc.dram_tensor("idf", [128, 128], F32, kind="ExternalInput").ap()
    onc_d = nc.dram_tensor("onc", [128, 1], BF16, kind="ExternalInput").ap()
    onrb_d = nc.dram_tensor("onrb", [1, 128], BF16, kind="ExternalInput").ap()
    on2_d = nc.dram_tensor("on2", [128, 128], BF16, kind="ExternalInput").ap()
    onrf_d = nc.dram_tensor("onrf", [1, 128], F32, kind="ExternalInput").ap()
    out_d = nc.dram_tensor("out", [T, C], F32, kind="ExternalOutput").ap()

    taps = {}
    if DEBUG_TAPS:
        for nm, shp, dt in [
            ("tap_xfm", [128, CC * T], BF16), ("tap_q", [128, CC * T], BF16),
            ("tap_k", [128, 2 * T], BF16), ("tap_vhat", [128, KVH * TC * 65], BF16),
            ("tap_negm", [128, NPAIR], F32), ("tap_ofm", [128, CC * T], BF16),
            ("tap_x2", [128, TC * C], F32), ("tap_hfm", [128, 24 * 1024], BF16),
            ("tap_x2fm", [128, CC * T], BF16), ("tap_r15", [128, C], F32),
            ("tap_ln", [128, 4], F32),
        ]:
            taps[nm] = nc.dram_tensor(nm, shp, dt, kind="ExternalOutput").ap()

    ev = {k: [] for k in ENGINES}

    def emit(eng, c, fn, waits=(), k=1, fences=()):
        prev_full = None
        if k >= 16:
            c = _dma_ctr(c)
            if c.n > 0:
                prev_full = (c, c.n)
        c.n += k
        ww = {}
        for t in waits:
            if t is None:
                continue
            wc, wv = t
            if wc is c:
                continue
            ww[wc] = max(ww.get(wc, 0), wv)
        for t in fences:
            if t is None:
                continue
            wc, wv = t
            ww[wc] = max(ww.get(wc, 0), wv)
        if prev_full is not None:
            wc, wv = prev_full
            ww[wc] = max(ww.get(wc, 0), wv)
        ev[eng].append((fn, list(ww.items()), c, k))
        return (c, c.n)

    with ExitStack() as top:
        E = top.enter_context
        A = E(nc.sbuf_tensor("A", [128, TC * C], F32))
        FM = E(nc.sbuf_tensor("FM", [128, CC * T], BF16))
        idb = E(nc.sbuf_tensor("idb_s", [128, 128], BF16))
        idf = E(nc.sbuf_tensor("idf_s", [128, 128], F32))
        onc = E(nc.sbuf_tensor("onc_s", [128, 1], BF16))
        onrb = E(nc.sbuf_tensor("onrb_s", [1, 128], BF16))
        on2 = E(nc.sbuf_tensor("on2_s", [128, 128], BF16))
        onrf = E(nc.sbuf_tensor("onrf_s", [1, 128], F32))
        negM = [E(nc.sbuf_tensor(f"negM{i}", [128, 1], F32)) for i in range(NPAIR)]
        epsv = E(nc.sbuf_tensor("epsv", [128, 1], F32))
        wp_bf = E(nc.sbuf_tensor("wp_bf", [128, CC * C], BF16))
        bpb = E(nc.sbuf_tensor("bpb", [128, C], F32))

        sems = {k: Ctr(E(nc.semaphore(f"s_{k}"))) for k in
                ("di", "dg", "pe", "act", "dve", "out")}
        di, dg, pe, act, dve, so = (sems[k] for k in ("di", "dg", "pe", "act", "dve", "out"))
        DIP = [Ctr(E(nc.semaphore(f"s_dip{i}"))) for i in range(6)]
        DGP = [Ctr(E(nc.semaphore(f"s_dgp{i}"))) for i in range(8)]
        _pool_idx = {"di": 0, "dg": 0}

        def _dma_ctr(c):
            if c is di:
                _pool_idx["di"] += 1
                return DIP[_pool_idx["di"] % len(DIP)]
            if c is dg:
                _pool_idx["dg"] += 1
                return DGP[_pool_idx["dg"] % len(DGP)]
            return c

        blk = E(nc.Block())

        sBC = ExitStack()
        EB = sBC.enter_context
        QBUF = EB(nc.sbuf_tensor("QBUF", [128, CC * T], BF16))
        Kfm = EB(nc.sbuf_tensor("Kfm", [128, 2 * T], BF16))
        VSL = 65
        Vhat = EB(nc.sbuf_tensor("Vhat", [128, KVH * TC * VSL], BF16))

        sB = ExitStack()
        EB2 = sB.enter_context
        wq_bf = EB2(nc.sbuf_tensor("wq_bf", [128, CC * C], BF16))
        wkv_bf = EB2(nc.sbuf_tensor("wkv_bf", [128, CC * 512], BF16))
        Vfm = EB2(nc.sbuf_tensor("Vfm", [128, 2 * T], BF16))
        q2s = Vfm
        mrow = EB2(nc.sbuf_tensor("mrow", [1, 64], F32))

        psq = [EB2(nc.psum_tensor(f"psq{i}", [128, 512], F32)) for i in range(4)]
        pstp = [EB2(nc.psum_tensor(f"pstp{i}", [128, 1024], BF16)) for i in range(2)]
        psn = EB2(nc.psum_tensor("psn", [128, 512], F32))

        t_eps = emit("vector", dve, lambda e: e.memset(epsv[:, :], EPS))
        t_ldx = emit("sync", di, lambda e: e.dma_start(
            out=_ap(A, 0, [[TC * C, 128], [C, TC], [1, C]]),
            in_=x_d.rearrange("(tc p) c -> p tc c", p=128)), k=16)
        t_idb = emit("sync", di, lambda e: e.dma_start(out=idb[:, :], in_=idb_d), k=16)
        t_idf = emit("sync", di, lambda e: e.dma_start(out=idf[:, :], in_=idf_d), k=16)
        t_onc = emit("sync", di, lambda e: e.dma_start(out=onc[:, :], in_=onc_d), k=16)
        t_onrb = emit("sync", di, lambda e: e.dma_start(out=onrb[:, :], in_=onrb_d), k=16)
        t_on2 = emit("sync", di, lambda e: e.dma_start(out=on2[:, :], in_=on2_d), k=16)
        t_onrf = emit("sync", di, lambda e: e.dma_start(out=onrf[:, :], in_=onrf_d), k=16)
        def _ldxb_chunk(k4):
            return emit("gpsimd", dg, lambda e, k4=k4: e.dma_start(
                out=_ap(QBUF, k4 * 4 * C, [[TC * C, 128], [C, 4], [1, C]]),
                in_=x_d.rearrange("(tc p) c -> p tc c", p=128)[:, 4 * k4:4 * k4 + 4, :]),
                k=16)
        t_ldxb = [None] * 4
        t_ldxb[0] = _ldxb_chunk(0)
        t_wq = emit("gpsimd", dg, lambda e: e.dma_start(
            out=_ap(wq_bf, 0, [[CC * C, 128], [C, CC], [1, C]]),
            in_=wqT_d.rearrange("(cc p) e -> p cc e", p=128)), k=16)
        t_ldxb[1] = _ldxb_chunk(1)
        t_wkv = emit("gpsimd", dg, lambda e: e.dma_start(
            out=_ap(wkv_bf, 0, [[CC * 512, 128], [512, CC], [1, 512]]),
            in_=wkvT_d.rearrange("(cc p) e -> p cc e", p=128)), k=16)
        t_ldxb[2] = _ldxb_chunk(2)
        t_ldxb[3] = _ldxb_chunk(3)
        t_wp = emit("gpsimd", dg, lambda e: e.dma_start(
            out=_ap(wp_bf, 0, [[CC * C, 128], [C, CC], [1, C]]),
            in_=wpT_d.rearrange("(cc p) e -> p cc e", p=128)), k=16)

        t_bpb = emit("sync", di, lambda e: e.dma_start(
            out=bpb[:, :], in_=_ap(bp_d, 0, [[0, 128], [1, C]])), k=16)

        evx = [None] * TC
        tpx = [None] * TC
        for tc in range(TC):
            pb = pstp[tc % 2]
            def f_tp(e, tc=tc, pb=pb):
                last = None
                for cc in range(CC):
                    last = e.transpose(
                        pb[:, cc * 128:(cc + 1) * 128],
                        QBUF[:, tc * C + cc * 128: tc * C + cc * 128 + 128],
                        idb[:, :])
                return last
            tpx[tc] = emit("tensor", pe, f_tp,
                           waits=[t_ldxb[tc // 4], t_idb, evx[tc - 2] if tc >= 2 else None])
            def f_ev(e, tc=tc, pb=pb):
                src = _ap(pb, 0, [[1024, 128], [128, CC], [1, 128]])
                dst = _ap(FM, tc * 128, [[CC * T, 128], [T, CC], [1, 128]])
                return e.tensor_copy(dst, src)
            evx[tc] = emit("vector", dve, f_ev, waits=[tpx[tc]])
        t_evx = evx[-1]

        qev, kvev = [], []
        qev_map, kvev_map = {}, {}
        allev = []
        gi = 0
        for tb in range(4):
            for ec in range(CC):
                ps = psq[gi % 4]
                def f_mm(e, ec=ec, tb=tb, ps=ps):
                    last = None
                    for cc in range(CC):
                        last = e.matmul(
                            ps[:, :],
                            wq_bf[:, cc * C + ec * 128: cc * C + ec * 128 + 128],
                            FM[:, cc * T + tb * 512: cc * T + tb * 512 + 512],
                            start=(cc == 0), stop=(cc == CC - 1))
                    return last
                tmm = emit("tensor", pe, f_mm,
                           waits=[t_wq, evx[4 * tb + 3],
                                  allev[gi - 4] if gi >= 4 else None])
                def f_evq(e, ec=ec, tb=tb, ps=ps):
                    return e.activation(
                        QBUF[:, ec * T + tb * 512: ec * T + tb * 512 + 512],
                        ps[:, :], AF.Copy, bias=0.0, scale=0.125)
                t_ev = emit("scalar", act, f_evq, waits=[tmm])
                qev.append(t_ev)
                qev_map[(ec, tb)] = t_ev
                allev.append(t_ev)
                gi += 1
            for ec in range(4):
                ps = psq[gi % 4]
                def f_mm(e, ec=ec, tb=tb, ps=ps):
                    last = None
                    for cc in range(CC):
                        last = e.matmul(
                            ps[:, :],
                            wkv_bf[:, cc * 512 + ec * 128: cc * 512 + ec * 128 + 128],
                            FM[:, cc * T + tb * 512: cc * T + tb * 512 + 512],
                            start=(cc == 0), stop=(cc == CC - 1))
                    return last
                tmm = emit("tensor", pe, f_mm,
                           waits=[t_wkv, evx[4 * tb + 3],
                                  allev[gi - 4] if gi >= 4 else None])
                dst = Kfm if ec < 2 else Vfm
                dec = ec if ec < 2 else ec - 2
                def f_evk(e, dst=dst, dec=dec, tb=tb, ps=ps):
                    return e.activation(
                        dst[:, dec * T + tb * 512: dec * T + tb * 512 + 512],
                        ps[:, :], AF.Copy, bias=0.0, scale=1.0)
                t_ev = emit("scalar", act, f_evk, waits=[tmm])
                kvev.append(t_ev)
                kvev_map[(ec, tb)] = t_ev
                allev.append(t_ev)
                gi += 1
        t_kvev_all = kvev[-1]

        t_vzero = emit("vector", dve, lambda e: e.memset(Vhat[:, :], 0.0), waits=[])
        vh = []
        for ccp in range(2):
            for sc in range(TC):
                i = ccp * TC + sc
                pb = pstp[i % 2]
                def f_tp(e, ccp=ccp, sc=sc, pb=pb):
                    return e.transpose(
                        pb[:, 0:128],
                        Vfm[:, ccp * T + sc * 128: ccp * T + sc * 128 + 128],
                        idb[:, :])
                tw = [kvev_map[(2 + ccp, sc // 4)], vh[i - 2] if i >= 2 else None, t_vzero]
                if i < 2:
                    tw.append(t_evx)
                t_tpv = emit("tensor", pe, f_tp, waits=tw)
                def f_ev(e, ccp=ccp, sc=sc, pb=pb):
                    g0 = 2 * ccp
                    src = _ap(pb, 0, [[1024, 128], [64, 2], [1, 64]])
                    dst = _ap(Vhat, (g0 * TC + sc) * VSL,
                              [[KVH * TC * VSL, 128], [TC * VSL, 2], [1, 64]])
                    return e.tensor_copy(dst, src)
                vh.append(emit("vector", dve, f_ev, waits=[t_tpv]))
        def f_ones(e):
            last = None
            for k in range(KVH * TC):
                last = e.memset(Vhat[:, k * VSL + 64: k * VSL + 65], 1.0)
            return last
        t_vones = emit("vector", dve, f_ones, fences=[t_vzero])
        t_vhat = vh[-1]

        nrm_rd = []
        sq_last = None
        for pc in range(CC + 2):
            qo = (pc % 2) * T
            war_sq = nrm_rd[(pc - 2) * 4 + 3] if pc >= 2 else None
            if pc < CC:
                def f_sq(e, pc=pc, qo=qo):
                    return e.activation(q2s[:, qo:qo + T], QBUF[:, pc * T:(pc + 1) * T],
                                        AF.Square, bias=0.0, scale=1.0)
                sqw = [qev_map[(pc, 3)], war_sq, t_vhat]
            else:
                def f_sq(e, pc=pc, qo=qo):
                    return e.activation(q2s[:, qo:qo + T], Kfm[:, (pc - CC) * T:(pc - CC + 1) * T],
                                        AF.Square, bias=0.0, scale=1.0)
                sqw = [t_kvev_all, war_sq, t_vhat]
            t_sq = emit("scalar", act, f_sq, waits=sqw)
            sq_last = t_sq
            for tb in range(4):
                k = pc * 4 + tb
                psx = psq[k % 3]
                def f_nm(e, tb=tb, psx=psx, qo=qo):
                    return e.matmul(psx[0:1, :], onc[:, :],
                                    q2s[:, qo + tb * 512: qo + (tb + 1) * 512],
                                    start=True, stop=True)
                t_nm = emit("tensor", pe, f_nm,
                            waits=[t_sq, t_onc,
                                   nrm_rd[k - 3] if k >= 3 else allev[44 + k]])
                def f_rd(e, pc=pc, tb=tb, psx=psx):
                    return e.tensor_reduce(mrow[0:1, 16 + pc * 4 + tb:17 + pc * 4 + tb],
                                           psx[0:1, :], AX.X, ALU.max)
                nrm_rd.append(emit("vector", dve, f_rd, waits=[t_nm]))
        def f_mx(e):
            last = None
            for pc in range(CC + 2):
                last = e.tensor_reduce(mrow[0:1, pc:pc + 1],
                                       mrow[0:1, 16 + 4 * pc:20 + 4 * pc],
                                       AX.X, ALU.max)
            return last
        nrm_rd.append(emit("vector", dve, f_mx, fences=[nrm_rd[-1]]))
        def f_prod(e):
            last = None
            for p in range(NPAIR):
                last = e.tensor_tensor(mrow[0:1, 24 + p:25 + p], mrow[0:1, p:p + 1],
                                       mrow[0:1, 8 + (p % 2):9 + (p % 2)], ALU.mult)
            return last
        t_prod = emit("vector", dve, f_prod, fences=[nrm_rd[-1]])
        t_ln = emit("scalar", act, lambda e: e.activation(
            mrow[0:1, 40:48], mrow[0:1, 24:32], AF.Ln, bias=0.0, scale=1.0),
            waits=[t_prod])
        t_ex = emit("scalar", act, lambda e: e.activation(
            mrow[0:1, 48:56], mrow[0:1, 40:48], AF.Exp, bias=0.0, scale=0.5),
            fences=[t_ln])
        t_mbc = emit("tensor", pe, lambda e: e.matmul(
            psn[:, 0:8], onrf[:, :], mrow[0:1, 48:56], start=True, stop=True),
            waits=[t_ex, t_onrf, nrm_rd[-1]])
        def f_negM(e):
            last = None
            for p in range(NPAIR):
                last = e.tensor_scalar(negM[p][:, :], psn[:, p:p + 1], -1.0, None,
                                       ALU.mult)
            return last
        t_negM = emit("vector", dve, f_negM, waits=[t_mbc])

        tap_gate_c = []
        tap_gate_d = []
        if DEBUG_TAPS:
            tap_gate_c.append(emit("sync", so, lambda e: e.dma_start(
                out=taps["tap_xfm"], in_=FM[:, :]), waits=[t_evx], k=16))
            tap_gate_d.append(emit("sync", so, lambda e: e.dma_start(
                out=taps["tap_q"], in_=QBUF[:, :]), waits=[qev[-1]], k=16))
            tap_gate_d.append(emit("sync", so, lambda e: e.dma_start(
                out=taps["tap_k"], in_=Kfm[:, :]), waits=[t_kvev_all], k=16))
            tap_gate_d.append(emit("sync", so, lambda e: e.dma_start(
                out=taps["tap_vhat"], in_=Vhat[:, :]), waits=[t_vhat, t_vones], k=16))
            for p in range(NPAIR):
                tap_gate_d.append(emit(
                    "sync", so,
                    lambda e, p=p: e.dma_start(out=taps["tap_negm"][:, p:p + 1],
                                               in_=negM[p][:, :],
                                               allow_slow_non_contiguous=True),
                    waits=[t_negM], k=16))

        sB.close()
        sC = ExitStack()
        EC = sC.enter_context
        Ew2 = [EC(nc.sbuf_tensor(f"Ew2_{i}", [128, 2 * 2432], BF16)) for i in range(2)]
        Pb = [EC(nc.sbuf_tensor(f"Pb{i}", [128, 1024], BF16)) for i in range(2)]
        PPb = [EC(nc.sbuf_tensor(f"PPb{i}", [128, 1024], BF16)) for i in range(3)]
        Osb = EC(nc.sbuf_tensor("Osb", [128, T], BF16))
        sig = EC(nc.sbuf_tensor("sig", [128, 512], F32))
        lnsig = EC(nc.sbuf_tensor("lnsig", [128, 512], F32))
        rs = EC(nc.sbuf_tensor("rs", [128, 512], BF16))
        STAB = [EC(nc.psum_tensor(f"STAB{i}", [128, 1024], F32)) for i in range(2)]
        OH = [[EC(nc.psum_tensor(f"OH{i}{hh}", [128, 512], F32)) for hh in range(2)]
              for i in range(2)]

        it = 0
        it_info = {}
        ew_ld, ew_war = {}, {}
        t_sig0 = emit("vector", dve, lambda e: e.memset(sig[:, :], 1.0),
                      waits=[t_mbc])
        exp_t, emu_t, mav_t = {}, {}, {}
        fmu_g, sgc_t, oev_t, rsg_t, mms_t = {}, {}, {}, {}, {}
        pair_last_emu = {}
        pending_sigma = [None]

        def sigma_s1(p, qb, qpar, qbg, last_it):
            lA, lB = mav_t[(last_it, 0)], mav_t[(last_it, 1)]
            def f_sgc(e, qpar=qpar):
                e.tensor_copy(sig[0:1, 0:512], OH[qpar][0][64:65, :])
                return e.tensor_copy(sig[64:65, 0:512], OH[qpar][1][64:65, :])
            sgc_t[qbg] = emit("vector", dve, f_sgc,
                              waits=[lA, lB, mms_t.get(qbg - 1)])

        def sigma_s2(p, qb, qpar, qbg, last_it):
            lA, lB = mav_t[(last_it, 0)], mav_t[(last_it, 1)]
            def f_oev(e, qb=qb, qpar=qpar):
                e.tensor_copy(Osb[0:64, qb * 512:(qb + 1) * 512], OH[qpar][0][0:64, :])
                return e.tensor_copy(Osb[64:128, qb * 512:(qb + 1) * 512],
                                     OH[qpar][1][0:64, :])
            oev_t[qbg] = emit("vector", dve, f_oev,
                              waits=[lA, lB, fmu_g.get(qbg - 4)])

        def sigma_s3(p, qb, qpar, qbg, last_it):
            rsg_a = emit("scalar", act, lambda e: e.activation(
                lnsig[:, :], sig[:, :], AF.Ln,
                bias=0.0, scale=1.0), waits=[sgc_t[qbg]])
            rsg_t[qbg] = emit("scalar", act, lambda e: e.activation(
                rs[:, :], lnsig[:, :], AF.Exp,
                bias=0.0, scale=-1.0), fences=[rsg_a],
                waits=[mms_t.get(qbg - 1)])

        def sigma_s4(p, qb, qpar, qbg, last_it):
            def f_mms(e, qpar=qpar):
                e.matmul(OH[qpar][0][0:64, :], on2[0:1, 0:64],
                         rs[0:1, 0:512], start=True, stop=True,
                         tile_position=(0, 0))
                return e.matmul(OH[qpar][0][64:128, :], on2[64:65, 0:64],
                                rs[64:65, 0:512],
                                start=True, stop=True, tile_position=(64, 64))
            mms_t[qbg] = emit("tensor", pe, f_mms,
                              waits=[rsg_t[qbg], t_on2,
                                     sgc_t[qbg], oev_t[qbg]])

        def sigma_s5(p, qb, qpar, qbg, last_it):
            def f_fmu(e, p=p, qb=qb, qpar=qpar):
                return e.tensor_tensor(
                    FM[:, p * T + qb * 512: p * T + (qb + 1) * 512],
                    Osb[:, qb * 512:(qb + 1) * 512], OH[qpar][0][:, :], ALU.mult)
            fmu_g[qbg] = emit("vector", dve, f_fmu,
                              waits=[mms_t[qbg], oev_t[qbg]] +
                                    (tap_gate_c if qbg == 0 else []))
        for p in range(NPAIR):
            hA, hB = 2 * p, 2 * p + 1
            gA, gB = hA % KVH, hB % KVH
            kc = p % 2
            def emit_ewld(p, qb):
                qbg = p * 4 + qb
                eww = [ew_war.get(qbg - 2)]
                if qbg <= 1:
                    eww += [t_mbc, t_negM]
                src = _ap(ewt_d, p * 128 * 2 * EWID + qb * 512,
                          [[2 * EWID, 128], [EWID, 2], [1, 2432]])
                ew_ld[qbg] = emit("gpsimd", dg, lambda e, src=src, qb=qb: e.dma_start(
                    out=Ew2[qb % 2][:, :], in_=src), waits=eww, k=16)
            if p == 0:
                emit_ewld(0, 0)
                emit_ewld(0, 1)
            def emit_mav(itx):
                if (itx, 0) in mav_t:
                    return
                px, qbx, scx = it_info[itx]
                par3 = itx % 3
                qpar = qbx % 2
                qbg = px * 4 + qbx
                gs = (2 * px % KVH, (2 * px + 1) % KVH)
                def f_av(e, gs=gs, scx=scx, par3=par3, qpar=qpar):
                    last = None
                    for hh in range(2):
                        last = e.matmul(
                            OH[qpar][hh][0:65, :],
                            Vhat[:, (gs[hh] * TC + scx) * 65:
                                 (gs[hh] * TC + scx) * 65 + 65],
                            PPb[par3][:, hh * 512: hh * 512 + 512],
                            start=(scx == 0), stop=(scx == TC - 1))
                    return last
                avw = [emu_t[itx]]
                if scx == 0:
                    avw.append(fmu_g.get(qbg - 2))
                ev = emit("tensor", pe, f_av, waits=avw)
                mav_t[(itx, 0)] = ev
                mav_t[(itx, 1)] = ev

            for qb in range(4):
                qbg = p * 4 + qb
                qpar = qb % 2
                for sc in range(TC):
                    it_info[it] = (p, qb, sc)
                    par = it % 2
                    def f_st(e, p=p, qb=qb, sc=sc, kc=kc, par=par):
                        e.matmul(
                            STAB[par][:, 0:512],
                            Kfm[0:64, kc * T + sc * 128: kc * T + sc * 128 + 128],
                            QBUF[0:64, p * T + qb * 512: p * T + qb * 512 + 512],
                            start=True, stop=True)
                        return e.matmul(
                            STAB[par][:, 512:1024],
                            Kfm[64:128, kc * T + sc * 128: kc * T + sc * 128 + 128],
                            QBUF[64:128, p * T + qb * 512: p * T + qb * 512 + 512],
                            start=True, stop=True)
                    stw = [exp_t.get(it - 2)]
                    if it == 0:
                        stw += [qev[-1], t_kvev_all, t_negM, t_vhat, t_vones,
                                sq_last]
                    t_st = emit("tensor", pe, f_st, waits=stw)
                    if it >= 3:
                        emit_mav(it - 3)
                    if pending_sigma[0] is not None:
                        if sc == 5:
                            sigma_s1(*pending_sigma[0])
                        elif sc == 6:
                            sigma_s2(*pending_sigma[0])
                        elif sc == 7:
                            sigma_s3(*pending_sigma[0])
                        elif sc == 11:
                            sigma_s4(*pending_sigma[0])
                        elif sc == 13:
                            sigma_s5(*pending_sigma[0])
                            pending_sigma[0] = None
                    def f_exp(e, par=par, p=p):
                        return e.activation(Pb[par][:, :], STAB[par][:, :], AF.Exp,
                                            bias=negM[p][:, :], scale=1.0)
                    exp_t[it] = emit("scalar", act, f_exp,
                                     waits=[t_st, emu_t.get(it - 2)])
                    offb = 1920 - sc * 128
                    def f_emu(e, par=par, par3=it % 3, offb=offb, qb=qb):
                        o3 = _ap(PPb[par3], 0, [[1024, 128], [512, 2], [1, 512]])
                        i3 = _ap(Pb[par], 0, [[1024, 128], [512, 2], [1, 512]])
                        e3 = _ap(Ew2[qb % 2], offb, [[2 * 2432, 128], [2432, 2], [1, 512]])
                        return e.tensor_tensor(o3, i3, e3, ALU.mult)
                    emu_t[it] = emit("vector", dve, f_emu,
                                     waits=[exp_t[it], mav_t.get((it - 3, 0)),
                                            mav_t.get((it - 3, 1))] +
                                           ([ew_ld[qbg]] if sc == 0 else []))
                    it += 1
                ew_war[qbg] = emu_t[it - 1]
                if qbg + 2 < NPAIR * 4:
                    emit_ewld((qbg + 2) // 4, (qbg + 2) % 4)
                if qbg == NPAIR * 4 - 1:
                    emit_mav(it - 3)
                    emit_mav(it - 2)
                    emit_mav(it - 1)
                pending_sigma[0] = (p, qb, qpar, qbg, it - 1)
            pair_last_emu[p] = emu_t[it - 1]
        sigma_s1(*pending_sigma[0])
        sigma_s2(*pending_sigma[0])
        sigma_s3(*pending_sigma[0])
        sigma_s4(*pending_sigma[0])
        sigma_s5(*pending_sigma[0])
        pending_sigma[0] = None
        t_fmu_last = fmu_g[NPAIR * 4 - 1]

        sC.close()
        sBC.close()

        sD = ExitStack()
        ED = sD.enter_context
        r_sb = [ED(nc.sbuf_tensor(f"r_sb{i}", [128, C], F32)) for i in range(3)]
        sq = [ED(nc.sbuf_tensor(f"sq{i}", [128, C], F32)) for i in range(3)]
        x2bf = [ED(nc.sbuf_tensor(f"x2bf{i}", [128, C], BF16)) for i in range(2)]
        lnS = [ED(nc.sbuf_tensor(f"lnS{i}", [128, 1], F32)) for i in range(3)]
        lnM = [ED(nc.sbuf_tensor(f"lnM{i}", [128, 1], F32)) for i in range(3)]
        lnV = [ED(nc.sbuf_tensor(f"lnV{i}", [128, 1], F32)) for i in range(3)]
        lnR = [ED(nc.sbuf_tensor(f"lnR{i}", [128, 1], F32)) for i in range(3)]
        psD = [ED(nc.psum_tensor(f"psD{i}", [128, 512], F32)) for i in range(6)]
        pstp2 = [ED(nc.psum_tensor(f"pstp2{i}", [128, 1024], BF16)) for i in range(2)]

        tap_gate_e = []
        if DEBUG_TAPS:
            tap_gate_e.append(emit("sync", so, lambda e: e.dma_start(
                out=taps["tap_ofm"], in_=FM[:, :]), waits=[t_fmu_last], k=16))
        t_dbar = emit("gpsimd", dg, lambda e: e.nop(),
                      waits=[t_fmu_last] + tap_gate_d)

        res_t, xn_t, xbf_t, tp2_t, ev2_t = [], [None] * TC, [None] * TC, [None] * TC, [None] * TC

        def emit_tp2(tcx):
            parx = tcx % 2
            pb = pstp2[parx]
            def f_tp2(e, parx=parx, pb=pb):
                last = None
                for cc in range(CC):
                    last = e.transpose(pb[:, cc * 128:(cc + 1) * 128],
                                       x2bf[parx][:, cc * 128:(cc + 1) * 128],
                                       idb[:, :])
                return last
            tp2_t[tcx] = emit("tensor", pe, f_tp2,
                              waits=[xbf_t[tcx],
                                     ev2_t[tcx - 2] if tcx >= 2 else None])
            def f_ev2(e, tcx=tcx, pb=pb):
                src = _ap(pb, 0, [[1024, 128], [128, CC], [1, 128]])
                dst = _ap(FM, tcx * 128, [[CC * T, 128], [T, CC], [1, 128]])
                return e.tensor_copy(dst, src)
            ev2_t[tcx] = emit("vector", dve, f_ev2,
                              waits=[tp2_t[tcx]] + (tap_gate_e if tcx == 0 else []))

        gi = 0
        for tc in range(TC):
            par = tc % 3
            for eb in range(2):
                ps = psD[gi % 6]
                def f_mmp(e, tc=tc, eb=eb, ps=ps):
                    last = None
                    for cc in range(CC):
                        last = e.matmul(
                            ps[:, :],
                            FM[:, cc * T + tc * 128: cc * T + tc * 128 + 128],
                            wp_bf[:, cc * C + eb * 512: cc * C + eb * 512 + 512],
                            start=(cc == 0), stop=(cc == CC - 1))
                    return last
                mpw = [t_wp, res_t[gi - 6] if gi >= 6 else None]
                if gi == 0:
                    mpw.append(t_fmu_last)
                    mpw.append(t_dbar)
                t_mmp = emit("tensor", pe, f_mmp, waits=mpw)
                def f_res1(e, tc=tc, eb=eb, ps=ps, par=par):
                    return e.scalar_tensor_tensor(
                        r_sb[par][:, eb * 512:(eb + 1) * 512], ps[:, :], 1.0,
                        A[:, tc * C + eb * 512: tc * C + eb * 512 + 512],
                        ALU.bypass, ALU.add)
                t_res1 = emit("vector", dve, f_res1,
                              waits=[t_mmp, t_bpb, t_dbar,
                                     xn_t[tc - 3] if tc >= 3 else None])
                def f_res2(e, eb=eb, par=par):
                    sl = r_sb[par][:, eb * 512:(eb + 1) * 512]
                    return e.tensor_tensor(
                        sl, sl, bpb[:, eb * 512:(eb + 1) * 512], ALU.add)
                res_t.append(emit("vector", dve, f_res2, fences=[t_res1]))
                gi += 1
            t_cp1 = emit("scalar", act, lambda e, par=par: e.activation(
                sq[par][:, :], r_sb[par][:, :], AF.Copy, bias=0.0, scale=1.0,
                accum_out=lnS[par][:, :]), waits=[res_t[-1]])
            t_rs1 = emit("vector", dve, lambda e, par=par: e.tensor_scalar(
                lnM[par][:, :], lnS[par][:, :], -1.0 / C, None, ALU.mult),
                waits=[t_cp1])
            def f_sq1(e, par=par):
                return e.activation(sq[par][:, :], r_sb[par][:, :], AF.Square,
                                    bias=lnM[par][:, :], scale=1.0,
                                    accum_out=lnV[par][:, :])
            t_sq1 = emit("scalar", act, f_sq1, waits=[t_rs1])
            t_rsa = emit("scalar", act, lambda e, par=par: e.activation(
                lnR[par][:, :], lnV[par][:, :], AF.Ln,
                bias=epsv[:, :], scale=1.0 / C), fences=[t_sq1], waits=[t_eps])
            t_rstd = emit("scalar", act, lambda e, par=par: e.activation(
                lnR[par][:, :], lnR[par][:, :], AF.Exp,
                bias=0.0, scale=-0.5), fences=[t_rsa])
            def f_xn(e, tc=tc, par=par):
                return e.tensor_scalar(A[:, tc * C:(tc + 1) * C], r_sb[par][:, :],
                                       lnM[par][:, :], lnR[par][:, :],
                                       ALU.add, ALU.mult)
            xn_t[tc] = emit("vector", dve, f_xn, waits=[t_rstd])
            def f_xbf(e, tc=tc):
                return e.activation(x2bf[tc % 2][:, :], A[:, tc * C:(tc + 1) * C],
                                    AF.Copy, bias=0.0, scale=1.0)
            if tc >= 2:
                emit_tp2(tc - 2)
            xbf_t[tc] = emit("scalar", act, f_xbf,
                             waits=[xn_t[tc], tp2_t[tc - 2] if tc >= 2 else None])
        emit_tp2(TC - 2)
        emit_tp2(TC - 1)
        if DEBUG_TAPS:
            emit("sync", so, lambda e: e.dma_start(
                out=taps["tap_r15"], in_=r_sb[1][:, :]), waits=[res_t[-1]], k=16)
            for i, tsr in enumerate([lnS[1], lnM[1], lnV[1], lnR[1]]):
                emit("sync", so, lambda e, i=i, tsr=tsr: e.dma_start(
                    out=taps["tap_ln"][:, i:i + 1], in_=tsr[:, :],
                    allow_slow_non_contiguous=True), waits=[xn_t[TC - 1]], k=16)
        tap_gate_f = []
        if DEBUG_TAPS:
            tap_gate_f.append(emit("sync", so, lambda e: e.dma_start(
                out=taps["tap_x2"], in_=A[:, :]), waits=[xn_t[TC - 1]], k=16))

        sD.close()

        sE = ExitStack()
        EE = sE.enter_context
        h_fm = EE(nc.sbuf_tensor("h_fm", [128, 24 * 1024], BF16))
        wg_s = [EE(nc.sbuf_tensor(f"wg_s{i}", [128, CC * 256], BF16)) for i in range(2)]
        wu_s = [EE(nc.sbuf_tensor(f"wu_s{i}", [128, CC * 256], BF16)) for i in range(2)]
        wd_e = [EE(nc.sbuf_tensor(f"wd_e{i}", [128, 24 * 128], BF16)) for i in range(2)]
        gsil = [EE(nc.sbuf_tensor(f"gsil{i}", [128, 512], BF16)) for i in range(2)]
        x3s = [EE(nc.sbuf_tensor(f"x3s{i}", [128, 512], F32)) for i in range(2)]
        ln2S = [EE(nc.sbuf_tensor(f"ln2S{i}", [128, 1], F32)) for i in range(2)]
        ln2M = [EE(nc.sbuf_tensor(f"ln2M{i}", [128, 1], F32)) for i in range(TC)]
        ln2V = [EE(nc.sbuf_tensor(f"ln2V{i}", [128, 1], F32)) for i in range(TC)]
        ln2R = [EE(nc.sbuf_tensor(f"ln2R{i}", [128, 1], F32)) for i in range(TC)]
        sq2 = [EE(nc.sbuf_tensor("sq2e", [128, C], F32))] * 2
        psg = [EE(nc.psum_tensor(f"psg{i}", [128, 512], F32)) for i in range(2)]
        psu = [EE(nc.psum_tensor(f"psu{i}", [128, 512], F32)) for i in range(2)]
        psd = [EE(nc.psum_tensor(f"psd{i}", [128, 512], F32)) for i in range(2)]
        ps3 = [EE(nc.psum_tensor(f"ps3{i}", [128, 512], F32)) for i in range(2)]

        wgT_r = wgT_d.rearrange("(cc p) h -> p cc h", p=128)
        wuT_r = wuT_d.rearrange("(cc p) h -> p cc h", p=128)
        wdT_r = wdT_d.rearrange("(hcb p) d -> p hcb d", p=128)

        emit_ln2_fns = {}
        ln2rv_t, ln2r_t = {}, {}
        mmg_t, mmu_t, sil_t, hmu_t = {}, {}, {}, {}
        mmd_t, xev_t, tp3_t, radd_t = {}, {}, {}, {}
        xn2_t = {}
        lwg_t, lwu_t, lwd_t = {}, {}, {}
        sto_t = {}

        def mk_lg(tt, sl):
            def f_lg(e, sl=sl):
                return e.dma_start(
                    out=wg_s[sl % 2].ap().rearrange("p (cc h) -> p cc h", cc=CC),
                    in_=wgT_r[:, :, sl * 256:(sl + 1) * 256])
            def f_lu(e, sl=sl):
                return e.dma_start(
                    out=wu_s[sl % 2].ap().rearrange("p (cc h) -> p cc h", cc=CC),
                    in_=wuT_r[:, :, sl * 256:(sl + 1) * 256])
            war_g = mmg_t.get((tt, sl - 2, 1, 1)) or mmg_t.get((tt - 1, NSL - 2 + sl, 1, 1))
            war_u = mmu_t.get((tt, sl - 2, 1, 1)) or mmu_t.get((tt - 1, NSL - 2 + sl, 1, 1))
            lww = [t_fmu_last] if (tt == 0 and sl < 2) else []
            lwg_t[(tt, sl)] = emit("gpsimd", dg, f_lg, waits=lww + [war_g], k=16)
            lwu_t[(tt, sl)] = emit("gpsimd", dg, f_lu, waits=lww + [war_u], k=16)

        def mk_sto(tti):
            def f_sto(e, tti=tti):
                src = _ap(A, tti * 8 * C, [[TC * C, 128], [C, 8], [1, C]])
                dst = out_d.rearrange("(g p) c -> p g c",
                                      p=128)[:, 8 * tti:8 * (tti + 1), :]
                return e.dma_start(out=dst, in_=src)
            return f_sto

        def mk_ld(tt, ec):
            def f_ld(e, ec=ec):
                return e.dma_start(
                    out=wd_e[ec % 2].ap().rearrange("p (hcb d) -> p hcb d", hcb=24),
                    in_=wdT_r[:, :, ec * 128:(ec + 1) * 128])
            war_d = mmd_t.get((tt, ec - 2, 1)) or mmd_t.get((tt - 1, CC - 2 + ec, 1))
            lwd_t[(tt, ec)] = emit("gpsimd", dg, f_ld,
                                   waits=([t_fmu_last] if (tt == 0 and ec < 2) else []) + [war_d],
                                   k=16)

        for tt in range(2):
            for sl in (0, 1):
                mk_lg(tt, sl)

            idx = 0
            for sl in range(NSL):
                for hb2 in range(2):
                    hcb = sl * 2 + hb2
                    for tb2 in range(2):
                        pg, pu = psg[idx % 2], psu[idx % 2]
                        def f_mg(e, sl=sl, hb2=hb2, tb2=tb2, pg=pg, tt=tt):
                            last = None
                            for cc in range(CC):
                                last = e.matmul(
                                    pg[:, :],
                                    wg_s[sl % 2][:, cc * 256 + hb2 * 128: cc * 256 + hb2 * 128 + 128],
                                    FM[:, cc * T + tt * 1024 + tb2 * 512: cc * T + tt * 1024 + tb2 * 512 + 512],
                                    start=(cc == 0), stop=(cc == CC - 1))
                            return last
                        mgw = [lwg_t[(tt, sl)], sil_t.get((tt, idx - 2))]
                        if tt == 0 and idx == 0:
                            mgw.append(ev2_t[7])
                            mgw.append(res_t[30])
                        if tt == 0 and idx == 1:
                            mgw.append(res_t[31])
                        if tt == 1 and idx == 0:
                            mgw.append(ev2_t[TC - 1])
                        mmg_t[(tt, sl, hb2, tb2)] = emit("tensor", pe, f_mg, waits=mgw)
                        def f_mu(e, sl=sl, hb2=hb2, tb2=tb2, pu=pu, tt=tt):
                            last = None
                            for cc in range(CC):
                                last = e.matmul(
                                    pu[:, :],
                                    wu_s[sl % 2][:, cc * 256 + hb2 * 128: cc * 256 + hb2 * 128 + 128],
                                    FM[:, cc * T + tt * 1024 + tb2 * 512: cc * T + tt * 1024 + tb2 * 512 + 512],
                                    start=(cc == 0), stop=(cc == CC - 1))
                            return last
                        muw = [lwu_t[(tt, sl)], hmu_t.get((tt, idx - 2))]
                        if idx < 2 and tt == 1:
                            muw.append(mmd_t.get((0, CC - 1, 1)))
                        if tt == 0 and idx == 0:
                            muw.append(res_t[26])
                        if tt == 0 and idx == 1:
                            muw.append(res_t[27])
                        mmu_t[(tt, sl, hb2, tb2)] = emit("tensor", pe, f_mu, waits=muw)
                        def f_sil(e, pg=pg, idx=idx):
                            return e.activation(gsil[idx % 2][:, :], pg[:, :], AF.Silu,
                                                bias=0.0, scale=DEQ)
                        sil_t[(tt, idx)] = emit(
                            "scalar", act, f_sil,
                            waits=[mmg_t[(tt, sl, hb2, tb2)], hmu_t.get((tt, idx - 2))])
                        def f_hmu(e, pu=pu, idx=idx, hcb=hcb, tb2=tb2):
                            return e.scalar_tensor_tensor(
                                h_fm[:, hcb * 1024 + tb2 * 512: hcb * 1024 + tb2 * 512 + 512],
                                pu[:, :], DEQ, gsil[idx % 2][:, :], ALU.mult, ALU.mult)
                        hmw = [mmu_t[(tt, sl, hb2, tb2)], sil_t[(tt, idx)]]
                        if idx < 2 and tt == 1:
                            hmw.append(mmd_t.get((0, CC - 1, 1)))
                        if idx < 2 and tt == 0:
                            hmw.append(xbf_t[TC - 1])
                        hmu_t[(tt, idx)] = emit("vector", dve, f_hmu, waits=hmw)
                        idx += 1
                if sl + 2 < NSL:
                    mk_lg(tt, sl + 2)
                if sl == NSL - 1:
                    for ec0 in (0, 1):
                        mk_ld(tt, ec0)
                if tt == 1 and sl < 8:
                    emit_ln2_fns[0](sl)
            t_hmu_last = hmu_t[(tt, idx - 1)]
            if DEBUG_TAPS and tt == 0:
                tap_gate_h = [emit("sync", so, lambda e: e.dma_start(
                    out=taps["tap_hfm"], in_=h_fm[:, :]), waits=[t_hmu_last], k=16)]
            elif DEBUG_TAPS and tt == 1:
                hmu_t[(1, -2)] = tap_gate_h[0]


            j = 0
            for ec in range(CC):
                for tb2 in range(2):
                    pd = psd[j % 2]
                    def f_md(e, ec=ec, tb2=tb2, pd=pd, tt=tt):
                        last = None
                        for hcb in range(24):
                            last = e.matmul(
                                pd[:, :],
                                wd_e[ec % 2][:, hcb * 128:(hcb + 1) * 128],
                                h_fm[:, hcb * 1024 + tb2 * 512: hcb * 1024 + tb2 * 512 + 512],
                                start=(hcb == 0), stop=(hcb == 23))
                        return last
                    mdw = [lwd_t[(tt, ec)], t_hmu_last, xev_t.get((tt, j - 2))]
                    if tt == 0 and j == 0:
                        mdw.append(res_t[28])
                    if tt == 0 and j == 1:
                        mdw.append(res_t[29])
                    mmd_t[(tt, ec, tb2)] = emit("tensor", pe, f_md, waits=mdw)
                    def f_xev(e, pd=pd, j=j):
                        return e.tensor_scalar(x3s[j % 2][:, :], pd[:, :], DEQ, None,
                                               ALU.mult)
                    xev_t[(tt, j)] = emit("vector", dve, f_xev,
                                          waits=[mmd_t[(tt, ec, tb2)],
                                                 tp3_t.get((tt, j - 2))])
                    p3 = ps3[j % 2]
                    def f_tp3(e, j=j, p3=p3):
                        last = None
                        for jj in range(4):
                            last = e.transpose(p3[:, jj * 128:(jj + 1) * 128],
                                               x3s[j % 2][:, jj * 128:(jj + 1) * 128],
                                               idf[:, :])
                        return last
                    tp3w = [xev_t[(tt, j)], t_idf, radd_t.get((tt, j - 2))]
                    if tt == 0 and j == 0:
                        tp3w.append(ev2_t[TC - 2])
                    if tt == 0 and j == 1:
                        tp3w.append(ev2_t[TC - 1])
                    tp3_t[(tt, j)] = emit("tensor", pe, f_tp3, waits=tp3w)
                    def f_radd(e, j=j, p3=p3, ec=ec, tb2=tb2, tt=tt):
                        last = None
                        for jj in range(4):
                            tcj = tt * 8 + tb2 * 4 + jj
                            sl_ = A[:, tcj * C + ec * 128: tcj * C + ec * 128 + 128]
                            last = e.tensor_tensor(sl_, p3[:, jj * 128:(jj + 1) * 128],
                                                   sl_, ALU.add)
                        return last
                    radd_t[(tt, j)] = emit("vector", dve, f_radd,
                                           waits=[tp3_t[(tt, j)]] +
                                                 (tap_gate_f if (tt == 0 and j == 0) else []))
                    j += 1
                    if tt == 1 and 1 <= j <= 8:
                        tc0 = j - 1
                        t_rsb0 = emit("scalar", act, lambda e, tc=tc0: e.activation(
                            ln2R[tc][:, :], ln2V[tc][:, :], AF.Ln,
                            bias=epsv[:, :], scale=1.0 / C),
                            fences=[ln2rv_t[tc0]], waits=[t_eps])
                        ln2r_t[tc0] = emit("scalar", act, lambda e, tc=tc0: e.activation(
                            ln2R[tc][:, :], ln2R[tc][:, :], AF.Exp,
                            bias=0.0, scale=-0.5), fences=[t_rsb0])
                        xn2_t[tc0] = emit("vector", dve, lambda e, tc=tc0: e.tensor_scalar(
                            A[:, tc * C:(tc + 1) * C], A[:, tc * C:(tc + 1) * C],
                            ln2M[tc][:, :], ln2R[tc][:, :],
                            ALU.add, ALU.mult), waits=[ln2r_t[tc0]])
                        if tc0 == 7:
                            sto_t[0] = emit("sync", so, mk_sto(0),
                                            waits=[xn2_t[k] for k in range(8)],
                                            k=16)
                if tb2 == 1 and ec + 2 < CC:
                    mk_ld(tt, ec + 2)

            def emit_ln2(tc):
                par = tc % 2
                t_red2 = emit("vector", dve, lambda e, tc=tc, par=par: e.tensor_reduce(
                    ln2S[par][:, :], A[:, tc * C:(tc + 1) * C], AX.X, ALU.add))
                t_rs2 = emit("vector", dve, lambda e, tc=tc, par=par: e.tensor_scalar(
                    ln2M[tc][:, :], ln2S[par][:, :], -1.0 / C, None, ALU.mult),
                    fences=[t_red2])
                def f_sq2(e, tc=tc, par=par):
                    return e.activation(sq2[par][:, :], A[:, tc * C:(tc + 1) * C],
                                        AF.Square,
                                        bias=ln2M[tc][:, :], scale=1.0,
                                        accum_out=ln2V[tc][:, :])
                ln2rv_t[tc] = emit("scalar", act, f_sq2, waits=[t_rs2],
                                   fences=[ln2rv_t.get(tc - 1)])
            emit_ln2_fns[tt] = emit_ln2

        for tc in range(8, TC):
            emit_ln2_fns[1](tc)
        for tc in range(8, TC):
            t_rsb = emit("scalar", act, lambda e, tc=tc: e.activation(
                ln2R[tc][:, :], ln2V[tc][:, :], AF.Ln,
                bias=epsv[:, :], scale=1.0 / C), fences=[ln2rv_t[tc]], waits=[t_eps])
            ln2r_t[tc] = emit("scalar", act, lambda e, tc=tc: e.activation(
                ln2R[tc][:, :], ln2R[tc][:, :], AF.Exp,
                bias=0.0, scale=-0.5), fences=[t_rsb])
        for tc in range(8, TC):
            def f_xn2(e, tc=tc):
                return e.tensor_scalar(A[:, tc * C:(tc + 1) * C],
                                       A[:, tc * C:(tc + 1) * C],
                                       ln2M[tc][:, :], ln2R[tc][:, :],
                                       ALU.add, ALU.mult)
            xn2_t[tc] = emit("vector", dve, f_xn2, waits=[ln2r_t[tc]])
        sto_t[1] = emit("sync", so, mk_sto(1),
                        waits=[xn2_t[8 + k] for k in range(8)], k=16)

        sE.close()

        def run(e, name):
            for fn, waits, c, k in ev[name]:
                for wc, wv in waits:
                    e.wait_ge(wc.h, wv)
                ins = fn(e)
                if ins is None:
                    ins = e.nop()
                ins.then_inc(c.h, k)
            e.wait_ge(so.h, so.n)

        @blk.sync
        def _(e):
            run(e, "sync")

        @blk.gpsimd
        def _(e):
            run(e, "gpsimd")

        @blk.tensor
        def _(e):
            run(e, "tensor")

        @blk.vector
        def _(e):
            run(e, "vector")

        @blk.scalar
        def _(e):
            run(e, "scalar")

    return nc


def _consts():
    u = np.arange(EWID, dtype=np.float32)[None, :]
    p = np.arange(128, dtype=np.float32)[:, None]
    t1 = np.maximum(u - 1920.0 - p, 0.0)
    ewt = np.empty((NPAIR, 128, 2 * EWID), np.float32)
    for pp in range(NPAIR):
        ewt[pp, :, 0:EWID] = np.exp(-SLOPES[2 * pp] * t1)
        ewt[pp, :, EWID:] = np.exp(-SLOPES[2 * pp + 1] * t1)
    return {
        "ewt": ewt.reshape(NPAIR * 128, 2 * EWID).astype(ml_dtypes.bfloat16),
        "idb": np.eye(128, dtype=ml_dtypes.bfloat16),
        "idf": np.eye(128, dtype=np.float32),
        "onc": np.ones((128, 1), dtype=ml_dtypes.bfloat16),
        "onrb": np.ones((1, 128), dtype=ml_dtypes.bfloat16),
        "on2": np.ones((128, 128), dtype=ml_dtypes.bfloat16),
        "onrf": np.ones((1, 128), dtype=np.float32),
    }


def kernel(L0_attn_x, L0_attn_Wq, L0_attn_Wkv, L0_attn_Wproj, L0_attn_bproj,
           L0_ffn_w_gate, L0_ffn_w_up, L0_ffn_w_down):
    x = np.ascontiguousarray(np.asarray(L0_attn_x, dtype=np.float32))
    wqT = np.ascontiguousarray(np.asarray(L0_attn_Wq, dtype=np.float32).T)
    wkvT = np.ascontiguousarray(np.asarray(L0_attn_Wkv, dtype=np.float32).T)
    wpT = np.ascontiguousarray(np.asarray(L0_attn_Wproj, dtype=np.float32).T)
    bp = np.ascontiguousarray(np.asarray(L0_attn_bproj, dtype=np.float32))
    wgT = np.ascontiguousarray(np.asarray(L0_ffn_w_gate, dtype=np.int32).T)
    wuT = np.ascontiguousarray(np.asarray(L0_ffn_w_up, dtype=np.int32).T)
    wdT = np.ascontiguousarray(np.asarray(L0_ffn_w_down, dtype=np.int32).T)
    cst = _consts()

    nc = build_nc()
    in_maps = []
    for b in range(B):
        m = dict(x=np.ascontiguousarray(x[b]), wqT=wqT, wkvT=wkvT, wpT=wpT,
                 bproj=bp, wgT=wgT, wuT=wuT, wdT=wdT)
        m.update(cst)
        in_maps.append(m)
    res = run_bass_kernel_spmd(nc, in_maps, core_ids=list(range(B)))
    return np.stack([res.results[i]["out"] for i in range(B)], axis=0)


if __name__ == "__main__":
    rng = np.random.default_rng(0)
    ins = {
        "L0_attn_x": rng.standard_normal((B, T, C), dtype=np.float32),
        "L0_attn_Wq": (rng.standard_normal((C, C)) * 0.02).astype(np.float32),
        "L0_attn_Wkv": (rng.standard_normal((512, C)) * 0.02).astype(np.float32),
        "L0_attn_Wproj": (rng.standard_normal((C, C)) * 0.02).astype(np.float32),
        "L0_attn_bproj": rng.standard_normal(C).astype(np.float32),
        "L0_ffn_w_gate": rng.integers(-127, 128, (HID, C)).astype(np.int32),
        "L0_ffn_w_up": rng.integers(-127, 128, (HID, C)).astype(np.int32),
        "L0_ffn_w_down": rng.integers(-127, 128, (C, HID)).astype(np.int32),
    }
    out = kernel(**ins)
    print(out.shape, out.dtype, np.abs(out).mean())


# version 42
# speedup vs baseline: 1.0313x; 1.0088x over previous
import numpy as np
import ml_dtypes
import concourse.bass as bass
import concourse.mybir as mybir
from concourse.bass_utils import run_bass_kernel_spmd
from contextlib import ExitStack

F32 = mybir.dt.float32
BF16 = mybir.dt.bfloat16
I32 = mybir.dt.int32
AF = mybir.ActivationFunctionType
ALU = mybir.AluOpType
AX = mybir.AxisListType

B, T, C = 8, 2048, 1024
H, KVH, D = 16, 4, 64
HID = 3072
TC = T // 128
CC = C // 128
NPAIR = H // 2
EWID = 3968
QB = 1024
DEQ = 1.0 / 127.0
EPS = 1e-5
NSL = 12

_start = 2.0 ** (-(2.0 ** (-(np.log2(H) - 3.0))))
SLOPES = [float(_start ** (h + 1)) / 8.0 for h in range(H)]

ENGINES = ("sync", "gpsimd", "tensor", "vector", "scalar")
DEBUG_TAPS = False


class Ctr:
    __slots__ = ("h", "n")

    def __init__(self, h):
        self.h = h
        self.n = 0


def _ap(t, off, dims):
    th = t.tensor if isinstance(t, bass.AP) else t
    return bass.AP(th, off, dims)


def build_nc():
    nc = bass.Bass("TRN2", target_bir_lowering=False)

    x_d = nc.dram_tensor("x", [T, C], F32, kind="ExternalInput").ap()
    wqT_d = nc.dram_tensor("wqT", [C, C], F32, kind="ExternalInput").ap()
    wkvT_d = nc.dram_tensor("wkvT", [C, 2 * KVH * D], F32, kind="ExternalInput").ap()
    wpT_d = nc.dram_tensor("wpT", [C, C], F32, kind="ExternalInput").ap()
    bp_d = nc.dram_tensor("bproj", [C], F32, kind="ExternalInput").ap()
    wgT_d = nc.dram_tensor("wgT", [C, HID], I32, kind="ExternalInput").ap()
    wuT_d = nc.dram_tensor("wuT", [C, HID], I32, kind="ExternalInput").ap()
    wdT_d = nc.dram_tensor("wdT", [HID, C], I32, kind="ExternalInput").ap()
    ewt_d = nc.dram_tensor("ewt", [NPAIR * 128, 2 * EWID], BF16, kind="ExternalInput").ap()
    idb_d = nc.dram_tensor("idb", [128, 128], BF16, kind="ExternalInput").ap()
    idf_d = nc.dram_tensor("idf", [128, 128], F32, kind="ExternalInput").ap()
    onc_d = nc.dram_tensor("onc", [128, 1], BF16, kind="ExternalInput").ap()
    onrb_d = nc.dram_tensor("onrb", [1, 128], BF16, kind="ExternalInput").ap()
    on2_d = nc.dram_tensor("on2", [128, 128], BF16, kind="ExternalInput").ap()
    onrf_d = nc.dram_tensor("onrf", [1, 128], F32, kind="ExternalInput").ap()
    out_d = nc.dram_tensor("out", [T, C], F32, kind="ExternalOutput").ap()

    taps = {}
    if DEBUG_TAPS:
        for nm, shp, dt in [
            ("tap_xfm", [128, CC * T], BF16), ("tap_q", [128, CC * T], BF16),
            ("tap_k", [128, 2 * T], BF16), ("tap_vhat", [128, KVH * TC * 65], BF16),
            ("tap_negm", [128, NPAIR], F32), ("tap_ofm", [128, CC * T], BF16),
            ("tap_x2", [128, TC * C], F32), ("tap_hfm", [128, 24 * 1024], BF16),
            ("tap_x2fm", [128, CC * T], BF16), ("tap_r15", [128, C], F32),
            ("tap_ln", [128, 4], F32),
        ]:
            taps[nm] = nc.dram_tensor(nm, shp, dt, kind="ExternalOutput").ap()

    ev = {k: [] for k in ENGINES}

    def emit(eng, c, fn, waits=(), k=1, fences=()):
        prev_full = None
        if k >= 16:
            c = _dma_ctr(c)
            if c.n > 0:
                prev_full = (c, c.n)
        c.n += k
        ww = {}
        for t in waits:
            if t is None:
                continue
            wc, wv = t
            if wc is c:
                continue
            ww[wc] = max(ww.get(wc, 0), wv)
        for t in fences:
            if t is None:
                continue
            wc, wv = t
            ww[wc] = max(ww.get(wc, 0), wv)
        if prev_full is not None:
            wc, wv = prev_full
            ww[wc] = max(ww.get(wc, 0), wv)
        ev[eng].append((fn, list(ww.items()), c, k))
        return (c, c.n)

    with ExitStack() as top:
        E = top.enter_context
        A = E(nc.sbuf_tensor("A", [128, TC * C], F32))
        FM = E(nc.sbuf_tensor("FM", [128, CC * T], BF16))
        idb = E(nc.sbuf_tensor("idb_s", [128, 128], BF16))
        idf = E(nc.sbuf_tensor("idf_s", [128, 128], F32))
        onc = E(nc.sbuf_tensor("onc_s", [128, 1], BF16))
        onrb = E(nc.sbuf_tensor("onrb_s", [1, 128], BF16))
        on2 = E(nc.sbuf_tensor("on2_s", [128, 128], BF16))
        onrf = E(nc.sbuf_tensor("onrf_s", [1, 128], F32))
        negM = [E(nc.sbuf_tensor(f"negM{i}", [128, 1], F32)) for i in range(NPAIR)]
        epsv = E(nc.sbuf_tensor("epsv", [128, 1], F32))
        wp_bf = E(nc.sbuf_tensor("wp_bf", [128, CC * C], BF16))
        bpb = E(nc.sbuf_tensor("bpb", [128, C], F32))

        sems = {k: Ctr(E(nc.semaphore(f"s_{k}"))) for k in
                ("di", "dg", "pe", "act", "dve", "out")}
        di, dg, pe, act, dve, so = (sems[k] for k in ("di", "dg", "pe", "act", "dve", "out"))
        DIP = [Ctr(E(nc.semaphore(f"s_dip{i}"))) for i in range(6)]
        DGP = [Ctr(E(nc.semaphore(f"s_dgp{i}"))) for i in range(8)]
        _pool_idx = {"di": 0, "dg": 0}

        def _dma_ctr(c):
            if c is di:
                _pool_idx["di"] += 1
                return DIP[_pool_idx["di"] % len(DIP)]
            if c is dg:
                _pool_idx["dg"] += 1
                return DGP[_pool_idx["dg"] % len(DGP)]
            return c

        blk = E(nc.Block())

        sBC = ExitStack()
        EB = sBC.enter_context
        QBUF = EB(nc.sbuf_tensor("QBUF", [128, CC * T], BF16))
        Kfm = EB(nc.sbuf_tensor("Kfm", [128, 2 * T], BF16))
        VSL = 65
        Vhat = EB(nc.sbuf_tensor("Vhat", [128, KVH * TC * VSL], BF16))

        sB = ExitStack()
        EB2 = sB.enter_context
        wq_bf = EB2(nc.sbuf_tensor("wq_bf", [128, CC * C], BF16))
        wkv_bf = EB2(nc.sbuf_tensor("wkv_bf", [128, CC * 512], BF16))
        Vfm = EB2(nc.sbuf_tensor("Vfm", [128, 2 * T], BF16))
        q2s = Vfm
        mrow = EB2(nc.sbuf_tensor("mrow", [1, 64], F32))

        psq = [EB2(nc.psum_tensor(f"psq{i}", [128, 512], F32)) for i in range(4)]
        pstp = [EB2(nc.psum_tensor(f"pstp{i}", [128, 1024], BF16)) for i in range(2)]
        psn = EB2(nc.psum_tensor("psn", [128, 512], F32))

        t_eps = emit("vector", dve, lambda e: e.memset(epsv[:, :], EPS))
        t_ldx = emit("sync", di, lambda e: e.dma_start(
            out=_ap(A, 0, [[TC * C, 128], [C, TC], [1, C]]),
            in_=x_d.rearrange("(tc p) c -> p tc c", p=128)), k=16)
        t_idb = emit("sync", di, lambda e: e.dma_start(out=idb[:, :], in_=idb_d), k=16)
        t_idf = emit("sync", di, lambda e: e.dma_start(out=idf[:, :], in_=idf_d), k=16)
        t_onc = emit("sync", di, lambda e: e.dma_start(out=onc[:, :], in_=onc_d), k=16)
        t_onrb = emit("sync", di, lambda e: e.dma_start(out=onrb[:, :], in_=onrb_d), k=16)
        t_on2 = emit("sync", di, lambda e: e.dma_start(out=on2[:, :], in_=on2_d), k=16)
        t_onrf = emit("sync", di, lambda e: e.dma_start(out=onrf[:, :], in_=onrf_d), k=16)
        def _ldxb_chunk(k4):
            return emit("gpsimd", dg, lambda e, k4=k4: e.dma_start(
                out=_ap(QBUF, k4 * 4 * C, [[TC * C, 128], [C, 4], [1, C]]),
                in_=x_d.rearrange("(tc p) c -> p tc c", p=128)[:, 4 * k4:4 * k4 + 4, :]),
                k=16)
        t_ldxb = [None] * 4
        t_ldxb[0] = _ldxb_chunk(0)
        t_wq = emit("gpsimd", dg, lambda e: e.dma_start(
            out=_ap(wq_bf, 0, [[CC * C, 128], [C, CC], [1, C]]),
            in_=wqT_d.rearrange("(cc p) e -> p cc e", p=128)), k=16)
        t_ldxb[1] = _ldxb_chunk(1)
        t_ldxb[2] = _ldxb_chunk(2)
        t_ldxb[3] = _ldxb_chunk(3)
        t_wkv = emit("gpsimd", dg, lambda e: e.dma_start(
            out=_ap(wkv_bf, 0, [[CC * 512, 128], [512, CC], [1, 512]]),
            in_=wkvT_d.rearrange("(cc p) e -> p cc e", p=128)), k=16)
        t_wp = emit("gpsimd", dg, lambda e: e.dma_start(
            out=_ap(wp_bf, 0, [[CC * C, 128], [C, CC], [1, C]]),
            in_=wpT_d.rearrange("(cc p) e -> p cc e", p=128)), k=16)

        t_bpb = emit("sync", di, lambda e: e.dma_start(
            out=bpb[:, :], in_=_ap(bp_d, 0, [[0, 128], [1, C]])), k=16)

        evx = [None] * TC
        tpx = [None] * TC
        for tc in range(TC):
            pb = pstp[tc % 2]
            def f_tp(e, tc=tc, pb=pb):
                last = None
                for cc in range(CC):
                    last = e.transpose(
                        pb[:, cc * 128:(cc + 1) * 128],
                        QBUF[:, tc * C + cc * 128: tc * C + cc * 128 + 128],
                        idb[:, :])
                return last
            tpx[tc] = emit("tensor", pe, f_tp,
                           waits=[t_ldxb[tc // 4], t_idb, evx[tc - 2] if tc >= 2 else None])
            def f_ev(e, tc=tc, pb=pb):
                src = _ap(pb, 0, [[1024, 128], [128, CC], [1, 128]])
                dst = _ap(FM, tc * 128, [[CC * T, 128], [T, CC], [1, 128]])
                return e.tensor_copy(dst, src)
            evx[tc] = emit("vector", dve, f_ev, waits=[tpx[tc]])
        t_evx = evx[-1]

        qev, kvev = [], []
        qev_map, kvev_map = {}, {}
        allev = []
        gi = 0
        for tb in range(4):
            for ec in range(CC):
                ps = psq[gi % 4]
                def f_mm(e, ec=ec, tb=tb, ps=ps):
                    last = None
                    for cc in range(CC):
                        last = e.matmul(
                            ps[:, :],
                            wq_bf[:, cc * C + ec * 128: cc * C + ec * 128 + 128],
                            FM[:, cc * T + tb * 512: cc * T + tb * 512 + 512],
                            start=(cc == 0), stop=(cc == CC - 1))
                    return last
                tmm = emit("tensor", pe, f_mm,
                           waits=[t_wq, evx[4 * tb + 3],
                                  allev[gi - 4] if gi >= 4 else None])
                def f_evq(e, ec=ec, tb=tb, ps=ps):
                    return e.activation(
                        QBUF[:, ec * T + tb * 512: ec * T + tb * 512 + 512],
                        ps[:, :], AF.Copy, bias=0.0, scale=0.125)
                t_ev = emit("scalar", act, f_evq, waits=[tmm])
                qev.append(t_ev)
                qev_map[(ec, tb)] = t_ev
                allev.append(t_ev)
                gi += 1
            for ec in range(4):
                ps = psq[gi % 4]
                def f_mm(e, ec=ec, tb=tb, ps=ps):
                    last = None
                    for cc in range(CC):
                        last = e.matmul(
                            ps[:, :],
                            wkv_bf[:, cc * 512 + ec * 128: cc * 512 + ec * 128 + 128],
                            FM[:, cc * T + tb * 512: cc * T + tb * 512 + 512],
                            start=(cc == 0), stop=(cc == CC - 1))
                    return last
                tmm = emit("tensor", pe, f_mm,
                           waits=[t_wkv, evx[4 * tb + 3],
                                  allev[gi - 4] if gi >= 4 else None])
                dst = Kfm if ec < 2 else Vfm
                dec = ec if ec < 2 else ec - 2
                def f_evk(e, dst=dst, dec=dec, tb=tb, ps=ps):
                    return e.activation(
                        dst[:, dec * T + tb * 512: dec * T + tb * 512 + 512],
                        ps[:, :], AF.Copy, bias=0.0, scale=1.0)
                t_ev = emit("scalar", act, f_evk, waits=[tmm])
                kvev.append(t_ev)
                kvev_map[(ec, tb)] = t_ev
                allev.append(t_ev)
                gi += 1
        t_kvev_all = kvev[-1]

        t_vzero = emit("vector", dve, lambda e: e.memset(Vhat[:, :], 0.0), waits=[])
        vh = []
        for ccp in range(2):
            for sc in range(TC):
                i = ccp * TC + sc
                pb = pstp[i % 2]
                def f_tp(e, ccp=ccp, sc=sc, pb=pb):
                    return e.transpose(
                        pb[:, 0:128],
                        Vfm[:, ccp * T + sc * 128: ccp * T + sc * 128 + 128],
                        idb[:, :])
                tw = [kvev_map[(2 + ccp, sc // 4)], vh[i - 2] if i >= 2 else None, t_vzero]
                if i < 2:
                    tw.append(t_evx)
                t_tpv = emit("tensor", pe, f_tp, waits=tw)
                def f_ev(e, ccp=ccp, sc=sc, pb=pb):
                    g0 = 2 * ccp
                    src = _ap(pb, 0, [[1024, 128], [64, 2], [1, 64]])
                    dst = _ap(Vhat, (g0 * TC + sc) * VSL,
                              [[KVH * TC * VSL, 128], [TC * VSL, 2], [1, 64]])
                    return e.tensor_copy(dst, src)
                vh.append(emit("vector", dve, f_ev, waits=[t_tpv]))
        def f_ones(e):
            last = None
            for k in range(KVH * TC):
                last = e.memset(Vhat[:, k * VSL + 64: k * VSL + 65], 1.0)
            return last
        t_vones = emit("vector", dve, f_ones, fences=[t_vzero])
        t_vhat = vh[-1]

        nrm_rd = []
        sq_last = None
        for pc in range(CC + 2):
            qo = (pc % 2) * T
            war_sq = nrm_rd[(pc - 2) * 4 + 3] if pc >= 2 else None
            if pc < CC:
                def f_sq(e, pc=pc, qo=qo):
                    return e.activation(q2s[:, qo:qo + T], QBUF[:, pc * T:(pc + 1) * T],
                                        AF.Square, bias=0.0, scale=1.0)
                sqw = [qev_map[(pc, 3)], war_sq, t_vhat]
            else:
                def f_sq(e, pc=pc, qo=qo):
                    return e.activation(q2s[:, qo:qo + T], Kfm[:, (pc - CC) * T:(pc - CC + 1) * T],
                                        AF.Square, bias=0.0, scale=1.0)
                sqw = [t_kvev_all, war_sq, t_vhat]
            t_sq = emit("scalar", act, f_sq, waits=sqw)
            sq_last = t_sq
            for tb in range(4):
                k = pc * 4 + tb
                psx = psq[k % 3]
                def f_nm(e, tb=tb, psx=psx, qo=qo):
                    return e.matmul(psx[0:1, :], onc[:, :],
                                    q2s[:, qo + tb * 512: qo + (tb + 1) * 512],
                                    start=True, stop=True)
                t_nm = emit("tensor", pe, f_nm,
                            waits=[t_sq, t_onc,
                                   nrm_rd[k - 3] if k >= 3 else allev[44 + k]])
                def f_rd(e, pc=pc, tb=tb, psx=psx):
                    return e.tensor_reduce(mrow[0:1, 16 + pc * 4 + tb:17 + pc * 4 + tb],
                                           psx[0:1, :], AX.X, ALU.max)
                nrm_rd.append(emit("vector", dve, f_rd, waits=[t_nm]))
        def f_mx(e):
            last = None
            for pc in range(CC + 2):
                last = e.tensor_reduce(mrow[0:1, pc:pc + 1],
                                       mrow[0:1, 16 + 4 * pc:20 + 4 * pc],
                                       AX.X, ALU.max)
            return last
        nrm_rd.append(emit("vector", dve, f_mx, fences=[nrm_rd[-1]]))
        def f_prod(e):
            last = None
            for p in range(NPAIR):
                last = e.tensor_tensor(mrow[0:1, 24 + p:25 + p], mrow[0:1, p:p + 1],
                                       mrow[0:1, 8 + (p % 2):9 + (p % 2)], ALU.mult)
            return last
        t_prod = emit("vector", dve, f_prod, fences=[nrm_rd[-1]])
        t_ln = emit("scalar", act, lambda e: e.activation(
            mrow[0:1, 40:48], mrow[0:1, 24:32], AF.Ln, bias=0.0, scale=1.0),
            waits=[t_prod])
        t_ex = emit("scalar", act, lambda e: e.activation(
            mrow[0:1, 48:56], mrow[0:1, 40:48], AF.Exp, bias=0.0, scale=0.5),
            fences=[t_ln])
        t_mbc = emit("tensor", pe, lambda e: e.matmul(
            psn[:, 0:8], onrf[:, :], mrow[0:1, 48:56], start=True, stop=True),
            waits=[t_ex, t_onrf, nrm_rd[-1]])
        def f_negM(e):
            last = None
            for p in range(NPAIR):
                last = e.tensor_scalar(negM[p][:, :], psn[:, p:p + 1], -1.0, None,
                                       ALU.mult)
            return last
        t_negM = emit("vector", dve, f_negM, waits=[t_mbc])

        tap_gate_c = []
        tap_gate_d = []
        if DEBUG_TAPS:
            tap_gate_c.append(emit("sync", so, lambda e: e.dma_start(
                out=taps["tap_xfm"], in_=FM[:, :]), waits=[t_evx], k=16))
            tap_gate_d.append(emit("sync", so, lambda e: e.dma_start(
                out=taps["tap_q"], in_=QBUF[:, :]), waits=[qev[-1]], k=16))
            tap_gate_d.append(emit("sync", so, lambda e: e.dma_start(
                out=taps["tap_k"], in_=Kfm[:, :]), waits=[t_kvev_all], k=16))
            tap_gate_d.append(emit("sync", so, lambda e: e.dma_start(
                out=taps["tap_vhat"], in_=Vhat[:, :]), waits=[t_vhat, t_vones], k=16))
            for p in range(NPAIR):
                tap_gate_d.append(emit(
                    "sync", so,
                    lambda e, p=p: e.dma_start(out=taps["tap_negm"][:, p:p + 1],
                                               in_=negM[p][:, :],
                                               allow_slow_non_contiguous=True),
                    waits=[t_negM], k=16))

        sB.close()
        sC = ExitStack()
        EC = sC.enter_context
        Ew2 = [EC(nc.sbuf_tensor(f"Ew2_{i}", [128, 2 * 2432], BF16)) for i in range(2)]
        Pb = [EC(nc.sbuf_tensor(f"Pb{i}", [128, 1024], BF16)) for i in range(2)]
        PPb = [EC(nc.sbuf_tensor(f"PPb{i}", [128, 1024], BF16)) for i in range(3)]
        Osb = EC(nc.sbuf_tensor("Osb", [128, T], BF16))
        sig = EC(nc.sbuf_tensor("sig", [128, 512], F32))
        lnsig = EC(nc.sbuf_tensor("lnsig", [128, 512], F32))
        rs = EC(nc.sbuf_tensor("rs", [128, 512], BF16))
        STAB = [EC(nc.psum_tensor(f"STAB{i}", [128, 1024], F32)) for i in range(2)]
        OH = [[EC(nc.psum_tensor(f"OH{i}{hh}", [128, 512], F32)) for hh in range(2)]
              for i in range(2)]

        it = 0
        it_info = {}
        ew_ld, ew_war = {}, {}
        t_sig0 = emit("vector", dve, lambda e: e.memset(sig[:, :], 1.0),
                      waits=[t_mbc])
        exp_t, emu_t, mav_t = {}, {}, {}
        fmu_g, sgc_t, oev_t, rsg_t, mms_t = {}, {}, {}, {}, {}
        pair_last_emu = {}
        pending_sigma = [None]

        def sigma_s1(p, qb, qpar, qbg, last_it):
            lA, lB = mav_t[(last_it, 0)], mav_t[(last_it, 1)]
            def f_sgc(e, qpar=qpar):
                e.tensor_copy(sig[0:1, 0:512], OH[qpar][0][64:65, :])
                return e.tensor_copy(sig[64:65, 0:512], OH[qpar][1][64:65, :])
            sgc_t[qbg] = emit("vector", dve, f_sgc,
                              waits=[lA, lB, mms_t.get(qbg - 1)])

        def sigma_s2(p, qb, qpar, qbg, last_it):
            lA, lB = mav_t[(last_it, 0)], mav_t[(last_it, 1)]
            def f_oev(e, qb=qb, qpar=qpar):
                e.tensor_copy(Osb[0:64, qb * 512:(qb + 1) * 512], OH[qpar][0][0:64, :])
                return e.tensor_copy(Osb[64:128, qb * 512:(qb + 1) * 512],
                                     OH[qpar][1][0:64, :])
            oev_t[qbg] = emit("vector", dve, f_oev,
                              waits=[lA, lB, fmu_g.get(qbg - 4)])

        def sigma_s3(p, qb, qpar, qbg, last_it):
            rsg_a = emit("scalar", act, lambda e: e.activation(
                lnsig[:, :], sig[:, :], AF.Ln,
                bias=0.0, scale=1.0), waits=[sgc_t[qbg]])
            rsg_t[qbg] = emit("scalar", act, lambda e: e.activation(
                rs[:, :], lnsig[:, :], AF.Exp,
                bias=0.0, scale=-1.0), fences=[rsg_a],
                waits=[mms_t.get(qbg - 1)])

        def sigma_s4(p, qb, qpar, qbg, last_it):
            def f_mms(e, qpar=qpar):
                e.matmul(OH[qpar][0][0:64, :], on2[0:1, 0:64],
                         rs[0:1, 0:512], start=True, stop=True,
                         tile_position=(0, 0))
                return e.matmul(OH[qpar][0][64:128, :], on2[64:65, 0:64],
                                rs[64:65, 0:512],
                                start=True, stop=True, tile_position=(64, 64))
            mms_t[qbg] = emit("tensor", pe, f_mms,
                              waits=[rsg_t[qbg], t_on2,
                                     sgc_t[qbg], oev_t[qbg]])

        def sigma_s5(p, qb, qpar, qbg, last_it):
            def f_fmu(e, p=p, qb=qb, qpar=qpar):
                return e.tensor_tensor(
                    FM[:, p * T + qb * 512: p * T + (qb + 1) * 512],
                    Osb[:, qb * 512:(qb + 1) * 512], OH[qpar][0][:, :], ALU.mult)
            fmu_g[qbg] = emit("vector", dve, f_fmu,
                              waits=[mms_t[qbg], oev_t[qbg]] +
                                    (tap_gate_c if qbg == 0 else []))
        for p in range(NPAIR):
            hA, hB = 2 * p, 2 * p + 1
            gA, gB = hA % KVH, hB % KVH
            kc = p % 2
            def emit_ewld(p, qb):
                qbg = p * 4 + qb
                eww = [ew_war.get(qbg - 2)]
                if qbg <= 1:
                    eww += [t_mbc, t_negM]
                src = _ap(ewt_d, p * 128 * 2 * EWID + qb * 512,
                          [[2 * EWID, 128], [EWID, 2], [1, 2432]])
                ew_ld[qbg] = emit("gpsimd", dg, lambda e, src=src, qb=qb: e.dma_start(
                    out=Ew2[qb % 2][:, :], in_=src), waits=eww, k=16)
            if p == 0:
                emit_ewld(0, 0)
                emit_ewld(0, 1)
            def emit_mav(itx):
                if (itx, 0) in mav_t:
                    return
                px, qbx, scx = it_info[itx]
                par3 = itx % 3
                qpar = qbx % 2
                qbg = px * 4 + qbx
                gs = (2 * px % KVH, (2 * px + 1) % KVH)
                def f_av(e, gs=gs, scx=scx, par3=par3, qpar=qpar):
                    last = None
                    for hh in range(2):
                        last = e.matmul(
                            OH[qpar][hh][0:65, :],
                            Vhat[:, (gs[hh] * TC + scx) * 65:
                                 (gs[hh] * TC + scx) * 65 + 65],
                            PPb[par3][:, hh * 512: hh * 512 + 512],
                            start=(scx == 0), stop=(scx == TC - 1))
                    return last
                avw = [emu_t[itx]]
                if scx == 0:
                    avw.append(fmu_g.get(qbg - 2))
                ev = emit("tensor", pe, f_av, waits=avw)
                mav_t[(itx, 0)] = ev
                mav_t[(itx, 1)] = ev

            for qb in range(4):
                qbg = p * 4 + qb
                qpar = qb % 2
                for sc in range(TC):
                    it_info[it] = (p, qb, sc)
                    par = it % 2
                    def f_st(e, p=p, qb=qb, sc=sc, kc=kc, par=par):
                        e.matmul(
                            STAB[par][:, 0:512],
                            Kfm[0:64, kc * T + sc * 128: kc * T + sc * 128 + 128],
                            QBUF[0:64, p * T + qb * 512: p * T + qb * 512 + 512],
                            start=True, stop=True)
                        return e.matmul(
                            STAB[par][:, 512:1024],
                            Kfm[64:128, kc * T + sc * 128: kc * T + sc * 128 + 128],
                            QBUF[64:128, p * T + qb * 512: p * T + qb * 512 + 512],
                            start=True, stop=True)
                    stw = [exp_t.get(it - 2)]
                    if it == 0:
                        stw += [qev[-1], t_kvev_all, t_negM, t_vhat, t_vones,
                                sq_last]
                    t_st = emit("tensor", pe, f_st, waits=stw)
                    if it >= 3:
                        emit_mav(it - 3)
                    if pending_sigma[0] is not None:
                        if sc == 5:
                            sigma_s1(*pending_sigma[0])
                        elif sc == 6:
                            sigma_s2(*pending_sigma[0])
                        elif sc == 7:
                            sigma_s3(*pending_sigma[0])
                        elif sc == 11:
                            sigma_s4(*pending_sigma[0])
                        elif sc == 13:
                            sigma_s5(*pending_sigma[0])
                            pending_sigma[0] = None
                    def f_exp(e, par=par, p=p):
                        return e.activation(Pb[par][:, :], STAB[par][:, :], AF.Exp,
                                            bias=negM[p][:, :], scale=1.0)
                    exp_t[it] = emit("scalar", act, f_exp,
                                     waits=[t_st, emu_t.get(it - 2)])
                    offb = 1920 - sc * 128
                    def f_emu(e, par=par, par3=it % 3, offb=offb, qb=qb):
                        o3 = _ap(PPb[par3], 0, [[1024, 128], [512, 2], [1, 512]])
                        i3 = _ap(Pb[par], 0, [[1024, 128], [512, 2], [1, 512]])
                        e3 = _ap(Ew2[qb % 2], offb, [[2 * 2432, 128], [2432, 2], [1, 512]])
                        return e.tensor_tensor(o3, i3, e3, ALU.mult)
                    emu_t[it] = emit("vector", dve, f_emu,
                                     waits=[exp_t[it], mav_t.get((it - 3, 0)),
                                            mav_t.get((it - 3, 1))] +
                                           ([ew_ld[qbg]] if sc == 0 else []))
                    it += 1
                ew_war[qbg] = emu_t[it - 1]
                if qbg + 2 < NPAIR * 4:
                    emit_ewld((qbg + 2) // 4, (qbg + 2) % 4)
                if qbg == NPAIR * 4 - 1:
                    emit_mav(it - 3)
                    emit_mav(it - 2)
                    emit_mav(it - 1)
                pending_sigma[0] = (p, qb, qpar, qbg, it - 1)
            pair_last_emu[p] = emu_t[it - 1]
        sigma_s1(*pending_sigma[0])
        sigma_s2(*pending_sigma[0])
        sigma_s3(*pending_sigma[0])
        sigma_s4(*pending_sigma[0])
        sigma_s5(*pending_sigma[0])
        pending_sigma[0] = None
        t_fmu_last = fmu_g[NPAIR * 4 - 1]

        sC.close()
        sBC.close()

        sD = ExitStack()
        ED = sD.enter_context
        r_sb = [ED(nc.sbuf_tensor(f"r_sb{i}", [128, C], F32)) for i in range(3)]
        sq = [ED(nc.sbuf_tensor(f"sq{i}", [128, C], F32)) for i in range(3)]
        x2bf = [ED(nc.sbuf_tensor(f"x2bf{i}", [128, C], BF16)) for i in range(2)]
        lnS = [ED(nc.sbuf_tensor(f"lnS{i}", [128, 1], F32)) for i in range(3)]
        lnM = [ED(nc.sbuf_tensor(f"lnM{i}", [128, 1], F32)) for i in range(3)]
        lnV = [ED(nc.sbuf_tensor(f"lnV{i}", [128, 1], F32)) for i in range(3)]
        lnR = [ED(nc.sbuf_tensor(f"lnR{i}", [128, 1], F32)) for i in range(3)]
        psD = [ED(nc.psum_tensor(f"psD{i}", [128, 512], F32)) for i in range(6)]
        pstp2 = [ED(nc.psum_tensor(f"pstp2{i}", [128, 1024], BF16)) for i in range(2)]

        tap_gate_e = []
        if DEBUG_TAPS:
            tap_gate_e.append(emit("sync", so, lambda e: e.dma_start(
                out=taps["tap_ofm"], in_=FM[:, :]), waits=[t_fmu_last], k=16))
        t_dbar = emit("gpsimd", dg, lambda e: e.nop(),
                      waits=[t_fmu_last] + tap_gate_d)

        res_t, xn_t, xbf_t, tp2_t, ev2_t = [], [None] * TC, [None] * TC, [None] * TC, [None] * TC

        def emit_tp2(tcx):
            parx = tcx % 2
            pb = pstp2[parx]
            def f_tp2(e, parx=parx, pb=pb):
                last = None
                for cc in range(CC):
                    last = e.transpose(pb[:, cc * 128:(cc + 1) * 128],
                                       x2bf[parx][:, cc * 128:(cc + 1) * 128],
                                       idb[:, :])
                return last
            tp2_t[tcx] = emit("tensor", pe, f_tp2,
                              waits=[xbf_t[tcx],
                                     ev2_t[tcx - 2] if tcx >= 2 else None])
            def f_ev2(e, tcx=tcx, pb=pb):
                src = _ap(pb, 0, [[1024, 128], [128, CC], [1, 128]])
                dst = _ap(FM, tcx * 128, [[CC * T, 128], [T, CC], [1, 128]])
                return e.tensor_copy(dst, src)
            ev2_t[tcx] = emit("vector", dve, f_ev2,
                              waits=[tp2_t[tcx]] + (tap_gate_e if tcx == 0 else []))

        gi = 0
        for tc in range(TC):
            par = tc % 3
            for eb in range(2):
                ps = psD[gi % 6]
                def f_mmp(e, tc=tc, eb=eb, ps=ps):
                    last = None
                    for cc in range(CC):
                        last = e.matmul(
                            ps[:, :],
                            FM[:, cc * T + tc * 128: cc * T + tc * 128 + 128],
                            wp_bf[:, cc * C + eb * 512: cc * C + eb * 512 + 512],
                            start=(cc == 0), stop=(cc == CC - 1))
                    return last
                mpw = [t_wp, res_t[gi - 6] if gi >= 6 else None]
                if gi == 0:
                    mpw.append(t_fmu_last)
                    mpw.append(t_dbar)
                t_mmp = emit("tensor", pe, f_mmp, waits=mpw)
                def f_res1(e, tc=tc, eb=eb, ps=ps, par=par):
                    return e.scalar_tensor_tensor(
                        r_sb[par][:, eb * 512:(eb + 1) * 512], ps[:, :], 1.0,
                        A[:, tc * C + eb * 512: tc * C + eb * 512 + 512],
                        ALU.bypass, ALU.add)
                t_res1 = emit("vector", dve, f_res1,
                              waits=[t_mmp, t_bpb, t_dbar,
                                     xn_t[tc - 3] if tc >= 3 else None])
                def f_res2(e, eb=eb, par=par):
                    sl = r_sb[par][:, eb * 512:(eb + 1) * 512]
                    return e.tensor_tensor(
                        sl, sl, bpb[:, eb * 512:(eb + 1) * 512], ALU.add)
                res_t.append(emit("vector", dve, f_res2, fences=[t_res1]))
                gi += 1
            t_cp1 = emit("scalar", act, lambda e, par=par: e.activation(
                sq[par][:, :], r_sb[par][:, :], AF.Copy, bias=0.0, scale=1.0,
                accum_out=lnS[par][:, :]), waits=[res_t[-1]])
            t_rs1 = emit("vector", dve, lambda e, par=par: e.tensor_scalar(
                lnM[par][:, :], lnS[par][:, :], -1.0 / C, None, ALU.mult),
                waits=[t_cp1])
            def f_sq1(e, par=par):
                return e.activation(sq[par][:, :], r_sb[par][:, :], AF.Square,
                                    bias=lnM[par][:, :], scale=1.0,
                                    accum_out=lnV[par][:, :])
            t_sq1 = emit("scalar", act, f_sq1, waits=[t_rs1])
            t_rsa = emit("scalar", act, lambda e, par=par: e.activation(
                lnR[par][:, :], lnV[par][:, :], AF.Ln,
                bias=epsv[:, :], scale=1.0 / C), fences=[t_sq1], waits=[t_eps])
            t_rstd = emit("scalar", act, lambda e, par=par: e.activation(
                lnR[par][:, :], lnR[par][:, :], AF.Exp,
                bias=0.0, scale=-0.5), fences=[t_rsa])
            def f_xn(e, tc=tc, par=par):
                return e.tensor_scalar(A[:, tc * C:(tc + 1) * C], r_sb[par][:, :],
                                       lnM[par][:, :], lnR[par][:, :],
                                       ALU.add, ALU.mult)
            xn_t[tc] = emit("vector", dve, f_xn, waits=[t_rstd])
            def f_xbf(e, tc=tc):
                return e.activation(x2bf[tc % 2][:, :], A[:, tc * C:(tc + 1) * C],
                                    AF.Copy, bias=0.0, scale=1.0)
            if tc >= 2:
                emit_tp2(tc - 2)
            xbf_t[tc] = emit("scalar", act, f_xbf,
                             waits=[xn_t[tc], tp2_t[tc - 2] if tc >= 2 else None])
        emit_tp2(TC - 2)
        emit_tp2(TC - 1)
        if DEBUG_TAPS:
            emit("sync", so, lambda e: e.dma_start(
                out=taps["tap_r15"], in_=r_sb[1][:, :]), waits=[res_t[-1]], k=16)
            for i, tsr in enumerate([lnS[1], lnM[1], lnV[1], lnR[1]]):
                emit("sync", so, lambda e, i=i, tsr=tsr: e.dma_start(
                    out=taps["tap_ln"][:, i:i + 1], in_=tsr[:, :],
                    allow_slow_non_contiguous=True), waits=[xn_t[TC - 1]], k=16)
        tap_gate_f = []
        if DEBUG_TAPS:
            tap_gate_f.append(emit("sync", so, lambda e: e.dma_start(
                out=taps["tap_x2"], in_=A[:, :]), waits=[xn_t[TC - 1]], k=16))

        sD.close()

        sE = ExitStack()
        EE = sE.enter_context
        h_fm = EE(nc.sbuf_tensor("h_fm", [128, 24 * 1024], BF16))
        wg_s = [EE(nc.sbuf_tensor(f"wg_s{i}", [128, CC * 256], BF16)) for i in range(2)]
        wu_s = [EE(nc.sbuf_tensor(f"wu_s{i}", [128, CC * 256], BF16)) for i in range(2)]
        wd_e = [EE(nc.sbuf_tensor(f"wd_e{i}", [128, 24 * 128], BF16)) for i in range(2)]
        gsil = [EE(nc.sbuf_tensor(f"gsil{i}", [128, 512], BF16)) for i in range(2)]
        x3s = [EE(nc.sbuf_tensor(f"x3s{i}", [128, 512], F32)) for i in range(2)]
        ln2S = [EE(nc.sbuf_tensor(f"ln2S{i}", [128, 1], F32)) for i in range(2)]
        ln2M = [EE(nc.sbuf_tensor(f"ln2M{i}", [128, 1], F32)) for i in range(TC)]
        ln2V = [EE(nc.sbuf_tensor(f"ln2V{i}", [128, 1], F32)) for i in range(TC)]
        ln2R = [EE(nc.sbuf_tensor(f"ln2R{i}", [128, 1], F32)) for i in range(TC)]
        sq2 = [EE(nc.sbuf_tensor("sq2e", [128, C], F32))] * 2
        psg = [EE(nc.psum_tensor(f"psg{i}", [128, 512], F32)) for i in range(2)]
        psu = [EE(nc.psum_tensor(f"psu{i}", [128, 512], F32)) for i in range(2)]
        psd = [EE(nc.psum_tensor(f"psd{i}", [128, 512], F32)) for i in range(2)]
        ps3 = [EE(nc.psum_tensor(f"ps3{i}", [128, 512], F32)) for i in range(2)]

        wgT_r = wgT_d.rearrange("(cc p) h -> p cc h", p=128)
        wuT_r = wuT_d.rearrange("(cc p) h -> p cc h", p=128)
        wdT_r = wdT_d.rearrange("(hcb p) d -> p hcb d", p=128)

        emit_ln2_fns = {}
        ln2rv_t, ln2r_t = {}, {}
        mmg_t, mmu_t, sil_t, hmu_t = {}, {}, {}, {}
        mmd_t, xev_t, tp3_t, radd_t = {}, {}, {}, {}
        xn2_t = {}
        lwg_t, lwu_t, lwd_t = {}, {}, {}
        sto_t = {}

        def mk_lg(tt, sl):
            def f_lg(e, sl=sl):
                return e.dma_start(
                    out=wg_s[sl % 2].ap().rearrange("p (cc h) -> p cc h", cc=CC),
                    in_=wgT_r[:, :, sl * 256:(sl + 1) * 256])
            def f_lu(e, sl=sl):
                return e.dma_start(
                    out=wu_s[sl % 2].ap().rearrange("p (cc h) -> p cc h", cc=CC),
                    in_=wuT_r[:, :, sl * 256:(sl + 1) * 256])
            war_g = mmg_t.get((tt, sl - 2, 1, 1)) or mmg_t.get((tt - 1, NSL - 2 + sl, 1, 1))
            war_u = mmu_t.get((tt, sl - 2, 1, 1)) or mmu_t.get((tt - 1, NSL - 2 + sl, 1, 1))
            lww = [t_fmu_last] if (tt == 0 and sl < 2) else []
            lwg_t[(tt, sl)] = emit("gpsimd", dg, f_lg, waits=lww + [war_g], k=16)
            lwu_t[(tt, sl)] = emit("gpsimd", dg, f_lu, waits=lww + [war_u], k=16)

        def mk_sto(tti):
            def f_sto(e, tti=tti):
                src = _ap(A, tti * 8 * C, [[TC * C, 128], [C, 8], [1, C]])
                dst = out_d.rearrange("(g p) c -> p g c",
                                      p=128)[:, 8 * tti:8 * (tti + 1), :]
                return e.dma_start(out=dst, in_=src)
            return f_sto

        def mk_ld(tt, ec):
            def f_ld(e, ec=ec):
                return e.dma_start(
                    out=wd_e[ec % 2].ap().rearrange("p (hcb d) -> p hcb d", hcb=24),
                    in_=wdT_r[:, :, ec * 128:(ec + 1) * 128])
            war_d = mmd_t.get((tt, ec - 2, 1)) or mmd_t.get((tt - 1, CC - 2 + ec, 1))
            lwd_t[(tt, ec)] = emit("gpsimd", dg, f_ld,
                                   waits=([t_fmu_last] if (tt == 0 and ec < 2) else []) + [war_d],
                                   k=16)

        for tt in range(2):
            for sl in (0, 1):
                mk_lg(tt, sl)

            idx = 0
            for sl in range(NSL):
                for hb2 in range(2):
                    hcb = sl * 2 + hb2
                    for tb2 in range(2):
                        pg, pu = psg[idx % 2], psu[idx % 2]
                        def f_mg(e, sl=sl, hb2=hb2, tb2=tb2, pg=pg, tt=tt):
                            last = None
                            for cc in range(CC):
                                last = e.matmul(
                                    pg[:, :],
                                    wg_s[sl % 2][:, cc * 256 + hb2 * 128: cc * 256 + hb2 * 128 + 128],
                                    FM[:, cc * T + tt * 1024 + tb2 * 512: cc * T + tt * 1024 + tb2 * 512 + 512],
                                    start=(cc == 0), stop=(cc == CC - 1))
                            return last
                        mgw = [lwg_t[(tt, sl)], sil_t.get((tt, idx - 2))]
                        if tt == 0 and idx == 0:
                            mgw.append(ev2_t[7])
                            mgw.append(res_t[30])
                        if tt == 0 and idx == 1:
                            mgw.append(res_t[31])
                        if tt == 1 and idx == 0:
                            mgw.append(ev2_t[TC - 1])
                        mmg_t[(tt, sl, hb2, tb2)] = emit("tensor", pe, f_mg, waits=mgw)
                        def f_mu(e, sl=sl, hb2=hb2, tb2=tb2, pu=pu, tt=tt):
                            last = None
                            for cc in range(CC):
                                last = e.matmul(
                                    pu[:, :],
                                    wu_s[sl % 2][:, cc * 256 + hb2 * 128: cc * 256 + hb2 * 128 + 128],
                                    FM[:, cc * T + tt * 1024 + tb2 * 512: cc * T + tt * 1024 + tb2 * 512 + 512],
                                    start=(cc == 0), stop=(cc == CC - 1))
                            return last
                        muw = [lwu_t[(tt, sl)], hmu_t.get((tt, idx - 2))]
                        if idx < 2 and tt == 1:
                            muw.append(mmd_t.get((0, CC - 1, 1)))
                        if tt == 0 and idx == 0:
                            muw.append(res_t[26])
                        if tt == 0 and idx == 1:
                            muw.append(res_t[27])
                        mmu_t[(tt, sl, hb2, tb2)] = emit("tensor", pe, f_mu, waits=muw)
                        def f_sil(e, pg=pg, idx=idx):
                            return e.activation(gsil[idx % 2][:, :], pg[:, :], AF.Silu,
                                                bias=0.0, scale=DEQ)
                        sil_t[(tt, idx)] = emit(
                            "scalar", act, f_sil,
                            waits=[mmg_t[(tt, sl, hb2, tb2)], hmu_t.get((tt, idx - 2))])
                        def f_hmu(e, pu=pu, idx=idx, hcb=hcb, tb2=tb2):
                            return e.scalar_tensor_tensor(
                                h_fm[:, hcb * 1024 + tb2 * 512: hcb * 1024 + tb2 * 512 + 512],
                                pu[:, :], DEQ, gsil[idx % 2][:, :], ALU.mult, ALU.mult)
                        hmw = [mmu_t[(tt, sl, hb2, tb2)], sil_t[(tt, idx)]]
                        if idx < 2 and tt == 1:
                            hmw.append(mmd_t.get((0, CC - 1, 1)))
                        if idx < 2 and tt == 0:
                            hmw.append(xbf_t[TC - 1])
                        hmu_t[(tt, idx)] = emit("vector", dve, f_hmu, waits=hmw)
                        idx += 1
                if sl + 2 < NSL:
                    mk_lg(tt, sl + 2)
                if sl == NSL - 1:
                    for ec0 in (0, 1):
                        mk_ld(tt, ec0)
                if tt == 1 and sl < 8:
                    emit_ln2_fns[0](sl)
            t_hmu_last = hmu_t[(tt, idx - 1)]
            if DEBUG_TAPS and tt == 0:
                tap_gate_h = [emit("sync", so, lambda e: e.dma_start(
                    out=taps["tap_hfm"], in_=h_fm[:, :]), waits=[t_hmu_last], k=16)]
            elif DEBUG_TAPS and tt == 1:
                hmu_t[(1, -2)] = tap_gate_h[0]


            j = 0
            for ec in range(CC):
                for tb2 in range(2):
                    pd = psd[j % 2]
                    def f_md(e, ec=ec, tb2=tb2, pd=pd, tt=tt):
                        last = None
                        for hcb in range(24):
                            last = e.matmul(
                                pd[:, :],
                                wd_e[ec % 2][:, hcb * 128:(hcb + 1) * 128],
                                h_fm[:, hcb * 1024 + tb2 * 512: hcb * 1024 + tb2 * 512 + 512],
                                start=(hcb == 0), stop=(hcb == 23))
                        return last
                    mdw = [lwd_t[(tt, ec)], t_hmu_last, xev_t.get((tt, j - 2))]
                    if tt == 0 and j == 0:
                        mdw.append(res_t[28])
                    if tt == 0 and j == 1:
                        mdw.append(res_t[29])
                    mmd_t[(tt, ec, tb2)] = emit("tensor", pe, f_md, waits=mdw)
                    def f_xev(e, pd=pd, j=j):
                        return e.tensor_scalar(x3s[j % 2][:, :], pd[:, :], DEQ, None,
                                               ALU.mult)
                    xev_t[(tt, j)] = emit("vector", dve, f_xev,
                                          waits=[mmd_t[(tt, ec, tb2)],
                                                 tp3_t.get((tt, j - 2))])
                    p3 = ps3[j % 2]
                    def f_tp3(e, j=j, p3=p3):
                        last = None
                        for jj in range(4):
                            last = e.transpose(p3[:, jj * 128:(jj + 1) * 128],
                                               x3s[j % 2][:, jj * 128:(jj + 1) * 128],
                                               idf[:, :])
                        return last
                    tp3w = [xev_t[(tt, j)], t_idf, radd_t.get((tt, j - 2))]
                    if tt == 0 and j == 0:
                        tp3w.append(ev2_t[TC - 2])
                    if tt == 0 and j == 1:
                        tp3w.append(ev2_t[TC - 1])
                    tp3_t[(tt, j)] = emit("tensor", pe, f_tp3, waits=tp3w)
                    def f_radd(e, j=j, p3=p3, ec=ec, tb2=tb2, tt=tt):
                        last = None
                        for jj in range(4):
                            tcj = tt * 8 + tb2 * 4 + jj
                            sl_ = A[:, tcj * C + ec * 128: tcj * C + ec * 128 + 128]
                            last = e.tensor_tensor(sl_, p3[:, jj * 128:(jj + 1) * 128],
                                                   sl_, ALU.add)
                        return last
                    radd_t[(tt, j)] = emit("vector", dve, f_radd,
                                           waits=[tp3_t[(tt, j)]] +
                                                 (tap_gate_f if (tt == 0 and j == 0) else []))
                    j += 1
                    if tt == 1 and 1 <= j <= 8:
                        tc0 = j - 1
                        t_rsb0 = emit("scalar", act, lambda e, tc=tc0: e.activation(
                            ln2R[tc][:, :], ln2V[tc][:, :], AF.Ln,
                            bias=epsv[:, :], scale=1.0 / C),
                            fences=[ln2rv_t[tc0]], waits=[t_eps])
                        ln2r_t[tc0] = emit("scalar", act, lambda e, tc=tc0: e.activation(
                            ln2R[tc][:, :], ln2R[tc][:, :], AF.Exp,
                            bias=0.0, scale=-0.5), fences=[t_rsb0])
                        xn2_t[tc0] = emit("vector", dve, lambda e, tc=tc0: e.tensor_scalar(
                            A[:, tc * C:(tc + 1) * C], A[:, tc * C:(tc + 1) * C],
                            ln2M[tc][:, :], ln2R[tc][:, :],
                            ALU.add, ALU.mult), waits=[ln2r_t[tc0]])
                        if tc0 == 7:
                            sto_t[0] = emit("sync", so, mk_sto(0),
                                            waits=[xn2_t[k] for k in range(8)],
                                            k=16)
                if tb2 == 1 and ec + 2 < CC:
                    mk_ld(tt, ec + 2)

            def emit_ln2(tc):
                par = tc % 2
                t_red2 = emit("vector", dve, lambda e, tc=tc, par=par: e.tensor_reduce(
                    ln2S[par][:, :], A[:, tc * C:(tc + 1) * C], AX.X, ALU.add))
                t_rs2 = emit("vector", dve, lambda e, tc=tc, par=par: e.tensor_scalar(
                    ln2M[tc][:, :], ln2S[par][:, :], -1.0 / C, None, ALU.mult),
                    fences=[t_red2])
                def f_sq2(e, tc=tc, par=par):
                    return e.activation(sq2[par][:, :], A[:, tc * C:(tc + 1) * C],
                                        AF.Square,
                                        bias=ln2M[tc][:, :], scale=1.0,
                                        accum_out=ln2V[tc][:, :])
                ln2rv_t[tc] = emit("scalar", act, f_sq2, waits=[t_rs2],
                                   fences=[ln2rv_t.get(tc - 1)])
            emit_ln2_fns[tt] = emit_ln2

        for tc in range(8, TC):
            emit_ln2_fns[1](tc)
        for tc in range(8, TC):
            t_rsb = emit("scalar", act, lambda e, tc=tc: e.activation(
                ln2R[tc][:, :], ln2V[tc][:, :], AF.Ln,
                bias=epsv[:, :], scale=1.0 / C), fences=[ln2rv_t[tc]], waits=[t_eps])
            ln2r_t[tc] = emit("scalar", act, lambda e, tc=tc: e.activation(
                ln2R[tc][:, :], ln2R[tc][:, :], AF.Exp,
                bias=0.0, scale=-0.5), fences=[t_rsb])
        for tc in range(8, TC):
            def f_xn2(e, tc=tc):
                return e.tensor_scalar(A[:, tc * C:(tc + 1) * C],
                                       A[:, tc * C:(tc + 1) * C],
                                       ln2M[tc][:, :], ln2R[tc][:, :],
                                       ALU.add, ALU.mult)
            xn2_t[tc] = emit("vector", dve, f_xn2, waits=[ln2r_t[tc]])
        sto_t[1] = emit("sync", so, mk_sto(1),
                        waits=[xn2_t[8 + k] for k in range(8)], k=16)

        sE.close()

        def run(e, name):
            for fn, waits, c, k in ev[name]:
                for wc, wv in waits:
                    e.wait_ge(wc.h, wv)
                ins = fn(e)
                if ins is None:
                    ins = e.nop()
                ins.then_inc(c.h, k)
            e.wait_ge(so.h, so.n)

        @blk.sync
        def _(e):
            run(e, "sync")

        @blk.gpsimd
        def _(e):
            run(e, "gpsimd")

        @blk.tensor
        def _(e):
            run(e, "tensor")

        @blk.vector
        def _(e):
            run(e, "vector")

        @blk.scalar
        def _(e):
            run(e, "scalar")

    return nc


def _consts():
    u = np.arange(EWID, dtype=np.float32)[None, :]
    p = np.arange(128, dtype=np.float32)[:, None]
    t1 = np.maximum(u - 1920.0 - p, 0.0)
    ewt = np.empty((NPAIR, 128, 2 * EWID), np.float32)
    for pp in range(NPAIR):
        ewt[pp, :, 0:EWID] = np.exp(-SLOPES[2 * pp] * t1)
        ewt[pp, :, EWID:] = np.exp(-SLOPES[2 * pp + 1] * t1)
    return {
        "ewt": ewt.reshape(NPAIR * 128, 2 * EWID).astype(ml_dtypes.bfloat16),
        "idb": np.eye(128, dtype=ml_dtypes.bfloat16),
        "idf": np.eye(128, dtype=np.float32),
        "onc": np.ones((128, 1), dtype=ml_dtypes.bfloat16),
        "onrb": np.ones((1, 128), dtype=ml_dtypes.bfloat16),
        "on2": np.ones((128, 128), dtype=ml_dtypes.bfloat16),
        "onrf": np.ones((1, 128), dtype=np.float32),
    }


def kernel(L0_attn_x, L0_attn_Wq, L0_attn_Wkv, L0_attn_Wproj, L0_attn_bproj,
           L0_ffn_w_gate, L0_ffn_w_up, L0_ffn_w_down):
    x = np.ascontiguousarray(np.asarray(L0_attn_x, dtype=np.float32))
    wqT = np.ascontiguousarray(np.asarray(L0_attn_Wq, dtype=np.float32).T)
    wkvT = np.ascontiguousarray(np.asarray(L0_attn_Wkv, dtype=np.float32).T)
    wpT = np.ascontiguousarray(np.asarray(L0_attn_Wproj, dtype=np.float32).T)
    bp = np.ascontiguousarray(np.asarray(L0_attn_bproj, dtype=np.float32))
    wgT = np.ascontiguousarray(np.asarray(L0_ffn_w_gate, dtype=np.int32).T)
    wuT = np.ascontiguousarray(np.asarray(L0_ffn_w_up, dtype=np.int32).T)
    wdT = np.ascontiguousarray(np.asarray(L0_ffn_w_down, dtype=np.int32).T)
    cst = _consts()

    nc = build_nc()
    in_maps = []
    for b in range(B):
        m = dict(x=np.ascontiguousarray(x[b]), wqT=wqT, wkvT=wkvT, wpT=wpT,
                 bproj=bp, wgT=wgT, wuT=wuT, wdT=wdT)
        m.update(cst)
        in_maps.append(m)
    res = run_bass_kernel_spmd(nc, in_maps, core_ids=list(range(B)))
    return np.stack([res.results[i]["out"] for i in range(B)], axis=0)


if __name__ == "__main__":
    rng = np.random.default_rng(0)
    ins = {
        "L0_attn_x": rng.standard_normal((B, T, C), dtype=np.float32),
        "L0_attn_Wq": (rng.standard_normal((C, C)) * 0.02).astype(np.float32),
        "L0_attn_Wkv": (rng.standard_normal((512, C)) * 0.02).astype(np.float32),
        "L0_attn_Wproj": (rng.standard_normal((C, C)) * 0.02).astype(np.float32),
        "L0_attn_bproj": rng.standard_normal(C).astype(np.float32),
        "L0_ffn_w_gate": rng.integers(-127, 128, (HID, C)).astype(np.int32),
        "L0_ffn_w_up": rng.integers(-127, 128, (HID, C)).astype(np.int32),
        "L0_ffn_w_down": rng.integers(-127, 128, (C, HID)).astype(np.int32),
    }
    out = kernel(**ins)
    print(out.shape, out.dtype, np.abs(out).mean())


# version 43 (final)
# speedup vs baseline: 1.0598x; 1.0276x over previous
import numpy as np
import ml_dtypes
import concourse.bass as bass
import concourse.mybir as mybir
from concourse.bass_utils import run_bass_kernel_spmd
from contextlib import ExitStack

F32 = mybir.dt.float32
BF16 = mybir.dt.bfloat16
I32 = mybir.dt.int32
AF = mybir.ActivationFunctionType
ALU = mybir.AluOpType
AX = mybir.AxisListType

B, T, C = 8, 2048, 1024
H, KVH, D = 16, 4, 64
HID = 3072
TC = T // 128
CC = C // 128
NPAIR = H // 2
EWID = 3968
QB = 1024
DEQ = 1.0 / 127.0
EPS = 1e-5
NSL = 12

_start = 2.0 ** (-(2.0 ** (-(np.log2(H) - 3.0))))
SLOPES = [float(_start ** (h + 1)) / 8.0 for h in range(H)]

ENGINES = ("sync", "gpsimd", "tensor", "vector", "scalar")
DEBUG_TAPS = False


class Ctr:
    __slots__ = ("h", "n")

    def __init__(self, h):
        self.h = h
        self.n = 0


def _ap(t, off, dims):
    th = t.tensor if isinstance(t, bass.AP) else t
    return bass.AP(th, off, dims)


def build_nc():
    nc = bass.Bass("TRN2", target_bir_lowering=False)

    x_d = nc.dram_tensor("x", [T, C], F32, kind="ExternalInput").ap()
    wqT_d = nc.dram_tensor("wqT", [C, C], F32, kind="ExternalInput").ap()
    wkvT_d = nc.dram_tensor("wkvT", [C, 2 * KVH * D], F32, kind="ExternalInput").ap()
    wpT_d = nc.dram_tensor("wpT", [C, C], F32, kind="ExternalInput").ap()
    bp_d = nc.dram_tensor("bproj", [C], F32, kind="ExternalInput").ap()
    wgT_d = nc.dram_tensor("wgT", [C, HID], I32, kind="ExternalInput").ap()
    wuT_d = nc.dram_tensor("wuT", [C, HID], I32, kind="ExternalInput").ap()
    wdT_d = nc.dram_tensor("wdT", [HID, C], I32, kind="ExternalInput").ap()
    ewt_d = nc.dram_tensor("ewt", [NPAIR * 128, 2 * EWID], BF16, kind="ExternalInput").ap()
    idb_d = nc.dram_tensor("idb", [128, 128], BF16, kind="ExternalInput").ap()
    idf_d = nc.dram_tensor("idf", [128, 128], F32, kind="ExternalInput").ap()
    onc_d = nc.dram_tensor("onc", [128, 1], BF16, kind="ExternalInput").ap()
    onrb_d = nc.dram_tensor("onrb", [1, 128], BF16, kind="ExternalInput").ap()
    on2_d = nc.dram_tensor("on2", [128, 128], BF16, kind="ExternalInput").ap()
    onrf_d = nc.dram_tensor("onrf", [1, 128], F32, kind="ExternalInput").ap()
    out_d = nc.dram_tensor("out", [T, C], F32, kind="ExternalOutput").ap()

    taps = {}
    if DEBUG_TAPS:
        for nm, shp, dt in [
            ("tap_xfm", [128, CC * T], BF16), ("tap_q", [128, CC * T], BF16),
            ("tap_k", [128, 2 * T], BF16), ("tap_vhat", [128, KVH * TC * 65], BF16),
            ("tap_negm", [128, NPAIR], F32), ("tap_ofm", [128, CC * T], BF16),
            ("tap_x2", [128, TC * C], F32), ("tap_hfm", [128, 24 * 1024], BF16),
            ("tap_x2fm", [128, CC * T], BF16), ("tap_r15", [128, C], F32),
            ("tap_ln", [128, 4], F32),
        ]:
            taps[nm] = nc.dram_tensor(nm, shp, dt, kind="ExternalOutput").ap()

    ev = {k: [] for k in ENGINES}

    def emit(eng, c, fn, waits=(), k=1, fences=()):
        prev_full = None
        if k >= 16:
            c = _dma_ctr(c)
            if c.n > 0:
                prev_full = (c, c.n)
        c.n += k
        ww = {}
        for t in waits:
            if t is None:
                continue
            wc, wv = t
            if wc is c:
                continue
            ww[wc] = max(ww.get(wc, 0), wv)
        for t in fences:
            if t is None:
                continue
            wc, wv = t
            ww[wc] = max(ww.get(wc, 0), wv)
        if prev_full is not None:
            wc, wv = prev_full
            ww[wc] = max(ww.get(wc, 0), wv)
        ev[eng].append((fn, list(ww.items()), c, k))
        return (c, c.n)

    with ExitStack() as top:
        E = top.enter_context
        A = E(nc.sbuf_tensor("A", [128, TC * C], F32))
        FM = E(nc.sbuf_tensor("FM", [128, CC * T], BF16))
        idb = E(nc.sbuf_tensor("idb_s", [128, 128], BF16))
        idf = E(nc.sbuf_tensor("idf_s", [128, 128], F32))
        onc = E(nc.sbuf_tensor("onc_s", [128, 1], BF16))
        onrb = E(nc.sbuf_tensor("onrb_s", [1, 128], BF16))
        on2 = E(nc.sbuf_tensor("on2_s", [128, 128], BF16))
        onrf = E(nc.sbuf_tensor("onrf_s", [1, 128], F32))
        negM = [E(nc.sbuf_tensor(f"negM{i}", [128, 1], F32)) for i in range(NPAIR)]
        epsv = E(nc.sbuf_tensor("epsv", [128, 1], F32))
        wp_bf = E(nc.sbuf_tensor("wp_bf", [128, CC * C], BF16))
        bpb = E(nc.sbuf_tensor("bpb", [128, C], F32))

        sems = {k: Ctr(E(nc.semaphore(f"s_{k}"))) for k in
                ("di", "dg", "pe", "act", "dve", "out")}
        di, dg, pe, act, dve, so = (sems[k] for k in ("di", "dg", "pe", "act", "dve", "out"))
        DIP = [Ctr(E(nc.semaphore(f"s_dip{i}"))) for i in range(6)]
        DGP = [Ctr(E(nc.semaphore(f"s_dgp{i}"))) for i in range(8)]
        _pool_idx = {"di": 0, "dg": 0}

        def _dma_ctr(c):
            if c is di:
                _pool_idx["di"] += 1
                return DIP[_pool_idx["di"] % len(DIP)]
            if c is dg:
                _pool_idx["dg"] += 1
                return DGP[_pool_idx["dg"] % len(DGP)]
            return c

        blk = E(nc.Block())

        sBC = ExitStack()
        EB = sBC.enter_context
        QBUF = EB(nc.sbuf_tensor("QBUF", [128, CC * T], BF16))
        Kfm = EB(nc.sbuf_tensor("Kfm", [128, 2 * T], BF16))
        VSL = 65
        Vhat = EB(nc.sbuf_tensor("Vhat", [128, KVH * TC * VSL], BF16))

        sB = ExitStack()
        EB2 = sB.enter_context
        wq_bf = EB2(nc.sbuf_tensor("wq_bf", [128, CC * C], BF16))
        wkv_bf = EB2(nc.sbuf_tensor("wkv_bf", [128, CC * 512], BF16))
        Vfm = EB2(nc.sbuf_tensor("Vfm", [128, 2 * T], BF16))
        q2s = Vfm
        mrow = EB2(nc.sbuf_tensor("mrow", [1, 64], F32))

        psq = [EB2(nc.psum_tensor(f"psq{i}", [128, 512], F32)) for i in range(4)]
        pstp = [EB2(nc.psum_tensor(f"pstp{i}", [128, 1024], BF16)) for i in range(2)]
        psn = EB2(nc.psum_tensor("psn", [128, 512], F32))

        t_eps = emit("vector", dve, lambda e: e.memset(epsv[:, :], EPS))
        t_ldx = emit("sync", di, lambda e: e.dma_start(
            out=_ap(A, 0, [[TC * C, 128], [C, TC], [1, C]]),
            in_=x_d.rearrange("(tc p) c -> p tc c", p=128)), k=16)
        t_idb = emit("sync", di, lambda e: e.dma_start(out=idb[:, :], in_=idb_d), k=16)
        t_idf = emit("sync", di, lambda e: e.dma_start(out=idf[:, :], in_=idf_d), k=16)
        t_onc = emit("sync", di, lambda e: e.dma_start(out=onc[:, :], in_=onc_d), k=16)
        t_onrb = emit("sync", di, lambda e: e.dma_start(out=onrb[:, :], in_=onrb_d), k=16)
        t_on2 = emit("sync", di, lambda e: e.dma_start(out=on2[:, :], in_=on2_d), k=16)
        t_onrf = emit("sync", di, lambda e: e.dma_start(out=onrf[:, :], in_=onrf_d), k=16)
        def _ldxb_chunk(k4):
            return emit("gpsimd", dg, lambda e, k4=k4: e.dma_start(
                out=_ap(QBUF, k4 * 4 * C, [[TC * C, 128], [C, 4], [1, C]]),
                in_=x_d.rearrange("(tc p) c -> p tc c", p=128)[:, 4 * k4:4 * k4 + 4, :]),
                k=16)
        t_ldxb = [None] * 4
        t_ldxb[0] = _ldxb_chunk(0)
        t_wq = emit("gpsimd", dg, lambda e: e.dma_start(
            out=_ap(wq_bf, 0, [[CC * C, 128], [C, CC], [1, C]]),
            in_=wqT_d.rearrange("(cc p) e -> p cc e", p=128)), k=16)
        t_ldxb[1] = _ldxb_chunk(1)
        t_wkv = emit("gpsimd", dg, lambda e: e.dma_start(
            out=_ap(wkv_bf, 0, [[CC * 512, 128], [512, CC], [1, 512]]),
            in_=wkvT_d.rearrange("(cc p) e -> p cc e", p=128)), k=16)
        t_ldxb[2] = _ldxb_chunk(2)
        t_ldxb[3] = _ldxb_chunk(3)
        t_wp = emit("gpsimd", dg, lambda e: e.dma_start(
            out=_ap(wp_bf, 0, [[CC * C, 128], [C, CC], [1, C]]),
            in_=wpT_d.rearrange("(cc p) e -> p cc e", p=128)), k=16)

        t_bpb = emit("sync", di, lambda e: e.dma_start(
            out=bpb[:, :], in_=_ap(bp_d, 0, [[0, 128], [1, C]])), k=16)

        evx = [None] * TC
        tpx = [None] * TC
        for tc in range(TC):
            pb = pstp[tc % 2]
            def f_tp(e, tc=tc, pb=pb):
                last = None
                for cc in range(CC):
                    last = e.transpose(
                        pb[:, cc * 128:(cc + 1) * 128],
                        QBUF[:, tc * C + cc * 128: tc * C + cc * 128 + 128],
                        idb[:, :])
                return last
            tpx[tc] = emit("tensor", pe, f_tp,
                           waits=[t_ldxb[tc // 4], t_idb, evx[tc - 2] if tc >= 2 else None])
            def f_ev(e, tc=tc, pb=pb):
                src = _ap(pb, 0, [[1024, 128], [128, CC], [1, 128]])
                dst = _ap(FM, tc * 128, [[CC * T, 128], [T, CC], [1, 128]])
                return e.tensor_copy(dst, src)
            evx[tc] = emit("vector", dve, f_ev, waits=[tpx[tc]])
        t_evx = evx[-1]

        qev, kvev = [], []
        qev_map, kvev_map = {}, {}
        allev = []
        gi = 0
        for tb in range(4):
            for ec in range(CC):
                ps = psq[gi % 4]
                def f_mm(e, ec=ec, tb=tb, ps=ps):
                    last = None
                    for cc in range(CC):
                        last = e.matmul(
                            ps[:, :],
                            wq_bf[:, cc * C + ec * 128: cc * C + ec * 128 + 128],
                            FM[:, cc * T + tb * 512: cc * T + tb * 512 + 512],
                            start=(cc == 0), stop=(cc == CC - 1))
                    return last
                tmm = emit("tensor", pe, f_mm,
                           waits=[t_wq, evx[4 * tb + 3],
                                  allev[gi - 4] if gi >= 4 else None])
                def f_evq(e, ec=ec, tb=tb, ps=ps):
                    return e.activation(
                        QBUF[:, ec * T + tb * 512: ec * T + tb * 512 + 512],
                        ps[:, :], AF.Copy, bias=0.0, scale=0.125)
                t_ev = emit("scalar", act, f_evq, waits=[tmm])
                qev.append(t_ev)
                qev_map[(ec, tb)] = t_ev
                allev.append(t_ev)
                gi += 1
            for ec in range(4):
                ps = psq[gi % 4]
                def f_mm(e, ec=ec, tb=tb, ps=ps):
                    last = None
                    for cc in range(CC):
                        last = e.matmul(
                            ps[:, :],
                            wkv_bf[:, cc * 512 + ec * 128: cc * 512 + ec * 128 + 128],
                            FM[:, cc * T + tb * 512: cc * T + tb * 512 + 512],
                            start=(cc == 0), stop=(cc == CC - 1))
                    return last
                tmm = emit("tensor", pe, f_mm,
                           waits=[t_wkv, evx[4 * tb + 3],
                                  allev[gi - 4] if gi >= 4 else None])
                dst = Kfm if ec < 2 else Vfm
                dec = ec if ec < 2 else ec - 2
                def f_evk(e, dst=dst, dec=dec, tb=tb, ps=ps):
                    return e.activation(
                        dst[:, dec * T + tb * 512: dec * T + tb * 512 + 512],
                        ps[:, :], AF.Copy, bias=0.0, scale=1.0)
                t_ev = emit("scalar", act, f_evk, waits=[tmm])
                kvev.append(t_ev)
                kvev_map[(ec, tb)] = t_ev
                allev.append(t_ev)
                gi += 1
        t_kvev_all = kvev[-1]

        t_vzero = emit("vector", dve, lambda e: e.memset(Vhat[:, :], 0.0), waits=[])
        vh = []
        for ccp in range(2):
            for sc in range(TC):
                i = ccp * TC + sc
                pb = pstp[i % 2]
                def f_tp(e, ccp=ccp, sc=sc, pb=pb):
                    return e.transpose(
                        pb[:, 0:128],
                        Vfm[:, ccp * T + sc * 128: ccp * T + sc * 128 + 128],
                        idb[:, :])
                tw = [kvev_map[(2 + ccp, sc // 4)], vh[i - 2] if i >= 2 else None, t_vzero]
                if i < 2:
                    tw.append(t_evx)
                t_tpv = emit("tensor", pe, f_tp, waits=tw)
                def f_ev(e, ccp=ccp, sc=sc, pb=pb):
                    g0 = 2 * ccp
                    src = _ap(pb, 0, [[1024, 128], [64, 2], [1, 64]])
                    dst = _ap(Vhat, (g0 * TC + sc) * VSL,
                              [[KVH * TC * VSL, 128], [TC * VSL, 2], [1, 64]])
                    return e.tensor_copy(dst, src)
                vh.append(emit("vector", dve, f_ev, waits=[t_tpv]))
        def f_ones(e):
            last = None
            for k in range(KVH * TC):
                last = e.memset(Vhat[:, k * VSL + 64: k * VSL + 65], 1.0)
            return last
        t_vones = emit("vector", dve, f_ones, fences=[t_vzero])
        t_vhat = vh[-1]

        nrm_rd = []
        sq_last = None
        for pc in range(CC + 2):
            qo = (pc % 2) * T
            war_sq = nrm_rd[(pc - 2) * 4 + 3] if pc >= 2 else None
            if pc < CC:
                def f_sq(e, pc=pc, qo=qo):
                    return e.activation(q2s[:, qo:qo + T], QBUF[:, pc * T:(pc + 1) * T],
                                        AF.Square, bias=0.0, scale=1.0)
                sqw = [qev_map[(pc, 3)], war_sq, t_vhat]
            else:
                def f_sq(e, pc=pc, qo=qo):
                    return e.activation(q2s[:, qo:qo + T], Kfm[:, (pc - CC) * T:(pc - CC + 1) * T],
                                        AF.Square, bias=0.0, scale=1.0)
                sqw = [t_kvev_all, war_sq, t_vhat]
            t_sq = emit("scalar", act, f_sq, waits=sqw)
            sq_last = t_sq
            for tb in range(4):
                k = pc * 4 + tb
                psx = psq[k % 3]
                def f_nm(e, tb=tb, psx=psx, qo=qo):
                    return e.matmul(psx[0:1, :], onc[:, :],
                                    q2s[:, qo + tb * 512: qo + (tb + 1) * 512],
                                    start=True, stop=True)
                t_nm = emit("tensor", pe, f_nm,
                            waits=[t_sq, t_onc,
                                   nrm_rd[k - 3] if k >= 3 else allev[44 + k]])
                def f_rd(e, pc=pc, tb=tb, psx=psx):
                    return e.tensor_reduce(mrow[0:1, 16 + pc * 4 + tb:17 + pc * 4 + tb],
                                           psx[0:1, :], AX.X, ALU.max)
                nrm_rd.append(emit("vector", dve, f_rd, waits=[t_nm]))
        def f_mx(e):
            last = None
            for pc in range(CC + 2):
                last = e.tensor_reduce(mrow[0:1, pc:pc + 1],
                                       mrow[0:1, 16 + 4 * pc:20 + 4 * pc],
                                       AX.X, ALU.max)
            return last
        nrm_rd.append(emit("vector", dve, f_mx, fences=[nrm_rd[-1]]))
        def f_prod(e):
            last = None
            for p in range(NPAIR):
                last = e.tensor_tensor(mrow[0:1, 24 + p:25 + p], mrow[0:1, p:p + 1],
                                       mrow[0:1, 8 + (p % 2):9 + (p % 2)], ALU.mult)
            return last
        t_prod = emit("vector", dve, f_prod, fences=[nrm_rd[-1]])
        t_ln = emit("scalar", act, lambda e: e.activation(
            mrow[0:1, 40:48], mrow[0:1, 24:32], AF.Ln, bias=0.0, scale=1.0),
            waits=[t_prod])
        t_ex = emit("scalar", act, lambda e: e.activation(
            mrow[0:1, 48:56], mrow[0:1, 40:48], AF.Exp, bias=0.0, scale=0.5),
            fences=[t_ln])
        t_mbc = emit("tensor", pe, lambda e: e.matmul(
            psn[:, 0:8], onrf[:, :], mrow[0:1, 48:56], start=True, stop=True),
            waits=[t_ex, t_onrf, nrm_rd[-1]])
        def f_negM(e):
            last = None
            for p in range(NPAIR):
                last = e.tensor_scalar(negM[p][:, :], psn[:, p:p + 1], -1.0, None,
                                       ALU.mult)
            return last
        t_negM = emit("vector", dve, f_negM, waits=[t_mbc])

        tap_gate_c = []
        tap_gate_d = []
        if DEBUG_TAPS:
            tap_gate_c.append(emit("sync", so, lambda e: e.dma_start(
                out=taps["tap_xfm"], in_=FM[:, :]), waits=[t_evx], k=16))
            tap_gate_d.append(emit("sync", so, lambda e: e.dma_start(
                out=taps["tap_q"], in_=QBUF[:, :]), waits=[qev[-1]], k=16))
            tap_gate_d.append(emit("sync", so, lambda e: e.dma_start(
                out=taps["tap_k"], in_=Kfm[:, :]), waits=[t_kvev_all], k=16))
            tap_gate_d.append(emit("sync", so, lambda e: e.dma_start(
                out=taps["tap_vhat"], in_=Vhat[:, :]), waits=[t_vhat, t_vones], k=16))
            for p in range(NPAIR):
                tap_gate_d.append(emit(
                    "sync", so,
                    lambda e, p=p: e.dma_start(out=taps["tap_negm"][:, p:p + 1],
                                               in_=negM[p][:, :],
                                               allow_slow_non_contiguous=True),
                    waits=[t_negM], k=16))

        sB.close()
        sC = ExitStack()
        EC = sC.enter_context
        Ew2 = [EC(nc.sbuf_tensor(f"Ew2_{i}", [128, 2 * 2432], BF16)) for i in range(2)]
        Pb = [EC(nc.sbuf_tensor(f"Pb{i}", [128, 1024], BF16)) for i in range(2)]
        PPb = [EC(nc.sbuf_tensor(f"PPb{i}", [128, 1024], BF16)) for i in range(3)]
        Osb = EC(nc.sbuf_tensor("Osb", [128, T], BF16))
        sig = EC(nc.sbuf_tensor("sig", [128, 512], F32))
        lnsig = EC(nc.sbuf_tensor("lnsig", [128, 512], F32))
        rs = EC(nc.sbuf_tensor("rs", [128, 512], BF16))
        STAB = [EC(nc.psum_tensor(f"STAB{i}", [128, 1024], F32)) for i in range(2)]
        OH = [[EC(nc.psum_tensor(f"OH{i}{hh}", [128, 512], F32)) for hh in range(2)]
              for i in range(2)]

        it = 0
        it_info = {}
        ew_ld, ew_war = {}, {}
        t_sig0 = emit("vector", dve, lambda e: e.memset(sig[:, :], 1.0),
                      waits=[t_mbc])
        exp_t, emu_t, mav_t = {}, {}, {}
        fmu_g, sgc_t, oev_t, rsg_t, mms_t = {}, {}, {}, {}, {}
        pair_last_emu = {}
        pending_sigma = [None]

        def sigma_s1(p, qb, qpar, qbg, last_it):
            lA, lB = mav_t[(last_it, 0)], mav_t[(last_it, 1)]
            def f_sgc(e, qpar=qpar):
                e.tensor_copy(sig[0:1, 0:512], OH[qpar][0][64:65, :])
                return e.tensor_copy(sig[64:65, 0:512], OH[qpar][1][64:65, :])
            sgc_t[qbg] = emit("vector", dve, f_sgc,
                              waits=[lA, lB, mms_t.get(qbg - 1)])

        def sigma_s2(p, qb, qpar, qbg, last_it):
            lA, lB = mav_t[(last_it, 0)], mav_t[(last_it, 1)]
            def f_oev(e, qb=qb, qpar=qpar):
                e.tensor_copy(Osb[0:64, qb * 512:(qb + 1) * 512], OH[qpar][0][0:64, :])
                return e.tensor_copy(Osb[64:128, qb * 512:(qb + 1) * 512],
                                     OH[qpar][1][0:64, :])
            oev_t[qbg] = emit("vector", dve, f_oev,
                              waits=[lA, lB, fmu_g.get(qbg - 4)])

        def sigma_s3(p, qb, qpar, qbg, last_it):
            rsg_a = emit("scalar", act, lambda e: e.activation(
                lnsig[:, :], sig[:, :], AF.Ln,
                bias=0.0, scale=1.0), waits=[sgc_t[qbg]])
            rsg_t[qbg] = emit("scalar", act, lambda e: e.activation(
                rs[:, :], lnsig[:, :], AF.Exp,
                bias=0.0, scale=-1.0), fences=[rsg_a],
                waits=[mms_t.get(qbg - 1)])

        def sigma_s4(p, qb, qpar, qbg, last_it):
            def f_mms(e, qpar=qpar):
                e.matmul(OH[qpar][0][0:64, :], on2[0:1, 0:64],
                         rs[0:1, 0:512], start=True, stop=True,
                         tile_position=(0, 0))
                return e.matmul(OH[qpar][0][64:128, :], on2[64:65, 0:64],
                                rs[64:65, 0:512],
                                start=True, stop=True, tile_position=(64, 64))
            mms_t[qbg] = emit("tensor", pe, f_mms,
                              waits=[rsg_t[qbg], t_on2,
                                     sgc_t[qbg], oev_t[qbg]])

        def sigma_s5(p, qb, qpar, qbg, last_it):
            def f_fmu(e, p=p, qb=qb, qpar=qpar):
                return e.tensor_tensor(
                    FM[:, p * T + qb * 512: p * T + (qb + 1) * 512],
                    Osb[:, qb * 512:(qb + 1) * 512], OH[qpar][0][:, :], ALU.mult)
            fmu_g[qbg] = emit("vector", dve, f_fmu,
                              waits=[mms_t[qbg], oev_t[qbg]] +
                                    (tap_gate_c if qbg == 0 else []))
        for p in range(NPAIR):
            hA, hB = 2 * p, 2 * p + 1
            gA, gB = hA % KVH, hB % KVH
            kc = p % 2
            def emit_ewld(p, qb):
                qbg = p * 4 + qb
                eww = [ew_war.get(qbg - 2)]
                if qbg <= 1:
                    eww += [t_mbc, t_negM]
                src = _ap(ewt_d, p * 128 * 2 * EWID + qb * 512,
                          [[2 * EWID, 128], [EWID, 2], [1, 2432]])
                ew_ld[qbg] = emit("gpsimd", dg, lambda e, src=src, qb=qb: e.dma_start(
                    out=Ew2[qb % 2][:, :], in_=src), waits=eww, k=16)
            if p == 0:
                emit_ewld(0, 0)
                emit_ewld(0, 1)
            def emit_mav(itx):
                if (itx, 0) in mav_t:
                    return
                px, qbx, scx = it_info[itx]
                par3 = itx % 3
                qpar = qbx % 2
                qbg = px * 4 + qbx
                gs = (2 * px % KVH, (2 * px + 1) % KVH)
                def f_av(e, gs=gs, scx=scx, par3=par3, qpar=qpar):
                    last = None
                    for hh in range(2):
                        last = e.matmul(
                            OH[qpar][hh][0:65, :],
                            Vhat[:, (gs[hh] * TC + scx) * 65:
                                 (gs[hh] * TC + scx) * 65 + 65],
                            PPb[par3][:, hh * 512: hh * 512 + 512],
                            start=(scx == 0), stop=(scx == TC - 1))
                    return last
                avw = [emu_t[itx]]
                if scx == 0:
                    avw.append(fmu_g.get(qbg - 2))
                ev = emit("tensor", pe, f_av, waits=avw)
                mav_t[(itx, 0)] = ev
                mav_t[(itx, 1)] = ev

            for qb in range(4):
                qbg = p * 4 + qb
                qpar = qb % 2
                for sc in range(TC):
                    it_info[it] = (p, qb, sc)
                    par = it % 2
                    def f_st(e, p=p, qb=qb, sc=sc, kc=kc, par=par):
                        e.matmul(
                            STAB[par][:, 0:512],
                            Kfm[0:64, kc * T + sc * 128: kc * T + sc * 128 + 128],
                            QBUF[0:64, p * T + qb * 512: p * T + qb * 512 + 512],
                            start=True, stop=True)
                        return e.matmul(
                            STAB[par][:, 512:1024],
                            Kfm[64:128, kc * T + sc * 128: kc * T + sc * 128 + 128],
                            QBUF[64:128, p * T + qb * 512: p * T + qb * 512 + 512],
                            start=True, stop=True)
                    stw = [exp_t.get(it - 2)]
                    if it == 0:
                        stw += [qev[-1], t_kvev_all, t_negM, t_vhat, t_vones,
                                sq_last]
                    t_st = emit("tensor", pe, f_st, waits=stw)
                    if it >= 3:
                        emit_mav(it - 3)
                    if pending_sigma[0] is not None:
                        if sc == 5:
                            sigma_s1(*pending_sigma[0])
                        elif sc == 6:
                            sigma_s2(*pending_sigma[0])
                        elif sc == 7:
                            sigma_s3(*pending_sigma[0])
                        elif sc == 11:
                            sigma_s4(*pending_sigma[0])
                        elif sc == 13:
                            sigma_s5(*pending_sigma[0])
                            pending_sigma[0] = None
                    def f_exp(e, par=par, p=p):
                        return e.activation(Pb[par][:, :], STAB[par][:, :], AF.Exp,
                                            bias=negM[p][:, :], scale=1.0)
                    exp_t[it] = emit("scalar", act, f_exp,
                                     waits=[t_st, emu_t.get(it - 2)])
                    offb = 1920 - sc * 128
                    def f_emu(e, par=par, par3=it % 3, offb=offb, qb=qb):
                        o3 = _ap(PPb[par3], 0, [[1024, 128], [512, 2], [1, 512]])
                        i3 = _ap(Pb[par], 0, [[1024, 128], [512, 2], [1, 512]])
                        e3 = _ap(Ew2[qb % 2], offb, [[2 * 2432, 128], [2432, 2], [1, 512]])
                        return e.tensor_tensor(o3, i3, e3, ALU.mult)
                    emu_t[it] = emit("vector", dve, f_emu,
                                     waits=[exp_t[it], mav_t.get((it - 3, 0)),
                                            mav_t.get((it - 3, 1))] +
                                           ([ew_ld[qbg]] if sc == 0 else []))
                    it += 1
                ew_war[qbg] = emu_t[it - 1]
                if qbg + 2 < NPAIR * 4:
                    emit_ewld((qbg + 2) // 4, (qbg + 2) % 4)
                if qbg == NPAIR * 4 - 1:
                    emit_mav(it - 3)
                    emit_mav(it - 2)
                    emit_mav(it - 1)
                pending_sigma[0] = (p, qb, qpar, qbg, it - 1)
            pair_last_emu[p] = emu_t[it - 1]
        sigma_s1(*pending_sigma[0])
        sigma_s2(*pending_sigma[0])
        sigma_s3(*pending_sigma[0])
        sigma_s4(*pending_sigma[0])
        sigma_s5(*pending_sigma[0])
        pending_sigma[0] = None
        t_fmu_last = fmu_g[NPAIR * 4 - 1]

        sC.close()
        sBC.close()

        sD = ExitStack()
        ED = sD.enter_context
        r_sb = [ED(nc.sbuf_tensor(f"r_sb{i}", [128, C], F32)) for i in range(3)]
        sq = [ED(nc.sbuf_tensor(f"sq{i}", [128, C], F32)) for i in range(3)]
        x2bf = [ED(nc.sbuf_tensor(f"x2bf{i}", [128, C], BF16)) for i in range(2)]
        lnS = [ED(nc.sbuf_tensor(f"lnS{i}", [128, 1], F32)) for i in range(3)]
        lnM = [ED(nc.sbuf_tensor(f"lnM{i}", [128, 1], F32)) for i in range(3)]
        lnV = [ED(nc.sbuf_tensor(f"lnV{i}", [128, 1], F32)) for i in range(3)]
        lnR = [ED(nc.sbuf_tensor(f"lnR{i}", [128, 1], F32)) for i in range(3)]
        psD = [ED(nc.psum_tensor(f"psD{i}", [128, 512], F32)) for i in range(6)]
        pstp2 = [ED(nc.psum_tensor(f"pstp2{i}", [128, 1024], BF16)) for i in range(2)]

        tap_gate_e = []
        if DEBUG_TAPS:
            tap_gate_e.append(emit("sync", so, lambda e: e.dma_start(
                out=taps["tap_ofm"], in_=FM[:, :]), waits=[t_fmu_last], k=16))
        t_dbar = emit("gpsimd", dg, lambda e: e.nop(),
                      waits=[t_fmu_last] + tap_gate_d)

        res_t, xn_t, xbf_t, tp2_t, ev2_t = [], [None] * TC, [None] * TC, [None] * TC, [None] * TC

        def emit_tp2(tcx):
            parx = tcx % 2
            pb = pstp2[parx]
            def f_tp2(e, parx=parx, pb=pb):
                last = None
                for cc in range(CC):
                    last = e.transpose(pb[:, cc * 128:(cc + 1) * 128],
                                       x2bf[parx][:, cc * 128:(cc + 1) * 128],
                                       idb[:, :])
                return last
            tp2_t[tcx] = emit("tensor", pe, f_tp2,
                              waits=[xbf_t[tcx],
                                     ev2_t[tcx - 2] if tcx >= 2 else None])
            def f_ev2(e, tcx=tcx, pb=pb):
                src = _ap(pb, 0, [[1024, 128], [128, CC], [1, 128]])
                dst = _ap(FM, tcx * 128, [[CC * T, 128], [T, CC], [1, 128]])
                return e.tensor_copy(dst, src)
            ev2_t[tcx] = emit("vector", dve, f_ev2,
                              waits=[tp2_t[tcx]] + (tap_gate_e if tcx == 0 else []))

        cp1_t, rstd_t = {}, {}

        def emit_ln_a(tc):
            par = tc % 3
            cp1_t[tc] = emit("scalar", act, lambda e, par=par: e.activation(
                sq[par][:, :], r_sb[par][:, :], AF.Copy, bias=0.0, scale=1.0,
                accum_out=lnS[par][:, :]), waits=[res_t[2 * tc + 1]])

        def emit_ln_b(tc):
            par = tc % 3
            t_rs1 = emit("vector", dve, lambda e, par=par: e.tensor_scalar(
                lnM[par][:, :], lnS[par][:, :], -1.0 / C, None, ALU.mult),
                waits=[cp1_t[tc]])
            def f_sq1(e, par=par):
                return e.activation(sq[par][:, :], r_sb[par][:, :], AF.Square,
                                    bias=lnM[par][:, :], scale=1.0,
                                    accum_out=lnV[par][:, :])
            t_sq1 = emit("scalar", act, f_sq1, waits=[t_rs1])
            t_rsa = emit("scalar", act, lambda e, par=par: e.activation(
                lnR[par][:, :], lnV[par][:, :], AF.Ln,
                bias=epsv[:, :], scale=1.0 / C), fences=[t_sq1], waits=[t_eps])
            rstd_t[tc] = emit("scalar", act, lambda e, par=par: e.activation(
                lnR[par][:, :], lnR[par][:, :], AF.Exp,
                bias=0.0, scale=-0.5), fences=[t_rsa])

        def emit_ln_c(tc):
            par = tc % 3
            def f_xn(e, tc=tc, par=par):
                return e.tensor_scalar(A[:, tc * C:(tc + 1) * C], r_sb[par][:, :],
                                       lnM[par][:, :], lnR[par][:, :],
                                       ALU.add, ALU.mult)
            xn_t[tc] = emit("vector", dve, f_xn, waits=[rstd_t[tc]])
            def f_xbf(e, tc=tc):
                return e.activation(x2bf[tc % 2][:, :], A[:, tc * C:(tc + 1) * C],
                                    AF.Copy, bias=0.0, scale=1.0)
            if tc >= 2:
                emit_tp2(tc - 2)
            xbf_t[tc] = emit("scalar", act, f_xbf,
                             waits=[xn_t[tc], tp2_t[tc - 2] if tc >= 2 else None])

        gi = 0
        for tc in range(TC):
            par = tc % 3
            for eb in range(2):
                ps = psD[gi % 6]
                def f_mmp(e, tc=tc, eb=eb, ps=ps):
                    last = None
                    for cc in range(CC):
                        last = e.matmul(
                            ps[:, :],
                            FM[:, cc * T + tc * 128: cc * T + tc * 128 + 128],
                            wp_bf[:, cc * C + eb * 512: cc * C + eb * 512 + 512],
                            start=(cc == 0), stop=(cc == CC - 1))
                    return last
                mpw = [t_wp, res_t[gi - 6] if gi >= 6 else None]
                if gi == 0:
                    mpw.append(t_fmu_last)
                    mpw.append(t_dbar)
                t_mmp = emit("tensor", pe, f_mmp, waits=mpw)
                def f_res1(e, tc=tc, eb=eb, ps=ps, par=par):
                    return e.scalar_tensor_tensor(
                        r_sb[par][:, eb * 512:(eb + 1) * 512], ps[:, :], 1.0,
                        A[:, tc * C + eb * 512: tc * C + eb * 512 + 512],
                        ALU.bypass, ALU.add)
                t_res1 = emit("vector", dve, f_res1,
                              waits=[t_mmp, t_bpb, t_dbar,
                                     xn_t[tc - 3] if tc >= 3 else None])
                def f_res2(e, eb=eb, par=par):
                    sl = r_sb[par][:, eb * 512:(eb + 1) * 512]
                    return e.tensor_tensor(
                        sl, sl, bpb[:, eb * 512:(eb + 1) * 512], ALU.add)
                res_t.append(emit("vector", dve, f_res2, fences=[t_res1]))
                gi += 1
            emit_ln_a(tc)
            if tc >= 1:
                emit_ln_b(tc - 1)
            if tc >= 2:
                emit_ln_c(tc - 2)
        emit_ln_b(TC - 1)
        emit_ln_c(TC - 2)
        emit_ln_c(TC - 1)
        emit_tp2(TC - 2)
        emit_tp2(TC - 1)
        if DEBUG_TAPS:
            emit("sync", so, lambda e: e.dma_start(
                out=taps["tap_r15"], in_=r_sb[1][:, :]), waits=[res_t[-1]], k=16)
            for i, tsr in enumerate([lnS[1], lnM[1], lnV[1], lnR[1]]):
                emit("sync", so, lambda e, i=i, tsr=tsr: e.dma_start(
                    out=taps["tap_ln"][:, i:i + 1], in_=tsr[:, :],
                    allow_slow_non_contiguous=True), waits=[xn_t[TC - 1]], k=16)
        tap_gate_f = []
        if DEBUG_TAPS:
            tap_gate_f.append(emit("sync", so, lambda e: e.dma_start(
                out=taps["tap_x2"], in_=A[:, :]), waits=[xn_t[TC - 1]], k=16))

        sD.close()

        sE = ExitStack()
        EE = sE.enter_context
        h_fm = EE(nc.sbuf_tensor("h_fm", [128, 24 * 1024], BF16))
        wg_s = [EE(nc.sbuf_tensor(f"wg_s{i}", [128, CC * 256], BF16)) for i in range(2)]
        wu_s = [EE(nc.sbuf_tensor(f"wu_s{i}", [128, CC * 256], BF16)) for i in range(2)]
        wd_e = [EE(nc.sbuf_tensor(f"wd_e{i}", [128, 24 * 128], BF16)) for i in range(2)]
        gsil = [EE(nc.sbuf_tensor(f"gsil{i}", [128, 512], BF16)) for i in range(2)]
        x3s = [EE(nc.sbuf_tensor(f"x3s{i}", [128, 512], F32)) for i in range(2)]
        ln2S = [EE(nc.sbuf_tensor(f"ln2S{i}", [128, 1], F32)) for i in range(2)]
        ln2M = [EE(nc.sbuf_tensor(f"ln2M{i}", [128, 1], F32)) for i in range(TC)]
        ln2V = [EE(nc.sbuf_tensor(f"ln2V{i}", [128, 1], F32)) for i in range(TC)]
        ln2R = [EE(nc.sbuf_tensor(f"ln2R{i}", [128, 1], F32)) for i in range(TC)]
        sq2 = [EE(nc.sbuf_tensor("sq2e", [128, C], F32))] * 2
        psg = [EE(nc.psum_tensor(f"psg{i}", [128, 512], F32)) for i in range(2)]
        psu = [EE(nc.psum_tensor(f"psu{i}", [128, 512], F32)) for i in range(2)]
        psd = [EE(nc.psum_tensor(f"psd{i}", [128, 512], F32)) for i in range(2)]
        ps3 = [EE(nc.psum_tensor(f"ps3{i}", [128, 512], F32)) for i in range(2)]

        wgT_r = wgT_d.rearrange("(cc p) h -> p cc h", p=128)
        wuT_r = wuT_d.rearrange("(cc p) h -> p cc h", p=128)
        wdT_r = wdT_d.rearrange("(hcb p) d -> p hcb d", p=128)

        emit_ln2_fns = {}
        ln2rv_t, ln2r_t = {}, {}
        mmg_t, mmu_t, sil_t, hmu_t = {}, {}, {}, {}
        mmd_t, xev_t, tp3_t, radd_t = {}, {}, {}, {}
        xn2_t = {}
        lwg_t, lwu_t, lwd_t = {}, {}, {}
        sto_t = {}

        def mk_lg(tt, sl):
            def f_lg(e, sl=sl):
                return e.dma_start(
                    out=wg_s[sl % 2].ap().rearrange("p (cc h) -> p cc h", cc=CC),
                    in_=wgT_r[:, :, sl * 256:(sl + 1) * 256])
            def f_lu(e, sl=sl):
                return e.dma_start(
                    out=wu_s[sl % 2].ap().rearrange("p (cc h) -> p cc h", cc=CC),
                    in_=wuT_r[:, :, sl * 256:(sl + 1) * 256])
            war_g = mmg_t.get((tt, sl - 2, 1, 1)) or mmg_t.get((tt - 1, NSL - 2 + sl, 1, 1))
            war_u = mmu_t.get((tt, sl - 2, 1, 1)) or mmu_t.get((tt - 1, NSL - 2 + sl, 1, 1))
            lww = [t_fmu_last] if (tt == 0 and sl < 2) else []
            lwg_t[(tt, sl)] = emit("gpsimd", dg, f_lg, waits=lww + [war_g], k=16)
            lwu_t[(tt, sl)] = emit("gpsimd", dg, f_lu, waits=lww + [war_u], k=16)

        def mk_sto(tti):
            def f_sto(e, tti=tti):
                src = _ap(A, tti * 8 * C, [[TC * C, 128], [C, 8], [1, C]])
                dst = out_d.rearrange("(g p) c -> p g c",
                                      p=128)[:, 8 * tti:8 * (tti + 1), :]
                return e.dma_start(out=dst, in_=src)
            return f_sto

        def mk_ld(tt, ec):
            def f_ld(e, ec=ec):
                return e.dma_start(
                    out=wd_e[ec % 2].ap().rearrange("p (hcb d) -> p hcb d", hcb=24),
                    in_=wdT_r[:, :, ec * 128:(ec + 1) * 128])
            war_d = mmd_t.get((tt, ec - 2, 1)) or mmd_t.get((tt - 1, CC - 2 + ec, 1))
            lwd_t[(tt, ec)] = emit("gpsimd", dg, f_ld,
                                   waits=([t_fmu_last] if (tt == 0 and ec < 2) else []) + [war_d],
                                   k=16)

        for tt in range(2):
            for sl in (0, 1):
                mk_lg(tt, sl)

            idx = 0
            for sl in range(NSL):
                for hb2 in range(2):
                    hcb = sl * 2 + hb2
                    for tb2 in range(2):
                        pg, pu = psg[idx % 2], psu[idx % 2]
                        def f_mg(e, sl=sl, hb2=hb2, tb2=tb2, pg=pg, tt=tt):
                            last = None
                            for cc in range(CC):
                                last = e.matmul(
                                    pg[:, :],
                                    wg_s[sl % 2][:, cc * 256 + hb2 * 128: cc * 256 + hb2 * 128 + 128],
                                    FM[:, cc * T + tt * 1024 + tb2 * 512: cc * T + tt * 1024 + tb2 * 512 + 512],
                                    start=(cc == 0), stop=(cc == CC - 1))
                            return last
                        mgw = [lwg_t[(tt, sl)], sil_t.get((tt, idx - 2))]
                        if tt == 0 and idx == 0:
                            mgw.append(ev2_t[7])
                            mgw.append(res_t[30])
                        if tt == 0 and idx == 1:
                            mgw.append(res_t[31])
                        if tt == 1 and idx == 0:
                            mgw.append(ev2_t[TC - 1])
                        mmg_t[(tt, sl, hb2, tb2)] = emit("tensor", pe, f_mg, waits=mgw)
                        def f_mu(e, sl=sl, hb2=hb2, tb2=tb2, pu=pu, tt=tt):
                            last = None
                            for cc in range(CC):
                                last = e.matmul(
                                    pu[:, :],
                                    wu_s[sl % 2][:, cc * 256 + hb2 * 128: cc * 256 + hb2 * 128 + 128],
                                    FM[:, cc * T + tt * 1024 + tb2 * 512: cc * T + tt * 1024 + tb2 * 512 + 512],
                                    start=(cc == 0), stop=(cc == CC - 1))
                            return last
                        muw = [lwu_t[(tt, sl)], hmu_t.get((tt, idx - 2))]
                        if idx < 2 and tt == 1:
                            muw.append(mmd_t.get((0, CC - 1, 1)))
                        if tt == 0 and idx == 0:
                            muw.append(res_t[26])
                        if tt == 0 and idx == 1:
                            muw.append(res_t[27])
                        mmu_t[(tt, sl, hb2, tb2)] = emit("tensor", pe, f_mu, waits=muw)
                        def f_sil(e, pg=pg, idx=idx):
                            return e.activation(gsil[idx % 2][:, :], pg[:, :], AF.Silu,
                                                bias=0.0, scale=DEQ)
                        sil_t[(tt, idx)] = emit(
                            "scalar", act, f_sil,
                            waits=[mmg_t[(tt, sl, hb2, tb2)], hmu_t.get((tt, idx - 2))])
                        def f_hmu(e, pu=pu, idx=idx, hcb=hcb, tb2=tb2):
                            return e.scalar_tensor_tensor(
                                h_fm[:, hcb * 1024 + tb2 * 512: hcb * 1024 + tb2 * 512 + 512],
                                pu[:, :], DEQ, gsil[idx % 2][:, :], ALU.mult, ALU.mult)
                        hmw = [mmu_t[(tt, sl, hb2, tb2)], sil_t[(tt, idx)]]
                        if idx < 2 and tt == 1:
                            hmw.append(mmd_t.get((0, CC - 1, 1)))
                        if idx < 2 and tt == 0:
                            hmw.append(xbf_t[TC - 1])
                        hmu_t[(tt, idx)] = emit("vector", dve, f_hmu, waits=hmw)
                        idx += 1
                if sl + 2 < NSL:
                    mk_lg(tt, sl + 2)
                if sl == NSL - 1:
                    for ec0 in (0, 1):
                        mk_ld(tt, ec0)
                if tt == 1 and sl < 8:
                    emit_ln2_fns[0](sl)
            t_hmu_last = hmu_t[(tt, idx - 1)]
            if DEBUG_TAPS and tt == 0:
                tap_gate_h = [emit("sync", so, lambda e: e.dma_start(
                    out=taps["tap_hfm"], in_=h_fm[:, :]), waits=[t_hmu_last], k=16)]
            elif DEBUG_TAPS and tt == 1:
                hmu_t[(1, -2)] = tap_gate_h[0]


            j = 0
            for ec in range(CC):
                for tb2 in range(2):
                    pd = psd[j % 2]
                    def f_md(e, ec=ec, tb2=tb2, pd=pd, tt=tt):
                        last = None
                        for hcb in range(24):
                            last = e.matmul(
                                pd[:, :],
                                wd_e[ec % 2][:, hcb * 128:(hcb + 1) * 128],
                                h_fm[:, hcb * 1024 + tb2 * 512: hcb * 1024 + tb2 * 512 + 512],
                                start=(hcb == 0), stop=(hcb == 23))
                        return last
                    mdw = [lwd_t[(tt, ec)], t_hmu_last, xev_t.get((tt, j - 2))]
                    if tt == 0 and j == 0:
                        mdw.append(res_t[28])
                    if tt == 0 and j == 1:
                        mdw.append(res_t[29])
                    mmd_t[(tt, ec, tb2)] = emit("tensor", pe, f_md, waits=mdw)
                    def f_xev(e, pd=pd, j=j):
                        return e.tensor_scalar(x3s[j % 2][:, :], pd[:, :], DEQ, None,
                                               ALU.mult)
                    xev_t[(tt, j)] = emit("vector", dve, f_xev,
                                          waits=[mmd_t[(tt, ec, tb2)],
                                                 tp3_t.get((tt, j - 2))])
                    p3 = ps3[j % 2]
                    def f_tp3(e, j=j, p3=p3):
                        last = None
                        for jj in range(4):
                            last = e.transpose(p3[:, jj * 128:(jj + 1) * 128],
                                               x3s[j % 2][:, jj * 128:(jj + 1) * 128],
                                               idf[:, :])
                        return last
                    tp3w = [xev_t[(tt, j)], t_idf, radd_t.get((tt, j - 2))]
                    if tt == 0 and j == 0:
                        tp3w.append(ev2_t[TC - 2])
                    if tt == 0 and j == 1:
                        tp3w.append(ev2_t[TC - 1])
                    tp3_t[(tt, j)] = emit("tensor", pe, f_tp3, waits=tp3w)
                    def f_radd(e, j=j, p3=p3, ec=ec, tb2=tb2, tt=tt):
                        last = None
                        for jj in range(4):
                            tcj = tt * 8 + tb2 * 4 + jj
                            sl_ = A[:, tcj * C + ec * 128: tcj * C + ec * 128 + 128]
                            last = e.tensor_tensor(sl_, p3[:, jj * 128:(jj + 1) * 128],
                                                   sl_, ALU.add)
                        return last
                    radd_t[(tt, j)] = emit("vector", dve, f_radd,
                                           waits=[tp3_t[(tt, j)]] +
                                                 (tap_gate_f if (tt == 0 and j == 0) else []))
                    j += 1
                    if tt == 1 and 1 <= j <= 8:
                        tc0 = j - 1
                        t_rsb0 = emit("scalar", act, lambda e, tc=tc0: e.activation(
                            ln2R[tc][:, :], ln2V[tc][:, :], AF.Ln,
                            bias=epsv[:, :], scale=1.0 / C),
                            fences=[ln2rv_t[tc0]], waits=[t_eps])
                        ln2r_t[tc0] = emit("scalar", act, lambda e, tc=tc0: e.activation(
                            ln2R[tc][:, :], ln2R[tc][:, :], AF.Exp,
                            bias=0.0, scale=-0.5), fences=[t_rsb0])
                        xn2_t[tc0] = emit("vector", dve, lambda e, tc=tc0: e.tensor_scalar(
                            A[:, tc * C:(tc + 1) * C], A[:, tc * C:(tc + 1) * C],
                            ln2M[tc][:, :], ln2R[tc][:, :],
                            ALU.add, ALU.mult), waits=[ln2r_t[tc0]])
                        if tc0 == 7:
                            sto_t[0] = emit("sync", so, mk_sto(0),
                                            waits=[xn2_t[k] for k in range(8)],
                                            k=16)
                if tb2 == 1 and ec + 2 < CC:
                    mk_ld(tt, ec + 2)

            def emit_ln2(tc):
                par = tc % 2
                t_red2 = emit("vector", dve, lambda e, tc=tc, par=par: e.tensor_reduce(
                    ln2S[par][:, :], A[:, tc * C:(tc + 1) * C], AX.X, ALU.add))
                t_rs2 = emit("vector", dve, lambda e, tc=tc, par=par: e.tensor_scalar(
                    ln2M[tc][:, :], ln2S[par][:, :], -1.0 / C, None, ALU.mult),
                    fences=[t_red2])
                def f_sq2(e, tc=tc, par=par):
                    return e.activation(sq2[par][:, :], A[:, tc * C:(tc + 1) * C],
                                        AF.Square,
                                        bias=ln2M[tc][:, :], scale=1.0,
                                        accum_out=ln2V[tc][:, :])
                ln2rv_t[tc] = emit("scalar", act, f_sq2, waits=[t_rs2],
                                   fences=[ln2rv_t.get(tc - 1)])
            emit_ln2_fns[tt] = emit_ln2

        for tc in range(8, TC):
            emit_ln2_fns[1](tc)
        for tc in range(8, TC):
            t_rsb = emit("scalar", act, lambda e, tc=tc: e.activation(
                ln2R[tc][:, :], ln2V[tc][:, :], AF.Ln,
                bias=epsv[:, :], scale=1.0 / C), fences=[ln2rv_t[tc]], waits=[t_eps])
            ln2r_t[tc] = emit("scalar", act, lambda e, tc=tc: e.activation(
                ln2R[tc][:, :], ln2R[tc][:, :], AF.Exp,
                bias=0.0, scale=-0.5), fences=[t_rsb])
        for tc in range(8, TC):
            def f_xn2(e, tc=tc):
                return e.tensor_scalar(A[:, tc * C:(tc + 1) * C],
                                       A[:, tc * C:(tc + 1) * C],
                                       ln2M[tc][:, :], ln2R[tc][:, :],
                                       ALU.add, ALU.mult)
            xn2_t[tc] = emit("vector", dve, f_xn2, waits=[ln2r_t[tc]])
        sto_t[1] = emit("sync", so, mk_sto(1),
                        waits=[xn2_t[8 + k] for k in range(8)], k=16)

        sE.close()

        def run(e, name):
            for fn, waits, c, k in ev[name]:
                for wc, wv in waits:
                    e.wait_ge(wc.h, wv)
                ins = fn(e)
                if ins is None:
                    ins = e.nop()
                ins.then_inc(c.h, k)
            e.wait_ge(so.h, so.n)

        @blk.sync
        def _(e):
            run(e, "sync")

        @blk.gpsimd
        def _(e):
            run(e, "gpsimd")

        @blk.tensor
        def _(e):
            run(e, "tensor")

        @blk.vector
        def _(e):
            run(e, "vector")

        @blk.scalar
        def _(e):
            run(e, "scalar")

    return nc


def _consts():
    u = np.arange(EWID, dtype=np.float32)[None, :]
    p = np.arange(128, dtype=np.float32)[:, None]
    t1 = np.maximum(u - 1920.0 - p, 0.0)
    ewt = np.empty((NPAIR, 128, 2 * EWID), np.float32)
    for pp in range(NPAIR):
        ewt[pp, :, 0:EWID] = np.exp(-SLOPES[2 * pp] * t1)
        ewt[pp, :, EWID:] = np.exp(-SLOPES[2 * pp + 1] * t1)
    return {
        "ewt": ewt.reshape(NPAIR * 128, 2 * EWID).astype(ml_dtypes.bfloat16),
        "idb": np.eye(128, dtype=ml_dtypes.bfloat16),
        "idf": np.eye(128, dtype=np.float32),
        "onc": np.ones((128, 1), dtype=ml_dtypes.bfloat16),
        "onrb": np.ones((1, 128), dtype=ml_dtypes.bfloat16),
        "on2": np.ones((128, 128), dtype=ml_dtypes.bfloat16),
        "onrf": np.ones((1, 128), dtype=np.float32),
    }


def kernel(L0_attn_x, L0_attn_Wq, L0_attn_Wkv, L0_attn_Wproj, L0_attn_bproj,
           L0_ffn_w_gate, L0_ffn_w_up, L0_ffn_w_down):
    x = np.ascontiguousarray(np.asarray(L0_attn_x, dtype=np.float32))
    wqT = np.ascontiguousarray(np.asarray(L0_attn_Wq, dtype=np.float32).T)
    wkvT = np.ascontiguousarray(np.asarray(L0_attn_Wkv, dtype=np.float32).T)
    wpT = np.ascontiguousarray(np.asarray(L0_attn_Wproj, dtype=np.float32).T)
    bp = np.ascontiguousarray(np.asarray(L0_attn_bproj, dtype=np.float32))
    wgT = np.ascontiguousarray(np.asarray(L0_ffn_w_gate, dtype=np.int32).T)
    wuT = np.ascontiguousarray(np.asarray(L0_ffn_w_up, dtype=np.int32).T)
    wdT = np.ascontiguousarray(np.asarray(L0_ffn_w_down, dtype=np.int32).T)
    cst = _consts()

    nc = build_nc()
    in_maps = []
    for b in range(B):
        m = dict(x=np.ascontiguousarray(x[b]), wqT=wqT, wkvT=wkvT, wpT=wpT,
                 bproj=bp, wgT=wgT, wuT=wuT, wdT=wdT)
        m.update(cst)
        in_maps.append(m)
    res = run_bass_kernel_spmd(nc, in_maps, core_ids=list(range(B)))
    return np.stack([res.results[i]["out"] for i in range(B)], axis=0)


if __name__ == "__main__":
    rng = np.random.default_rng(0)
    ins = {
        "L0_attn_x": rng.standard_normal((B, T, C), dtype=np.float32),
        "L0_attn_Wq": (rng.standard_normal((C, C)) * 0.02).astype(np.float32),
        "L0_attn_Wkv": (rng.standard_normal((512, C)) * 0.02).astype(np.float32),
        "L0_attn_Wproj": (rng.standard_normal((C, C)) * 0.02).astype(np.float32),
        "L0_attn_bproj": rng.standard_normal(C).astype(np.float32),
        "L0_ffn_w_gate": rng.integers(-127, 128, (HID, C)).astype(np.int32),
        "L0_ffn_w_up": rng.integers(-127, 128, (HID, C)).astype(np.int32),
        "L0_ffn_w_down": rng.integers(-127, 128, (C, HID)).astype(np.int32),
    }
    out = kernel(**ins)
    print(out.shape, out.dtype, np.abs(out).mean())
